# version 12
# baseline (speedup 1.0000x reference)
import os
import numpy as np
from contextlib import ExitStack

os.environ.setdefault("TRN_ALLOW_ICI_ALL_TO_ALL", "1")

import concourse.bass as bass
import concourse.mybir as mybir
from concourse.bass_utils import run_bass_kernel_spmd

F32 = mybir.dt.float32
F16 = mybir.dt.float16
FP8 = mybir.dt.float8e4
F32R = mybir.dt.float32r
U8 = mybir.dt.uint8
AF = mybir.ActivationFunctionType
ALU = mybir.AluOpType
AX = mybir.AxisListType

NC = 8
S = 8192
D = 2048
H = 16
HD = 128
I = 6144
NH = 2
T = S // NC
ND = D // 128
NI = I // 128
NTC = S // 512
NT128 = S // 128
NKC = S // 128
NQC = S // 512
NTT = T // 128
INV_SQRT_HD = float(1.0 / np.sqrt(np.float32(HD)))
EPS = 1e-5
KB = 1024

TAPS = []


K_DETECT = os.environ.get("KDETECT", "0") == "1"
K_SERIAL = os.environ.get("KSERIAL", "0") == "1"


class Ctx:

    def __init__(self):
        self.mode = "count"
        self.cnt = {}
        self.cks = {}
        self.eng = None
        self.sems = {}

    def reset(self, mode, eng=None):
        self.mode = mode
        self.eng = eng
        self.cnt = {k: 0 for k in self.sems}

    AUTO_SERIAL = ("ve", "ac", "mm", "gc")
    DMA_CHAINS = ("sy", "sq", "gp")

    def op(self, chain, fn, k=1):
        if chain in self.AUTO_SERIAL and self.cnt[chain] > 0 and self.mode == "emit":
            self.eng.wait_ge(self.sems[chain], self.cnt[chain])
        if self.mode == "emit":
            fn().then_inc(self.sems[chain], k)
        self.cnt[chain] += k
        if K_SERIAL and chain in self.DMA_CHAINS and self.mode == "emit":
            self.eng.wait_ge(self.sems[chain], self.cnt[chain])
        return self.cnt[chain]

    def raw(self, fn):
        if self.mode == "emit":
            fn()

    def ck(self, key, chain):
        if self.mode == "count":
            self.cks[key] = self.cnt[chain]
        else:
            assert self.cks.get(key) == self.cnt[chain], (
                key, self.cks.get(key), self.cnt[chain])
        bnd = self.DMA_CHAINS + (("sc", "sd") if K_SERIAL else ())
        if chain in bnd and self.mode == "emit" and self.cnt[chain] > 0:
            self.eng.wait_ge(self.sems[chain], self.cnt[chain])
        return self.cks[key]

    def at(self, key):
        if self.mode == "count":
            return self.cks.get(key, 0)
        return self.cks[key]

    def wait(self, chain, count):
        if count and count > 0 and self.mode == "emit":
            self.eng.wait_ge(self.sems[chain], count)


def build_program():
    nc = bass.Bass("TRN2", target_bir_lowering=False,
                   detect_race_conditions=K_DETECT)

    x_full = nc.dram_tensor("x_full", [S, D], F32, kind="ExternalInput").ap()
    x_own = nc.dram_tensor("x_own", [T, D], F32, kind="ExternalInput").ap()
    wqkT = nc.dram_tensor("wqkT", [D, 512], F32, kind="ExternalInput").ap()
    wvT = nc.dram_tensor("wvT", [D, 256], F32, kind="ExternalInput").ap()
    wpT = nc.dram_tensor("wpT", [D, D], F32, kind="ExternalInput").ap()
    wg = nc.dram_tensor("wg", [D, I], F32, kind="ExternalInput").ap()
    wu = nc.dram_tensor("wu", [D, I], F32, kind="ExternalInput").ap()
    wd = nc.dram_tensor("wd", [I, D], F32, kind="ExternalInput").ap()
    gs_in = nc.dram_tensor("gs", [I], F32, kind="ExternalInput").ap()
    us_in = nc.dram_tensor("us", [I], F32, kind="ExternalInput").ap()
    ds_in = nc.dram_tensor("ds", [D], F32, kind="ExternalInput").ap()
    ident_in = nc.dram_tensor("ident", [128, 128], F32, kind="ExternalInput").ap()
    y_out = nc.dram_tensor("y", [T, D], F32, kind="ExternalOutput").ap()

    def scratch(name, shape, dt):
        kind = "ExternalOutput" if name in TAPS else "Internal"
        return nc.dram_tensor(name, shape, dt, kind=kind).ap()

    wg16 = scratch("wg16", [NI, 128, ND, 128], F16)
    wu16 = scratch("wu16", [NI, 128, ND, 128], F16)
    wd16 = scratch("wd16", [4, NI, 128, 512], F16)
    q_scr = scratch("q_scr", [2, NH, 128, S], F16)
    k_scr = scratch("k_scr", [2, NH, 128, S], F16)
    v_scr = scratch("v_scr", [2, NH, 128, NKC, 128], F16)
    a2a_in = scratch("a2a_in", [NC, NH, 128, T], F32)
    a2a_out = scratch("a2a_out", [NC, NH, 128, T], F32)
    po_scr = scratch("po_scr", [NTT, 128, D], F32)
    x2_scr = scratch("x2_scr", [NTT, 128, D], F32)
    ln2_scr = scratch("ln2_scr", [2, ND, 128, T], F16)
    bscr = scratch("bscr", [1, 2048], F32)
    c_scr = scratch("c_scr", [128, 4], F16)

    es = ExitStack()
    sb = lambda n_, sh, dt: es.enter_context(nc.sbuf_tensor(n_, sh, dt))

    AR1 = sb("AR1", [128, 86 * KB], U8)
    AR2 = sb("AR2", [128, 96 * KB], U8)
    ident = sb("ident_sb", [128, 128], F32)
    ones32 = sb("ones32", [128, 1], F32)
    ones16 = sb("ones16", [128, 1], F16)
    junk = sb("junk", [128, D], F32)
    stats = sb("stats", [128, 16], F32)
    nstat = sb("nstat", [1, 24], F32)
    negc = sb("negc", [128, NH], F32)
    invb = sb("invb", [128, 512], F32)
    nr_t = sb("nr_t", [1, 512], F32)
    cmax = sb("cmax", [128, 4], F16)
    rm_sb = sb("rm_sb", [128, 1], F32)
    ncrow = [sb("ncrow0", [1, 512], F16), sb("ncrow1", [1, 512], F16)]
    negones = sb("negones", [1, 128], F16)
    gs_sb = sb("gs_sb", [128, NI], F32)
    us_sb = sb("us_sb", [128, NI], F32)
    dsb = sb("dsb", [128, D], F32)

    def v1(ar, off, nbytes, dt, lead=None):
        ap = ar[:, off:off + nbytes].bitcast(dt)
        if lead is not None:
            ap = ap.rearrange("p (a b) -> p a b", a=lead)
        return ap

    wqkh = v1(AR1, 0, 16 * KB, F16, ND)
    wqkl = v1(AR1, 16 * KB, 16 * KB, F16, ND)
    wvh = v1(AR1, 32 * KB, 8 * KB, F16, ND)
    wvl = v1(AR1, 40 * KB, 8 * KB, F16, ND)
    qk_ev = [[[v1(AR1, 48 * KB + (e * 2 + hl) * KB, KB, F16)
               for hl in range(2)] for e in range(4)] for p in range(1)] * 2
    v_ev = [[v1(AR1, 56 * KB + hl * 2 * KB, 2 * KB, F16, 4)
             for hl in range(2)] for p in range(1)] * 2
    wstrip = [v1(AR1, 60 * KB, 2 * KB, F32), v1(AR1, 62 * KB, 2 * KB, F32)]
    q_in = [v1(AR1, 64 * KB + 4 * i * KB, 4 * KB, F32) for i in range(4)]
    q_8 = [v1(AR1, 80 * KB, KB, FP8), v1(AR1, 81 * KB, KB, FP8)]
    q_16 = [v1(AR1, 82 * KB, 2 * KB, F16), v1(AR1, 84 * KB, 2 * KB, F16)]
    xb = [v1(AR2, 0, 8 * KB, F32), v1(AR2, 8 * KB, 8 * KB, F32)]
    lnt = [v1(AR2, 16 * KB, 8 * KB, F32), v1(AR2, 24 * KB, 8 * KB, F32)]
    lns = [[v1(AR2, 32 * KB + (p * 2 + hl) * 16 * KB, 16 * KB, F16, ND)
            for hl in range(2)] for p in range(2)]

    kh_sb = v1(AR1, 0, 16 * KB, F16)
    kl_sb = v1(AR1, 16 * KB, 16 * KB, F16)
    vh_sb = v1(AR1, 32 * KB, 16 * KB, F16, NKC)
    vl_sb = v1(AR1, 48 * KB, 16 * KB, F16, NKC)
    qb = [[v1(AR2, (p * 2 + hl) * KB, KB, F16) for hl in range(2)] for p in range(2)]
    p_sb = [v1(AR2, 4 * KB + i * 2 * KB, 2 * KB, F32) for i in range(4)]
    ph_sb = [v1(AR2, 12 * KB + i * KB, KB, F16) for i in range(4)]
    pl_sb = [v1(AR2, 16 * KB + i * KB, KB, F16) for i in range(4)]
    acc_sb = [v1(AR2, 20 * KB, 2 * KB, F32), v1(AR2, 22 * KB, 2 * KB, F32)]
    acch = v1(AR2, 24 * KB, KB, F16)
    accl = v1(AR2, 25 * KB, KB, F16)
    att_sb = v1(AR2, 26 * KB, 2 * KB, F32)

    wph = v1(AR1, 0, 32 * KB, F16, ND)
    wpl = v1(AR1, 32 * KB, 32 * KB, F16, ND)
    pstrip = [v1(AR1, 64 * KB, 4 * KB, F32), v1(AR1, 68 * KB, 4 * KB, F32)]
    d_l2h = v1(AR1, 72 * KB, 4 * KB, F16, ND)
    d_l2l = v1(AR1, 76 * KB, 4 * KB, F16, ND)
    atth = v1(AR2, 0, 32 * KB, F16, ND)
    attl = v1(AR2, 32 * KB, 32 * KB, F16, ND)
    astrip = [v1(AR2, 64 * KB, 4 * KB, F32), v1(AR2, 68 * KB, 4 * KB, F32)]
    po_sb = v1(AR2, 72 * KB, 8 * KB, F32)
    xo_sb = v1(AR2, 80 * KB, 8 * KB, F32)
    t_sb = v1(AR2, 88 * KB, 8 * KB, F32)

    fuh = v1(AR2, 0, 48 * KB, F16, NI)
    ful = v1(AR2, 48 * KB, 48 * KB, F16, NI)
    l2h = v1(AR1, 0, 16 * KB, F16, ND)
    l2l = v1(AR1, 16 * KB, 16 * KB, F16, ND)
    x2t = v1(AR1, 0, 8 * KB, F32)
    t_f = v1(AR1, 8 * KB, 8 * KB, F32)
    wgblk = [v1(AR1, 32 * KB + p * 4 * KB, 4 * KB, F16, ND) for p in range(2)]
    wublk = [v1(AR1, 40 * KB + p * 4 * KB, 4 * KB, F16, ND) for p in range(2)]
    wdblk = [v1(AR1, 48 * KB + p * KB, KB, F16) for p in range(2)]
    dout = v1(AR1, 50 * KB, 32 * KB, F32, 4)
    fg_sb = v1(AR1, 82 * KB, 2 * KB, F32)

    ctx = Ctx()
    for s_ in ["sy", "sq", "sc", "sd", "gp", "gc", "mm", "ve", "ac", "cc"]:
        ctx.sems[s_] = es.enter_context(nc.semaphore(s_))

    B = [es.enter_context(nc.psum_tensor(f"PB{i}", [128, 512], F32)) for i in range(8)]
    BF16_ = mybir.dt.bfloat16
    psT = [B[0][:], B[6][:]]
    PQ = [B[1][:], B[2][:], B[3][:], B[4][:]]
    PV = [B[5][:], B[7][:]]
    PN = B[7][:]
    psS = [B[1][:], B[2][:], B[6][:]]
    psAV = [B[3][:], B[4][:]]
    psP = [B[1][:], B[2][:]]
    psM = [B[0][:], B[5][:]]
    psG = B[1][:]
    psU = B[2][:]
    psD = [B[3][:], B[4][:], B[5][:], B[7][:]]

    def phase_pre(eng):
        if eng == "sy":
            ctx.op("sy", lambda: ctx.eng.dma_start(out=ident[:], in_=ident_in), 16)
            ctx.op("sy", lambda: ctx.eng.dma_start(
                out=gs_sb[:], in_=gs_in.rearrange("(a p) -> p a", p=128),
                allow_slow_non_contiguous=True), 16)
            ctx.op("sy", lambda: ctx.eng.dma_start(
                out=us_sb[:], in_=us_in.rearrange("(a p) -> p a", p=128),
                allow_slow_non_contiguous=True), 16)
            ctx.op("sy", lambda: ctx.eng.dma_start(
                out=dsb[:], in_=ds_in[None, :].to_broadcast([128, D])), 16)
            ctx.ck("pre.sy", "sy")
        elif eng == "ve":
            ctx.op("ve", lambda: ctx.eng.memset(ones32[:], 1.0))
            ctx.op("ve", lambda: ctx.eng.memset(ones16[:], 1.0))
            ctx.op("ve", lambda: ctx.eng.memset(nstat[:, 0:4], 0.0))
            ctx.op("ve", lambda: ctx.eng.memset(stats[:, 11:12], EPS))
            ctx.op("ve", lambda: ctx.eng.memset(stats[:, 12:13], 2.0))
            ctx.op("ve", lambda: ctx.eng.memset(negones[:, :], -1.0))
            ctx.ck("pre.ve", "ve")

    NMINI = 288

    def q_mini_src_dst(i):
        if i < 96:
            r, qi = i // 6, i % 6
            src = wg[r * 128:(r + 1) * 128, qi * 1024:(qi + 1) * 1024]
            dst = wg16[qi * 8:(qi + 1) * 8, :, r, :].transpose([1, 0, 2])
            lead = 8
        elif i < 192:
            j = i - 96
            r, qi = j // 6, j % 6
            src = wu[r * 128:(r + 1) * 128, qi * 1024:(qi + 1) * 1024]
            dst = wu16[qi * 8:(qi + 1) * 8, :, r, :].transpose([1, 0, 2])
            lead = 8
        else:
            j = i - 192
            r, di = j // 2, j % 2
            src = wd[r * 128:(r + 1) * 128, di * 1024:(di + 1) * 1024]
            dst = wd16[di * 2:(di + 1) * 2, r, :, :].transpose([1, 0, 2])
            lead = 2
        return src, dst, lead

    def q_ld(i):
        src, dst, lead = q_mini_src_dst(i)
        ctx.op("sc", lambda src=src, b=i % 4: ctx.eng.dma_start(
            out=q_in[b][:], in_=src), 16)
        ctx.ck(f"a.qld.{i}", "sc")

    def emit_q_minis(eng, tc):
        if eng != "ac":
            return
        lo, hi = 18 * tc, min(NMINI, 18 * (tc + 1))
        if tc == 0:
            for i in range(3):
                q_ld(i)
        for i in range(lo, hi):
            b = i % 4
            src, dst, lead = q_mini_src_dst(i)
            if i + 3 < NMINI:
                q_ld(i + 3)
            ctx.wait("sc", ctx.at(f"a.qld.{i}"))
            ctx.op("ac", lambda b=b: ctx.eng.copy(q_8[b % 2][:], q_in[b][:]))
            ctx.ck(f"a.qc1.{i}", "ac")
            if i >= 1:
                ctx.wait("sd", ctx.at(f"a.qst.{i-1}"))
            ctx.op("ac", lambda b=b: ctx.eng.copy(q_16[b % 2][:], q_8[b % 2][:]))
            ctx.ck(f"a.qc2.{i}", "ac")
            ctx.wait("ac", ctx.at(f"a.qc2.{i}"))
            ctx.op("sd", lambda dst=dst, b=b, lead=lead: ctx.eng.dma_start(
                out=dst, in_=q_16[b % 2].rearrange("p (a c) -> p a c", a=lead)), 16)
            ctx.ck(f"a.qst.{i}", "sd")

    def phase_a(eng):
        for i in range(ND):
            b = i % 2
            if eng == "sy":
                if i >= 2:
                    ctx.wait("ve", ctx.at(f"a.wl.{i-2}"))
                ctx.op("sy", lambda i=i, b=b: ctx.eng.dma_start(
                    out=wstrip[b][:, 0:512], in_=wqkT[i * 128:(i + 1) * 128, :]), 16)
                ctx.ck(f"a.wqk.{i}", "sy")
            elif eng == "ac":
                ctx.wait("sy", ctx.at(f"a.wqk.{i}"))
                ctx.op("ac", lambda i=i, b=b: ctx.eng.copy(
                    wqkh[:, i, :], wstrip[b][:, 0:512]))
                ctx.ck(f"a.wh.{i}", "ac")
            elif eng == "ve":
                ctx.wait("sy", ctx.at(f"a.wqk.{i}"))
                ctx.wait("ac", ctx.at(f"a.wh.{i}"))
                ctx.op("ve", lambda i=i, b=b: ctx.eng.tensor_tensor(
                    wqkl[:, i, :], wstrip[b][:, 0:512], wqkh[:, i, :], op=ALU.subtract))
                ctx.ck(f"a.wl.{i}", "ve")
        for i in range(ND):
            b = i % 2
            if eng == "sy":
                ctx.wait("ve", ctx.at(f"a.wvl.{i-2}") if i >= 2
                         else ctx.at(f"a.wl.{ND-2+i}"))
                ctx.op("sy", lambda i=i, b=b: ctx.eng.dma_start(
                    out=wstrip[b][:, 0:256], in_=wvT[i * 128:(i + 1) * 128, :]), 16)
                ctx.ck(f"a.wv.{i}", "sy")
            elif eng == "ac":
                ctx.wait("sy", ctx.at(f"a.wv.{i}"))
                ctx.op("ac", lambda i=i, b=b: ctx.eng.copy(
                    wvh[:, i, :], wstrip[b][:, 0:256]))
                ctx.ck(f"a.wvh.{i}", "ac")
            elif eng == "ve":
                ctx.wait("sy", ctx.at(f"a.wv.{i}"))
                ctx.wait("ac", ctx.at(f"a.wvh.{i}"))
                ctx.op("ve", lambda i=i, b=b: ctx.eng.tensor_tensor(
                    wvl[:, i, :], wstrip[b][:, 0:256], wvh[:, i, :], op=ALU.subtract))
                ctx.ck(f"a.wvl.{i}", "ve")

        for tc in range(NTC):
            par = tc % 2
            for t in range(4):
                g = tc * 4 + t
                gb = g % 2
                if eng == "sy":
                    if g >= 2:
                        ctx.wait("ve", ctx.at(f"a.ln.{g-2}"))
                    ctx.op("sy", lambda g=g, gb=gb: ctx.eng.dma_start(
                        out=xb[gb][:], in_=x_full[g * 128:(g + 1) * 128, :]), 16)
                    ctx.ck(f"a.x.{g}", "sy")
                elif eng == "ve":
                    ctx.wait("sy", ctx.at(f"a.x.{g}"))
                    ctx.op("ve", lambda gb=gb: ctx.eng.reduce_sum(
                        stats[:, 0:1], xb[gb][:], axis=AX.X))
                    ctx.op("ve", lambda: ctx.eng.tensor_scalar_mul(
                        stats[:, 1:2], stats[:, 0:1], 1.0 / D))
                    ctx.op("ve", lambda: ctx.eng.tensor_tensor(
                        stats[:, 2:3], stats[:, 1:2], stats[:, 1:2], op=ALU.mult))
                    ctx.ck(f"a.vm.{g}", "ve")
                    ctx.wait("ac", ctx.at(f"a.ssq.{g}"))
                    ctx.op("ve", lambda: ctx.eng.scalar_tensor_tensor(
                        stats[:, 3:4], stats[:, 4:5], 1.0 / D, stats[:, 2:3],
                        op0=ALU.mult, op1=ALU.subtract))
                    ctx.ck(f"a.var.{g}", "ve")
                    ctx.wait("ac", ctx.at(f"a.std.{g}"))
                    ctx.op("ve", lambda: ctx.eng.reciprocal(stats[:, 5:6], stats[:, 10:11]))
                    ctx.op("ve", lambda: ctx.eng.tensor_tensor(
                        stats[:, 6:7], stats[:, 10:11], stats[:, 5:6], op=ALU.mult))
                    ctx.op("ve", lambda: ctx.eng.tensor_scalar(
                        stats[:, 7:8], stats[:, 6:7], -1.0, 2.0,
                        op0=ALU.mult, op1=ALU.add))
                    ctx.op("ve", lambda: ctx.eng.tensor_tensor(
                        stats[:, 8:9], stats[:, 5:6], stats[:, 7:8], op=ALU.mult))
                    ctx.op("ve", lambda: ctx.eng.scalar_tensor_tensor(
                        stats[:, 9:10], stats[:, 1:2], -1.0, stats[:, 8:9],
                        op0=ALU.mult, op1=ALU.mult))
                    ctx.ck(f"a.rstd.{g}", "ve")
                    if g >= 2:
                        ctx.wait("mm", ctx.at(f"a.tr.{g-2}"))
                    ctx.op("ve", lambda gb=gb: ctx.eng.tensor_scalar(
                        lnt[gb][:], xb[gb][:], stats[:, 8:9], stats[:, 9:10],
                        op0=ALU.mult, op1=ALU.add))
                    ctx.ck(f"a.ln.{g}", "ve")
                elif eng == "ac":
                    ctx.wait("sy", ctx.at(f"a.x.{g}"))
                    ctx.wait("ve", ctx.at(f"a.var.{g-1}") if g >= 1 else 0)
                    ctx.op("ac", lambda gb=gb: ctx.eng.activation(
                        out=junk[:], in_=xb[gb][:], func=AF.Square,
                        accum_out=stats[:, 4:5]))
                    ctx.ck(f"a.ssq.{g}", "ac")
                    ctx.wait("ve", ctx.at(f"a.var.{g}"))
                    if g == 0:
                        ctx.wait("ve", ctx.at("pre.ve"))
                    ctx.op("ac", lambda: ctx.eng.activation(
                        out=stats[:, 10:11], in_=stats[:, 3:4], func=AF.Sqrt, bias=stats[:, 11:12]))
                    ctx.ck(f"a.std.{g}", "ac")
                elif eng == "mm":
                    ctx.wait("ve", ctx.at(f"a.ln.{g}"))
                    for f in range(4):
                        fid = g * 4 + f
                        pT = psT[fid % 2]
                        if fid >= 2:
                            ctx.wait("ac", ctx.at(f"a.feh.{fid-2}"))
                            ctx.wait("ve", ctx.at(f"a.fel.{fid-2}"))
                        for j in range(4):
                            dch = f * 4 + j
                            fn = (lambda gb=gb, dch=dch, j=j, pT=pT: ctx.eng.transpose(
                                pT[:, j * 128:(j + 1) * 128],
                                lnt[gb][:, dch * 128:(dch + 1) * 128], ident[:]))
                            if j == 3:
                                ctx.op("mm", fn)
                            else:
                                ctx.raw(fn)
                        ctx.ck(f"a.fill.{fid}", "mm")
                    ctx.ck(f"a.tr.{g}", "mm")
                for f in range(4):
                    fid = g * 4 + f
                    pT = psT[fid % 2]
                    if eng == "ac":
                        ctx.wait("mm", ctx.at(f"a.fill.{fid}"))
                        if tc >= 2 and t == 0 and f == 0:
                            ctx.wait("mm", ctx.at(f"a.qkv.{tc-2}"))
                        for j in range(4):
                            dch = f * 4 + j
                            fn = (lambda par=par, dch=dch, t=t, j=j, pT=pT: ctx.eng.copy(
                                lns[par][0][:, dch, t * 128:(t + 1) * 128],
                                pT[:, j * 128:(j + 1) * 128]))
                            if j == 3:
                                ctx.op("ac", fn)
                            else:
                                ctx.raw(fn)
                        ctx.ck(f"a.feh.{fid}", "ac")
                    elif eng == "ve":
                        ctx.wait("mm", ctx.at(f"a.fill.{fid}"))
                        ctx.wait("ac", ctx.at(f"a.feh.{fid}"))
                        if tc >= 2 and t == 0 and f == 0:
                            ctx.wait("mm", ctx.at(f"a.qkv.{tc-2}"))
                        for j in range(4):
                            dch = f * 4 + j
                            fn = (lambda par=par, dch=dch, t=t, j=j, pT=pT:
                                  ctx.eng.tensor_tensor(
                                      lns[par][1][:, dch, t * 128:(t + 1) * 128],
                                      pT[:, j * 128:(j + 1) * 128],
                                      lns[par][0][:, dch, t * 128:(t + 1) * 128],
                                      op=ALU.subtract))
                            if j == 3:
                                ctx.op("ve", fn)
                            else:
                                ctx.raw(fn)
                        ctx.ck(f"a.fel.{fid}", "ve")

            if eng == "mm":
                lastfid = (tc * 4 + 3) * 4 + 3
                ctx.wait("ac", ctx.at(f"a.feh.{lastfid}"))
                ctx.wait("ve", ctx.at(f"a.fel.{lastfid}"))
                if tc == 0:
                    ctx.wait("ve", ctx.at(f"a.wl.{ND-1}"))
                    ctx.wait("ve", ctx.at(f"a.wvl.{ND-1}"))
                for e in range(4):
                    if tc >= 1:
                        ctx.wait("ac", ctx.at(f"a.evq.{tc-1}.{e}"))
                        ctx.wait("ve", ctx.at(f"a.evql.{tc-1}.{e}"))
                    for dch in range(ND):
                        w_h = wqkh[:, dch, e * 128:(e + 1) * 128]
                        w_l = wqkl[:, dch, e * 128:(e + 1) * 128]
                        r_h = lns[par][0][:, dch, :]
                        r_l = lns[par][1][:, dch, :]
                        st = (dch == 0)
                        ctx.raw(lambda e=e, w_h=w_h, r_h=r_h, st=st: ctx.eng.matmul(
                            PQ[e][:, 0:512], w_h, r_h, start=st, stop=False))
                        ctx.raw(lambda e=e, w_h=w_h, r_l=r_l: ctx.eng.matmul(
                            PQ[e][:, 0:512], w_h, r_l, start=False, stop=False))
                        lastmm = (dch == ND - 1)
                        fn = (lambda e=e, w_l=w_l, r_h=r_h, lastmm=lastmm: ctx.eng.matmul(
                            PQ[e][:, 0:512], w_l, r_h, start=False, stop=lastmm))
                        if lastmm:
                            ctx.op("mm", fn)
                        else:
                            ctx.raw(fn)
                    ctx.ck(f"a.mmq.{tc}.{e}", "mm")
                for vt in range(4):
                    gv = tc * 4 + vt
                    col = (vt % 2) * 256
                    if gv >= 2:
                        ctx.wait("ac", ctx.at(f"a.evvh.{(gv-2)//4}.{(gv-2)%4}"))
                        ctx.wait("ve", ctx.at(f"a.evvl.{(gv-2)//4}.{(gv-2)%4}"))
                    pv = PV[vt % 2]
                    for dch in range(ND):
                        a_h = lns[par][0][:, dch, vt * 128:(vt + 1) * 128]
                        a_l = lns[par][1][:, dch, vt * 128:(vt + 1) * 128]
                        st = (dch == 0)
                        ctx.raw(lambda pv=pv, a_h=a_h, dch=dch, st=st: ctx.eng.matmul(
                            pv[:, 0:256], a_h, wvh[:, dch, :],
                            start=st, stop=False))
                        ctx.raw(lambda pv=pv, a_h=a_h, dch=dch: ctx.eng.matmul(
                            pv[:, 0:256], a_h, wvl[:, dch, :],
                            start=False, stop=False))
                        lastmm = (dch == ND - 1)
                        fn = (lambda pv=pv, a_l=a_l, dch=dch, lastmm=lastmm:
                              ctx.eng.matmul(pv[:, 0:256], a_l, wvh[:, dch, :],
                                             start=False, stop=lastmm))
                        if lastmm:
                            ctx.op("mm", fn)
                        else:
                            ctx.raw(fn)
                    ctx.ck(f"a.mmv.{tc}.{vt}", "mm")
                ctx.ck(f"a.qkv.{tc}", "mm")
            elif eng == "ac":
                for e in range(4):
                    ctx.wait("mm", ctx.at(f"a.mmq.{tc}.{e}"))
                    if tc >= 1:
                        ctx.wait("sq", ctx.at(f"a.stq.{tc-1}.{e}"))
                    scale = INV_SQRT_HD if e < 2 else 1.0
                    ctx.op("ac", lambda e=e, par=par, scale=scale: ctx.eng.activation(
                        out=qk_ev[par][e][0][:], in_=PQ[e][:, 0:512],
                        func=AF.Copy, scale=scale))
                    ctx.ck(f"a.evq.{tc}.{e}", "ac")
                for vt in range(4):
                    ctx.wait("mm", ctx.at(f"a.mmv.{tc}.{vt}"))
                    if tc >= 1 and vt == 0:
                        ctx.wait("sq", ctx.at(f"a.stv.{tc-1}"))
                    ctx.op("ac", lambda vt=vt, par=par: ctx.eng.activation(
                        out=v_ev[par][0][:, vt, :], in_=PV[vt % 2][:, 0:256],
                        func=AF.Copy))
                    ctx.ck(f"a.evvh.{tc}.{vt}", "ac")
                    if vt == 3:
                        emit_q_minis("ac", tc)
            elif eng == "ve":
                for e in range(4):
                    ctx.wait("mm", ctx.at(f"a.mmq.{tc}.{e}"))
                    ctx.wait("ac", ctx.at(f"a.evq.{tc}.{e}"))
                    if tc >= 1:
                        ctx.wait("sq", ctx.at(f"a.stq.{tc-1}.{e}"))
                    scale = INV_SQRT_HD if e < 2 else 1.0
                    ctx.op("ve", lambda e=e, par=par, scale=scale:
                           ctx.eng.scalar_tensor_tensor(
                               qk_ev[par][e][1][:], PQ[e][:, 0:512], scale,
                               qk_ev[par][e][0][:], op0=ALU.mult, op1=ALU.subtract))
                    ctx.ck(f"a.evql.{tc}.{e}", "ve")
                for vt in range(4):
                    ctx.wait("mm", ctx.at(f"a.mmv.{tc}.{vt}"))
                    ctx.wait("ac", ctx.at(f"a.evvh.{tc}.{vt}"))
                    if tc >= 1 and vt == 0:
                        ctx.wait("sq", ctx.at(f"a.stv.{tc-1}"))
                    ctx.op("ve", lambda vt=vt, par=par: ctx.eng.tensor_tensor(
                        v_ev[par][1][:, vt, :], PV[vt % 2][:, 0:256],
                        v_ev[par][0][:, vt, :], op=ALU.subtract))
                    ctx.ck(f"a.evvl.{tc}.{vt}", "ve")
            elif eng == "sq":
                pass
            if eng == "sy":
                for e in range(4):
                    ctx.wait("ac", ctx.at(f"a.evq.{tc}.{e}"))
                    ctx.wait("ve", ctx.at(f"a.evql.{tc}.{e}"))
                    hd = e % 2
                    scr = q_scr if e < 2 else k_scr
                    for hl in range(2):
                        ctx.op("sq", lambda scr=scr, hl=hl, hd=hd, tc=tc, e=e, par=par:
                               ctx.eng.dma_start(
                                   out=scr[hl, hd, :, tc * 512:(tc + 1) * 512],
                                   in_=qk_ev[par][e][hl][:]), 16)
                    ctx.ck(f"a.stq.{tc}.{e}", "sq")
                ctx.wait("ac", ctx.at(f"a.evvh.{tc}.3"))
                ctx.wait("ve", ctx.at(f"a.evvl.{tc}.3"))
                for hl in range(2):
                    for hd in range(NH):
                        dst = v_scr[hl, hd, :, tc * 4:(tc + 1) * 4, :]
                        src = v_ev[par][hl][:, :, hd * 128:(hd + 1) * 128]
                        ctx.op("sq", lambda dst=dst, src=src: ctx.eng.dma_start(
                            out=dst, in_=src), 16)
                ctx.ck(f"a.stv.{tc}", "sq")
                emit_q_minis("sy", tc)

    def phase_b(eng):
        def mm_maxstep(it, par, j):
            sub, km = j // 16, j % 16
            bank = psM[j % 2]
            if j >= 2:
                ctx.wait("ve", ctx.at(f"b.rmax.{it}.{j-2}"))
            elif it >= 1:
                ctx.wait("ve", ctx.at(f"b.rmax.{it-1}.{62+j}"))
            ctx.op("mm", lambda bank=bank, sub=sub, km=km, par=par:
                   ctx.eng.matmul(
                       bank[:, 0:512],
                       qb[par][0][:, sub * 128:(sub + 1) * 128],
                       kh_sb[:, km * 512:(km + 1) * 512],
                       start=True, stop=True))
            ctx.ck(f"b.mx.{it}.{j}", "mm")

        def mm_maxpass(it, par):
            for j in range(64):
                mm_maxstep(it, par, j)

        def ve_maxinit(it):
            if it >= 1:
                ctx.wait("gp", ctx.at(f"b.cst.{it-1}"))
            ctx.op("ve", lambda: ctx.eng.memset(cmax[:, :], -60000.0))

        def ve_maxstep(it, j):
            sub = j // 16
            ctx.wait("mm", ctx.at(f"b.mx.{it}.{j}"))
            ctx.op("ve", lambda j=j: ctx.eng.reduce_max(
                rm_sb[:, :], psM[j % 2][:, 0:512], axis=AX.X))
            ctx.op("ve", lambda sub=sub: ctx.eng.tensor_tensor(
                cmax[:, sub:sub + 1], cmax[:, sub:sub + 1],
                rm_sb[:, :], op=ALU.max))
            ctx.ck(f"b.rmax.{it}.{j}", "ve")

        def ve_maxred(it):
            ve_maxinit(it)
            for j in range(64):
                ve_maxstep(it, j)

        MID = 16
        for h in range(NH):
            for qc in range(NQC):
                it = h * NQC + qc
                par = qc % 2
                pipe = qc < NQC - 1
                if eng == "sy":
                    if qc == 0:
                        if h == 0:
                            ctx.wait("sq", ctx.at(f"a.stq.{NTC-1}.3"))
                            ctx.wait("sq", ctx.at(f"a.stv.{NTC-1}"))
                        else:
                            ctx.wait("mm", ctx.at(f"b.avmm.{h-1}.{NQC-1}"))
                        for hl in range(2):
                            ctx.op("sy", lambda hl=hl, h=h: ctx.eng.dma_start(
                                out=(kh_sb if hl == 0 else kl_sb)[:],
                                in_=k_scr[hl, h, :, :]), 16)
                            ctx.op("sy", lambda hl=hl, h=h: ctx.eng.dma_start(
                                out=(vh_sb if hl == 0 else vl_sb)[:, :, :],
                                in_=v_scr[hl, h, :, :, :]), 16)
                        ctx.ck(f"b.kv.{h}", "sy")
                        ctx.op("sy", lambda h=h, qc=qc, par=par: ctx.eng.dma_start(
                            out=qb[par][0][:],
                            in_=q_scr[0, h, :, qc * 512:(qc + 1) * 512]), 16)
                        ctx.op("sy", lambda h=h, qc=qc, par=par: ctx.eng.dma_start(
                            out=qb[par][1][:],
                            in_=q_scr[1, h, :, qc * 512:(qc + 1) * 512]), 16)
                        ctx.ck(f"b.q.{it}", "sy")
                    if pipe:
                        if it >= 1:
                            ctx.wait("mm", ctx.at(f"b.avmm.{(it-1)//NQC}.{(it-1)%NQC}"))
                        np_ = (qc + 1) % 2
                        ctx.op("sy", lambda h=h, qc=qc, np_=np_: ctx.eng.dma_start(
                            out=qb[np_][0][:],
                            in_=q_scr[0, h, :, (qc + 1) * 512:(qc + 2) * 512]), 16)
                        ctx.op("sy", lambda h=h, qc=qc, np_=np_: ctx.eng.dma_start(
                            out=qb[np_][1][:],
                            in_=q_scr[1, h, :, (qc + 1) * 512:(qc + 2) * 512]), 16)
                        ctx.ck(f"b.q.{it+1}", "sy")
                    if it >= 1:
                        ctx.wait("ve", ctx.at(f"b.att.{it-1}"))
                        pq = (it - 1) % NQC
                        ph_ = (it - 1) // NQC
                        d = pq // 2
                        off = (pq % 2) * 512
                        ctx.op("sy", lambda d=d, off=off, ph_=ph_: ctx.eng.dma_start(
                            out=a2a_in[d, ph_, :, off:off + 512], in_=att_sb[:]), 16)
                        ctx.ck(f"b.atst.{it-1}", "sy")
                    if it == NH * NQC - 1:
                        ctx.wait("ve", ctx.at(f"b.att.{it}"))
                        d = qc // 2
                        off = (qc % 2) * 512
                        ctx.op("sy", lambda d=d, off=off, h=h: ctx.eng.dma_start(
                            out=a2a_in[d, h, :, off:off + 512], in_=att_sb[:]), 16)
                        ctx.ck(f"b.atst.{it}", "sy")
                elif eng == "mm":
                    if qc == 0:
                        ctx.wait("sy", ctx.at(f"b.kv.{h}"))
                        ctx.wait("sy", ctx.at(f"b.q.{it}"))
                        mm_maxpass(it, par)
                    for kc in range(NKC + 1):
                        if pipe and kc < NKC:
                            if kc == 0:
                                ctx.wait("sy", ctx.at(f"b.q.{it+1}"))
                            mm_maxstep(it + 1, (qc + 1) % 2, kc)
                        if kc < NKC:
                            kb = kc % 3
                            if kc >= 3:
                                ctx.wait("ac", ctx.at(f"b.exp.{it}.{kc-3}"))
                            elif it >= 1:
                                ctx.wait("ac", ctx.at(f"b.exp.{it-1}.{NKC-3+kc}"))
                            if kc == 0:
                                ctx.wait("gp", ctx.at(f"b.ncr.{it}"))
                            k_h = kh_sb[:, kc * 128:(kc + 1) * 128]
                            k_l = kl_sb[:, kc * 128:(kc + 1) * 128]
                            ctx.raw(lambda k_h=k_h, kb=kb, par=par: ctx.eng.matmul(
                                psS[kb][:, 0:512], k_h, qb[par][0][:],
                                start=True, stop=False))
                            ctx.raw(lambda k_h=k_h, kb=kb, par=par: ctx.eng.matmul(
                                psS[kb][:, 0:512], k_h, qb[par][1][:],
                                start=False, stop=False))
                            ctx.raw(lambda k_l=k_l, kb=kb, par=par: ctx.eng.matmul(
                                psS[kb][:, 0:512], k_l, qb[par][0][:],
                                start=False, stop=False))
                            ctx.op("mm", lambda kb=kb, nr=ncrow[it % 2]:
                                   ctx.eng.matmul(
                                       psS[kb][:, 0:512], negones[:], nr[:],
                                       start=False, stop=True))
                            ctx.ck(f"b.smm.{it}.{kc}", "mm")
                        if kc >= 1:
                            ka = kc - 1
                            kb = ka % 4
                            ctx.wait("ac", ctx.at(f"b.ph.{it}.{ka}"))
                            ctx.wait("ve", ctx.at(f"b.pl.{it}.{ka}"))
                            if ka == 0 and it >= 2:
                                ctx.wait("ve", ctx.at(f"b.att.{it-2}"))
                            v_h = vh_sb[:, ka, :]
                            v_l = vl_sb[:, ka, :]
                            st = (ka == 0)
                            lastmm = (ka == NKC - 1)
                            pav = psAV[it % 2]
                            ctx.raw(lambda v_h=v_h, kb=kb, st=st, pav=pav:
                                    ctx.eng.matmul(pav[:, 0:512], v_h, ph_sb[kb][:],
                                                   start=st, stop=False))
                            ctx.raw(lambda v_h=v_h, kb=kb, pav=pav: ctx.eng.matmul(
                                pav[:, 0:512], v_h, pl_sb[kb][:],
                                start=False, stop=False))
                            fn = (lambda v_l=v_l, kb=kb, lastmm=lastmm, pav=pav:
                                  ctx.eng.matmul(pav[:, 0:512], v_l, ph_sb[kb][:],
                                                 start=False, stop=lastmm))
                            if lastmm:
                                ctx.op("mm", fn)
                            else:
                                ctx.raw(fn)
                            ctx.ck(f"b.avp.{it}.{ka}", "mm")
                    ctx.ck(f"b.avmm.{h}.{qc}", "mm")
                    ctx.wait("ac", ctx.at(f"b.acch.{it}"))
                    ctx.wait("ve", ctx.at(f"b.accl.{it}"))
                    if it >= 1:
                        ctx.wait("ve", ctx.at(f"b.inv.{it-1}"))
                    ctx.raw(lambda: ctx.eng.matmul(
                        PN[0:1, 0:512], ones16[:], acch[:], start=True, stop=False))
                    ctx.op("mm", lambda: ctx.eng.matmul(
                        PN[0:1, 0:512], ones16[:], accl[:], start=False, stop=True))
                    ctx.ck(f"b.summ.{it}", "mm")
                elif eng == "ac":
                    for kc in range(NKC):
                        kb = kc % 3
                        pb = kc % 4
                        ctx.wait("mm", ctx.at(f"b.smm.{it}.{kc}"))
                        if kc >= 4:
                            ctx.wait("mm", ctx.at(f"b.avp.{it}.{kc-4}"))
                            ctx.wait("ve", ctx.at(f"b.acc.{it}.{kc-4}"))
                        ctx.op("ac", lambda kb=kb, pb=pb: ctx.eng.activation(
                            out=p_sb[pb][:], in_=psS[kb][:, 0:512], func=AF.Exp,
                            bias=stats[:, 12:13]))
                        ctx.ck(f"b.exp.{it}.{kc}", "ac")
                        ctx.op("ac", lambda pb=pb: ctx.eng.copy(
                            ph_sb[pb][:], p_sb[pb][:]))
                        ctx.ck(f"b.ph.{it}.{kc}", "ac")
                    ctx.wait("ve", ctx.at(f"b.accsum.{it}"))
                    ctx.op("ac", lambda: ctx.eng.copy(acch[:], acc_sb[0][:]))
                    ctx.ck(f"b.acch.{it}", "ac")
                elif eng == "ve":
                    if qc == 0:
                        ve_maxred(it)
                    if it >= 1:
                        ctx.wait("mm", ctx.at(f"b.summ.{it-1}"))
                        ctx.wait("ac", ctx.at(f"b.acch.{it-1}"))
                    ctx.op("ve", lambda: ctx.eng.memset(acc_sb[0][:], 0.0))
                    ctx.op("ve", lambda: ctx.eng.memset(acc_sb[1][:], 0.0))
                    ctx.ck(f"b.acc0.{it}", "ve")
                    for kc in range(NKC):
                        if pipe:
                            if kc == 0:
                                ve_maxinit(it + 1)
                            ve_maxstep(it + 1, kc)
                        pb = kc % 4
                        ab = kc % 2
                        ctx.wait("ac", ctx.at(f"b.ph.{it}.{kc}"))
                        if kc >= 4:
                            ctx.wait("mm", ctx.at(f"b.avp.{it}.{kc-4}"))
                        ctx.op("ve", lambda pb=pb: ctx.eng.tensor_tensor(
                            pl_sb[pb][:], p_sb[pb][:], ph_sb[pb][:], op=ALU.subtract))
                        ctx.ck(f"b.pl.{it}.{kc}", "ve")
                        ctx.op("ve", lambda pb=pb, ab=ab: ctx.eng.tensor_tensor(
                            acc_sb[ab][:], acc_sb[ab][:], p_sb[pb][:], op=ALU.add))
                        ctx.ck(f"b.acc.{it}.{kc}", "ve")
                    ctx.op("ve", lambda: ctx.eng.tensor_tensor(
                        acc_sb[0][:], acc_sb[0][:], acc_sb[1][:], op=ALU.add))
                    ctx.ck(f"b.accsum.{it}", "ve")
                    ctx.wait("ac", ctx.at(f"b.acch.{it}"))
                    ctx.op("ve", lambda: ctx.eng.tensor_tensor(
                        accl[:], acc_sb[0][:], acch[:], op=ALU.subtract))
                    ctx.ck(f"b.accl.{it}", "ve")
                    ctx.wait("mm", ctx.at(f"b.summ.{it}"))
                    ctx.op("ve", lambda: ctx.eng.reciprocal(invb[0:1, :], PN[0:1, 0:512]))
                    ctx.op("ve", lambda: ctx.eng.tensor_tensor(
                        nr_t[:, :], PN[0:1, 0:512], invb[0:1, :], op=ALU.mult))
                    ctx.op("ve", lambda: ctx.eng.tensor_scalar(
                        nr_t[:, :], nr_t[:, :], -1.0, 2.0,
                        op0=ALU.mult, op1=ALU.add))
                    ctx.op("ve", lambda: ctx.eng.tensor_tensor(
                        invb[0:1, :], invb[0:1, :], nr_t[:, :], op=ALU.mult))
                    ctx.ck(f"b.inv.{it}", "ve")
                    ctx.wait("gp", ctx.at(f"b.invbc.{it}"))
                    ctx.wait("mm", ctx.at(f"b.avmm.{h}.{qc}"))
                    if it >= 1:
                        ctx.wait("sy", ctx.at(f"b.atst.{it-1}"))
                    ctx.op("ve", lambda pav=psAV[it % 2]: ctx.eng.tensor_tensor(
                        att_sb[:], pav[:, 0:512], invb[:, :], op=ALU.mult))
                    ctx.ck(f"b.att.{it}", "ve")
                elif eng == "gp":
                    if qc == 0:
                        ctx.wait("ve", ctx.at(f"b.rmax.{it}.63"))
                        ctx.op("gp", lambda: ctx.eng.dma_start(
                            out=c_scr, in_=cmax[:, :]), 16)
                        ctx.ck(f"b.cst.{it}", "gp")
                        ctx.wait("gp", ctx.at(f"b.cst.{it}"))
                        ctx.op("gp", lambda nr=ncrow[it % 2]: ctx.eng.dma_start(
                            out=nr[:, :], in_=c_scr.transpose([1, 0]),
                            allow_slow_non_contiguous=True), 16)
                        ctx.ck(f"b.ncr.{it}", "gp")
                    if pipe:
                        ctx.wait("ve", ctx.at(f"b.rmax.{it+1}.63"))
                        ctx.op("gp", lambda: ctx.eng.dma_start(
                            out=c_scr, in_=cmax[:, :]), 16)
                        ctx.ck(f"b.cst.{it+1}", "gp")
                        ctx.wait("gp", ctx.at(f"b.cst.{it+1}"))
                        ctx.op("gp", lambda nr=ncrow[(it + 1) % 2]:
                               ctx.eng.dma_start(
                                   out=nr[:, :], in_=c_scr.transpose([1, 0]),
                                   allow_slow_non_contiguous=True), 16)
                        ctx.ck(f"b.ncr.{it+1}", "gp")
                    ctx.wait("ve", ctx.at(f"b.inv.{it}"))
                    ctx.op("gp", lambda: ctx.eng.dma_start(
                        out=bscr[0:1, 0:512], in_=invb[0:1, :]), 16)
                    ctx.ck(f"b.invst.{it}", "gp")
                    ctx.wait("gp", ctx.at(f"b.invst.{it}"))
                    ctx.op("gp", lambda: ctx.eng.dma_start(
                        out=invb[:, :], in_=bscr[0:1, 0:512].to_broadcast([128, 512])), 16)
                    ctx.ck(f"b.invbc.{it}", "gp")

    def phase_c(eng):
        if eng == "gp":
            ctx.wait("sy", ctx.at(f"b.atst.{NH * NQC - 1}"))
            from concourse.aot_env import allow_ici_all_to_all

            def do_cc():
                with allow_ici_all_to_all():
                    return ctx.eng.collective_compute(
                        "AllToAll", ALU.bypass,
                        replica_groups=[list(range(NC))],
                        ins=[a2a_in.opt()],
                        outs=[a2a_out.opt()],
                    )
            ctx.op("cc", do_cc)
            ctx.ck("c.done", "cc")

    def phase_d(eng):
        for c in range(ND):
            b = c % 2
            if eng == "sy":
                if c == 0:
                    ctx.wait("cc", ctx.at("c.done"))
                if c >= 2:
                    ctx.wait("ve", ctx.at(f"d.al.{c-2}"))
                ctx.op("sy", lambda c=c, b=b: ctx.eng.dma_start(
                    out=astrip[b][:], in_=a2a_out[c // 2, c % 2, :, :]), 16)
                ctx.ck(f"d.ast.{c}", "sy")
            elif eng == "ac":
                ctx.wait("sy", ctx.at(f"d.ast.{c}"))
                ctx.op("ac", lambda c=c, b=b: ctx.eng.copy(atth[:, c, :], astrip[b][:]))
                ctx.ck(f"d.ah.{c}", "ac")
            elif eng == "ve":
                ctx.wait("sy", ctx.at(f"d.ast.{c}"))
                ctx.wait("ac", ctx.at(f"d.ah.{c}"))
                ctx.op("ve", lambda c=c, b=b: ctx.eng.tensor_tensor(
                    attl[:, c, :], astrip[b][:], atth[:, c, :], op=ALU.subtract))
                ctx.ck(f"d.al.{c}", "ve")
        for half in range(2):
            for c in range(ND):
                b = c % 2
                if eng == "sy":
                    if half > 0 and c < 2:
                        ctx.wait("mm", ctx.at(f"d.pj.{half-1}.{NTT-1}"))
                    if c >= 2:
                        ctx.wait("ve", ctx.at(f"d.wl.{half}.{c-2}"))
                    ctx.op("sy", lambda half=half, c=c, b=b: ctx.eng.dma_start(
                        out=pstrip[b][:],
                        in_=wpT[c * 128:(c + 1) * 128,
                                half * 1024:(half + 1) * 1024]), 16)
                    ctx.ck(f"d.wst.{half}.{c}", "sy")
                elif eng == "ac":
                    ctx.wait("sy", ctx.at(f"d.wst.{half}.{c}"))
                    if half > 0 and c == 0:
                        ctx.wait("mm", ctx.at(f"d.pj.{half-1}.{NTT-1}"))
                    ctx.op("ac", lambda half=half, c=c, b=b: ctx.eng.copy(
                        wph[:, c, :], pstrip[b][:]))
                    ctx.ck(f"d.wh.{half}.{c}", "ac")
                elif eng == "ve":
                    ctx.wait("sy", ctx.at(f"d.wst.{half}.{c}"))
                    ctx.wait("ac", ctx.at(f"d.wh.{half}.{c}"))
                    if half > 0 and c == 0:
                        ctx.wait("mm", ctx.at(f"d.pj.{half-1}.{NTT-1}"))
                    ctx.op("ve", lambda half=half, c=c, b=b: ctx.eng.tensor_tensor(
                        wpl[:, c, :], pstrip[b][:], wph[:, c, :], op=ALU.subtract))
                    ctx.ck(f"d.wl.{half}.{c}", "ve")
            for t in range(NTT):
                gt = half * NTT + t
                if eng == "mm":
                    if half == 0 and t == 0:
                        ctx.wait("ve", ctx.at(f"d.al.{ND-1}"))
                    if t == 0:
                        ctx.wait("ve", ctx.at(f"d.wl.{half}.{ND-1}"))
                    if gt >= 1:
                        ctx.wait("ac", ctx.at(f"d.pev.{(gt-1)//NTT}.{(gt-1)%NTT}"))
                    for c in range(ND):
                        a_h = atth[:, c, t * 128:(t + 1) * 128]
                        a_l = attl[:, c, t * 128:(t + 1) * 128]
                        for n in range(2):
                            w_h = wph[:, c, n * 512:(n + 1) * 512]
                            w_l = wpl[:, c, n * 512:(n + 1) * 512]
                            st = (c == 0)
                            lastmm = (c == ND - 1)
                            ctx.raw(lambda a_h=a_h, w_h=w_h, n=n, st=st: ctx.eng.matmul(
                                psP[n][:, 0:512], a_h, w_h, start=st, stop=False))
                            ctx.raw(lambda a_h=a_h, w_l=w_l, n=n: ctx.eng.matmul(
                                psP[n][:, 0:512], a_h, w_l, start=False, stop=False))
                            fn = (lambda a_l=a_l, w_h=w_h, n=n, lastmm=lastmm:
                                  ctx.eng.matmul(psP[n][:, 0:512], a_l, w_h,
                                                 start=False, stop=lastmm))
                            if lastmm:
                                ctx.op("mm", fn)
                            else:
                                ctx.raw(fn)
                    ctx.ck(f"d.pj.{half}.{t}", "mm")
                elif eng == "ac":
                    ctx.wait("mm", ctx.at(f"d.pj.{half}.{t}"))
                    if gt >= 1:
                        ctx.wait("sy", ctx.at(f"d.post.{(gt-1)//NTT}.{(gt-1)%NTT}"))
                    ctx.op("ac", lambda half=half: ctx.eng.copy(
                        po_sb[:, half * 1024:half * 1024 + 512], psP[0][:, 0:512]))
                    ctx.op("ac", lambda half=half: ctx.eng.copy(
                        po_sb[:, half * 1024 + 512:half * 1024 + 1024],
                        psP[1][:, 0:512]))
                    ctx.ck(f"d.pev.{half}.{t}", "ac")
                elif eng == "sy":
                    ctx.wait("ac", ctx.at(f"d.pev.{half}.{t}"))
                    ctx.op("sy", lambda half=half, t=t: ctx.eng.dma_start(
                        out=po_scr[t, :, half * 1024:(half + 1) * 1024],
                        in_=po_sb[:, half * 1024:(half + 1) * 1024]), 16)
                    ctx.ck(f"d.post.{half}.{t}", "sy")
        for t in range(NTT):
            if eng == "sy":
                if t == 0:
                    ctx.wait("sy", ctx.at(f"d.post.1.{NTT-1}"))
                else:
                    ctx.wait("ve", ctx.at(f"d.ln2.{t-1}"))
                ctx.op("sy", lambda t=t: ctx.eng.dma_start(
                    out=po_sb[:], in_=po_scr[t, :, :]), 16)
                ctx.op("sy", lambda t=t: ctx.eng.dma_start(
                    out=xo_sb[:], in_=x_own[t * 128:(t + 1) * 128, :]), 16)
                ctx.ck(f"d.ld.{t}", "sy")
                ctx.wait("ve", ctx.at(f"d.x2.{t}"))
                ctx.op("sy", lambda t=t: ctx.eng.dma_start(
                    out=x2_scr[t, :, :], in_=xo_sb[:]), 16)
                ctx.ck(f"d.x2st.{t}", "sy")
                ctx.wait("ac", ctx.at(f"d.l2h.{t}"))
                ctx.wait("ve", ctx.at(f"d.l2l.{t}"))
                for hl in range(2):
                    src = d_l2h if hl == 0 else d_l2l
                    dst = ln2_scr[hl, :, :, t * 128:(t + 1) * 128].transpose([1, 0, 2])
                    ctx.op("sy", lambda dst=dst, src=src: ctx.eng.dma_start(
                        out=dst, in_=src[:, :, :]), 16)
                ctx.ck(f"d.l2st.{t}", "sy")
            elif eng == "ve":
                ctx.wait("sy", ctx.at(f"d.ld.{t}"))
                ctx.op("ve", lambda: ctx.eng.reduce_sum(stats[:, 0:1], po_sb[:], axis=AX.X))
                ctx.op("ve", lambda: ctx.eng.tensor_scalar_mul(
                    stats[:, 1:2], stats[:, 0:1], 1.0 / D))
                ctx.op("ve", lambda: ctx.eng.tensor_tensor(
                    stats[:, 2:3], stats[:, 1:2], stats[:, 1:2], op=ALU.mult))
                ctx.ck(f"d.vm.{t}", "ve")
                ctx.wait("ac", ctx.at(f"d.ssq.{t}"))
                ctx.op("ve", lambda: ctx.eng.scalar_tensor_tensor(
                    stats[:, 3:4], stats[:, 4:5], 1.0 / D, stats[:, 2:3],
                    op0=ALU.mult, op1=ALU.subtract))
                ctx.ck(f"d.var.{t}", "ve")
                ctx.wait("ac", ctx.at(f"d.std.{t}"))
                ctx.op("ve", lambda: ctx.eng.reciprocal(stats[:, 5:6], stats[:, 10:11]))
                ctx.op("ve", lambda: ctx.eng.tensor_tensor(
                    stats[:, 6:7], stats[:, 10:11], stats[:, 5:6], op=ALU.mult))
                ctx.op("ve", lambda: ctx.eng.tensor_scalar(
                    stats[:, 7:8], stats[:, 6:7], -1.0, 2.0, op0=ALU.mult, op1=ALU.add))
                ctx.op("ve", lambda: ctx.eng.tensor_tensor(
                    stats[:, 8:9], stats[:, 5:6], stats[:, 7:8], op=ALU.mult))
                ctx.op("ve", lambda: ctx.eng.scalar_tensor_tensor(
                    stats[:, 9:10], stats[:, 1:2], -1.0, stats[:, 8:9],
                    op0=ALU.mult, op1=ALU.mult))
                ctx.ck(f"d.rstd.{t}", "ve")
                if t >= 1:
                    ctx.wait("mm", ctx.at(f"d.tr.{t-1}"))
                ctx.op("ve", lambda: ctx.eng.tensor_scalar(
                    t_sb[:], po_sb[:], stats[:, 8:9], stats[:, 9:10],
                    op0=ALU.mult, op1=ALU.add))
                ctx.ck(f"d.t.{t}", "ve")
                ctx.op("ve", lambda: ctx.eng.tensor_tensor(
                    xo_sb[:], xo_sb[:], t_sb[:], op=ALU.add))
                ctx.ck(f"d.x2.{t}", "ve")
                ctx.op("ve", lambda: ctx.eng.reduce_sum(stats[:, 0:1], xo_sb[:], axis=AX.X))
                ctx.op("ve", lambda: ctx.eng.tensor_scalar_mul(
                    stats[:, 1:2], stats[:, 0:1], 1.0 / D))
                ctx.op("ve", lambda: ctx.eng.tensor_tensor(
                    stats[:, 2:3], stats[:, 1:2], stats[:, 1:2], op=ALU.mult))
                ctx.ck(f"d.vm2.{t}", "ve")
                ctx.wait("ac", ctx.at(f"d.ssq2.{t}"))
                ctx.op("ve", lambda: ctx.eng.scalar_tensor_tensor(
                    stats[:, 3:4], stats[:, 4:5], 1.0 / D, stats[:, 2:3],
                    op0=ALU.mult, op1=ALU.subtract))
                ctx.ck(f"d.var2.{t}", "ve")
                ctx.wait("ac", ctx.at(f"d.std2.{t}"))
                ctx.op("ve", lambda: ctx.eng.reciprocal(stats[:, 5:6], stats[:, 10:11]))
                ctx.op("ve", lambda: ctx.eng.tensor_tensor(
                    stats[:, 6:7], stats[:, 10:11], stats[:, 5:6], op=ALU.mult))
                ctx.op("ve", lambda: ctx.eng.tensor_scalar(
                    stats[:, 7:8], stats[:, 6:7], -1.0, 2.0, op0=ALU.mult, op1=ALU.add))
                ctx.op("ve", lambda: ctx.eng.tensor_tensor(
                    stats[:, 8:9], stats[:, 5:6], stats[:, 7:8], op=ALU.mult))
                ctx.op("ve", lambda: ctx.eng.scalar_tensor_tensor(
                    stats[:, 9:10], stats[:, 1:2], -1.0, stats[:, 8:9],
                    op0=ALU.mult, op1=ALU.mult))
                ctx.ck(f"d.rstd2.{t}", "ve")
                ctx.op("ve", lambda: ctx.eng.tensor_scalar(
                    t_sb[:], xo_sb[:], stats[:, 8:9], stats[:, 9:10],
                    op0=ALU.mult, op1=ALU.add))
                ctx.ck(f"d.ln2.{t}", "ve")
                for f in range(4):
                    fid = t * 4 + f
                    pT = psT[fid % 2]
                    ctx.wait("mm", ctx.at(f"d.fill.{fid}"))
                    ctx.wait("ac", ctx.at(f"d.feh.{fid}"))
                    if t >= 1 and f == 0:
                        ctx.wait("sy", ctx.at(f"d.l2st.{t-1}"))
                    for j in range(4):
                        dch = f * 4 + j
                        fn = (lambda dch=dch, j=j, pT=pT: ctx.eng.tensor_tensor(
                            d_l2l[:, dch, :], pT[:, j * 128:(j + 1) * 128],
                            d_l2h[:, dch, :], op=ALU.subtract))
                        if j == 3:
                            ctx.op("ve", fn)
                        else:
                            ctx.raw(fn)
                    ctx.ck(f"d.fel.{fid}", "ve")
                ctx.ck(f"d.l2l.{t}", "ve")
            elif eng == "ac":
                ctx.wait("sy", ctx.at(f"d.ld.{t}"))
                ctx.op("ac", lambda: ctx.eng.activation(
                    out=junk[:], in_=po_sb[:], func=AF.Square, accum_out=stats[:, 4:5]))
                ctx.ck(f"d.ssq.{t}", "ac")
                ctx.wait("ve", ctx.at(f"d.var.{t}"))
                ctx.op("ac", lambda: ctx.eng.activation(
                    out=stats[:, 10:11], in_=stats[:, 3:4], func=AF.Sqrt, bias=stats[:, 11:12]))
                ctx.ck(f"d.std.{t}", "ac")
                ctx.wait("ve", ctx.at(f"d.x2.{t}"))
                ctx.op("ac", lambda: ctx.eng.activation(
                    out=junk[:], in_=xo_sb[:], func=AF.Square, accum_out=stats[:, 4:5]))
                ctx.ck(f"d.ssq2.{t}", "ac")
                ctx.wait("ve", ctx.at(f"d.var2.{t}"))
                ctx.op("ac", lambda: ctx.eng.activation(
                    out=stats[:, 10:11], in_=stats[:, 3:4], func=AF.Sqrt, bias=stats[:, 11:12]))
                ctx.ck(f"d.std2.{t}", "ac")
                for f in range(4):
                    fid = t * 4 + f
                    pT = psT[fid % 2]
                    ctx.wait("mm", ctx.at(f"d.fill.{fid}"))
                    if t >= 1 and f == 0:
                        ctx.wait("sy", ctx.at(f"d.l2st.{t-1}"))
                    for j in range(4):
                        dch = f * 4 + j
                        fn = (lambda dch=dch, j=j, pT=pT: ctx.eng.copy(
                            d_l2h[:, dch, :], pT[:, j * 128:(j + 1) * 128]))
                        if j == 3:
                            ctx.op("ac", fn)
                        else:
                            ctx.raw(fn)
                    ctx.ck(f"d.feh.{fid}", "ac")
                ctx.ck(f"d.l2h.{t}", "ac")
            elif eng == "mm":
                ctx.wait("ve", ctx.at(f"d.ln2.{t}"))
                for f in range(4):
                    fid = t * 4 + f
                    pT = psT[fid % 2]
                    if fid >= 2:
                        ctx.wait("ac", ctx.at(f"d.feh.{fid-2}"))
                        ctx.wait("ve", ctx.at(f"d.fel.{fid-2}"))
                    for j in range(4):
                        dch = f * 4 + j
                        fn = (lambda dch=dch, j=j, pT=pT: ctx.eng.transpose(
                            pT[:, j * 128:(j + 1) * 128],
                            t_sb[:, dch * 128:(dch + 1) * 128], ident[:]))
                        if j == 3:
                            ctx.op("mm", fn)
                        else:
                            ctx.raw(fn)
                    ctx.ck(f"d.fill.{fid}", "mm")
                ctx.ck(f"d.tr.{t}", "mm")

    def phase_ef(eng):
        for tc2 in range(2):
            if eng == "sy":
                if tc2 == 0:
                    ctx.wait("sy", ctx.at(f"d.l2st.{NTT-1}"))
                else:
                    ctx.wait("ve", ctx.at(f"f.y.{tc2-1}.3"))
                    ctx.wait("sy", ctx.at(f"f.yst.{tc2-1}.3"))
                    ctx.wait("mm", ctx.at(f"e.gu.{tc2-1}.{NI-1}"))
                for hl in range(2):
                    src = ln2_scr[hl, :, :, tc2 * 512:(tc2 + 1) * 512].transpose([1, 0, 2])
                    dst = l2h if hl == 0 else l2l
                    ctx.op("sy", lambda src=src, dst=dst: ctx.eng.dma_start(
                        out=dst[:, :, :], in_=src), 16)
                ctx.ck(f"e.l2.{tc2}", "sy")
                for ich in range(NI):
                    b = ich % 2
                    gi = tc2 * NI + ich
                    if gi >= 2:
                        ctx.wait("mm", ctx.at(f"e.gu.{(gi-2)//NI}.{(gi-2)%NI}"))
                    if gi == 0:
                        ctx.wait("sd", ctx.at(f"a.qst.{NMINI-1}"))
                    ctx.op("sy", lambda ich=ich, b=b: ctx.eng.dma_start(
                        out=wgblk[b][:, :, :], in_=wg16[ich, :, :, :]), 16)
                    ctx.op("sy", lambda ich=ich, b=b: ctx.eng.dma_start(
                        out=wublk[b][:, :, :], in_=wu16[ich, :, :, :]), 16)
                    ctx.ck(f"e.wb.{tc2}.{ich}", "sy")
                for d2 in range(4):
                    for ich in range(NI):
                        b = ich % 2
                        gdi = tc2 * 4 * NI + d2 * NI + ich
                        if gdi >= 2:
                            pd = gdi - 2
                            ptc = pd // (4 * NI)
                            pr = pd % (4 * NI)
                            ctx.wait("mm", ctx.at(f"e.dmm.{ptc}.{pr//NI}.{pr%NI}"))
                        ctx.op("sy", lambda d2=d2, ich=ich, b=b: ctx.eng.dma_start(
                            out=wdblk[b][:], in_=wd16[d2, ich, :, :]), 16)
                        ctx.ck(f"e.wd.{tc2}.{d2}.{ich}", "sy")
            elif eng == "mm":
                ctx.wait("sy", ctx.at(f"e.l2.{tc2}"))
                for ich in range(NI):
                    b = ich % 2
                    ctx.wait("sy", ctx.at(f"e.wb.{tc2}.{ich}"))
                    gi = tc2 * NI + ich
                    if gi >= 1:
                        pg = gi - 1
                        ctx.wait("ac", ctx.at(f"e.silu.{pg//NI}.{pg%NI}"))
                        ctx.wait("ve", ctx.at(f"e.ff.{pg//NI}.{pg%NI}"))
                    if tc2 == 0 and ich == 0:
                        ctx.wait("ac", ctx.at(f"d.pev.1.{NTT-1}"))
                    for dch in range(ND):
                        st = (dch == 0)
                        lastmm = (dch == ND - 1)
                        ctx.raw(lambda b=b, dch=dch, st=st: ctx.eng.matmul(
                            psG[:, 0:512], wgblk[b][:, dch, :], l2h[:, dch, :],
                            start=st, stop=False))
                        fn = (lambda b=b, dch=dch, lastmm=lastmm: ctx.eng.matmul(
                            psG[:, 0:512], wgblk[b][:, dch, :], l2l[:, dch, :],
                            start=False, stop=lastmm))
                        if lastmm:
                            ctx.op("mm", fn)
                        else:
                            ctx.raw(fn)
                    ctx.ck(f"e.gmm.{tc2}.{ich}", "mm")
                    for dch in range(ND):
                        st = (dch == 0)
                        lastmm = (dch == ND - 1)
                        ctx.raw(lambda b=b, dch=dch, st=st: ctx.eng.matmul(
                            psU[:, 0:512], wublk[b][:, dch, :], l2h[:, dch, :],
                            start=st, stop=False))
                        fn = (lambda b=b, dch=dch, lastmm=lastmm: ctx.eng.matmul(
                            psU[:, 0:512], wublk[b][:, dch, :], l2l[:, dch, :],
                            start=False, stop=lastmm))
                        if lastmm:
                            ctx.op("mm", fn)
                        else:
                            ctx.raw(fn)
                    ctx.ck(f"e.umm.{tc2}.{ich}", "mm")
                    ctx.ck(f"e.gu.{tc2}.{ich}", "mm")
                for d2 in range(4):
                    for ich in range(NI):
                        b = ich % 2
                        ctx.wait("sy", ctx.at(f"e.wd.{tc2}.{d2}.{ich}"))
                        if d2 == 0:
                            ctx.wait("ac", ctx.at(f"e.fh.{tc2}.{ich}"))
                            ctx.wait("ve", ctx.at(f"e.fl.{tc2}.{ich}"))
                        if d2 == 0 and ich == 0 and tc2 >= 1:
                            ctx.wait("ve", ctx.at(f"e.dse.{tc2-1}.3"))
                        if d2 >= 1 and ich == 0:
                            ctx.wait("ve", ctx.at(f"e.dse.{tc2}.{d2-1}"))
                        for tg in range(4):
                            st = (ich == 0)
                            lastmm = (ich == NI - 1)
                            ctx.raw(lambda b=b, ich=ich, tg=tg, st=st: ctx.eng.matmul(
                                psD[tg][:, 0:512],
                                fuh[:, ich, tg * 128:(tg + 1) * 128],
                                wdblk[b][:], start=st, stop=False))
                            fn = (lambda b=b, ich=ich, tg=tg, lastmm=lastmm:
                                  ctx.eng.matmul(
                                      psD[tg][:, 0:512],
                                      ful[:, ich, tg * 128:(tg + 1) * 128],
                                      wdblk[b][:], start=False, stop=lastmm))
                            if tg == 3:
                                ctx.op("mm", fn)
                            else:
                                ctx.raw(fn)
                        ctx.ck(f"e.dmm.{tc2}.{d2}.{ich}", "mm")
                    ctx.ck(f"e.dcol.{tc2}.{d2}", "mm")
            elif eng == "ac":
                for ich in range(NI):
                    ctx.wait("mm", ctx.at(f"e.gmm.{tc2}.{ich}"))
                    ctx.op("ac", lambda ich=ich: ctx.eng.activation(
                        out=fg_sb[:], in_=psG[:, 0:512], func=AF.Silu,
                        scale=gs_sb[:, ich:ich + 1]))
                    ctx.ck(f"e.silu.{tc2}.{ich}", "ac")
                    ctx.wait("ve", ctx.at(f"e.ff.{tc2}.{ich}"))
                    ctx.op("ac", lambda ich=ich: ctx.eng.copy(
                        fuh[:, ich, :], junk[:, 0:512]))
                    ctx.ck(f"e.fh.{tc2}.{ich}", "ac")
            elif eng == "ve":
                for ich in range(NI):
                    ctx.wait("mm", ctx.at(f"e.umm.{tc2}.{ich}"))
                    ctx.wait("ac", ctx.at(f"e.silu.{tc2}.{ich}"))
                    if tc2 >= 1 and ich == 0:
                        ctx.wait("mm", ctx.at(f"e.dcol.{tc2-1}.3"))
                        ctx.wait("ac", ctx.at(f"f.ssq.{tc2-1}.3"))
                    ctx.op("ve", lambda ich=ich: ctx.eng.scalar_tensor_tensor(
                        junk[:, 0:512], psU[:, 0:512], us_sb[:, ich:ich + 1],
                        fg_sb[:], op0=ALU.mult, op1=ALU.mult))
                    ctx.ck(f"e.ff.{tc2}.{ich}", "ve")
                    ctx.wait("ac", ctx.at(f"e.fh.{tc2}.{ich}"))
                    ctx.op("ve", lambda ich=ich: ctx.eng.tensor_tensor(
                        ful[:, ich, :], junk[:, 0:512], fuh[:, ich, :],
                        op=ALU.subtract))
                    ctx.ck(f"e.fl.{tc2}.{ich}", "ve")
                for d2 in range(4):
                    ctx.wait("mm", ctx.at(f"e.dmm.{tc2}.{d2}.{NI-1}"))
                    if tc2 >= 1 and d2 == 0:
                        ctx.wait("sy", ctx.at(f"f.yst.{tc2-1}.3"))
                    for tg in range(4):
                        ctx.op("ve", lambda d2=d2, tg=tg: ctx.eng.tensor_tensor(
                            dout[:, tg, d2 * 512:(d2 + 1) * 512], psD[tg][:, 0:512],
                            dsb[:, d2 * 512:(d2 + 1) * 512], op=ALU.mult))
                    ctx.ck(f"e.dse.{tc2}.{d2}", "ve")
            for tg in range(4):
                gt = tc2 * 4 + tg
                if eng == "sy":
                    ctx.wait("ve", ctx.at(f"e.dse.{tc2}.3"))
                    if gt >= 1:
                        ctx.wait("ve", ctx.at(f"f.y.{(gt-1)//4}.{(gt-1)%4}"))
                    ctx.op("sy", lambda gt=gt: ctx.eng.dma_start(
                        out=x2t[:], in_=x2_scr[gt, :, :]), 16)
                    ctx.ck(f"f.x2.{tc2}.{tg}", "sy")
                    ctx.wait("ve", ctx.at(f"f.y.{tc2}.{tg}"))
                    ctx.op("sy", lambda gt=gt: ctx.eng.dma_start(
                        out=y_out[gt * 128:(gt + 1) * 128, :], in_=x2t[:]), 16)
                    ctx.ck(f"f.yst.{tc2}.{tg}", "sy")
                elif eng == "ve":
                    src = dout[:, tg, :]
                    if tg == 0:
                        pass
                    ctx.op("ve", lambda src=src: ctx.eng.reduce_sum(
                        stats[:, 0:1], src, axis=AX.X))
                    ctx.op("ve", lambda: ctx.eng.tensor_scalar_mul(
                        stats[:, 1:2], stats[:, 0:1], 1.0 / D))
                    ctx.op("ve", lambda: ctx.eng.tensor_tensor(
                        stats[:, 2:3], stats[:, 1:2], stats[:, 1:2], op=ALU.mult))
                    ctx.ck(f"f.vm.{tc2}.{tg}", "ve")
                    ctx.wait("ac", ctx.at(f"f.ssq.{tc2}.{tg}"))
                    ctx.op("ve", lambda: ctx.eng.scalar_tensor_tensor(
                        stats[:, 3:4], stats[:, 4:5], 1.0 / D, stats[:, 2:3],
                        op0=ALU.mult, op1=ALU.subtract))
                    ctx.ck(f"f.var.{tc2}.{tg}", "ve")
                    ctx.wait("ac", ctx.at(f"f.std.{tc2}.{tg}"))
                    ctx.op("ve", lambda: ctx.eng.reciprocal(stats[:, 5:6], stats[:, 10:11]))
                    ctx.op("ve", lambda: ctx.eng.tensor_tensor(
                        stats[:, 6:7], stats[:, 10:11], stats[:, 5:6], op=ALU.mult))
                    ctx.op("ve", lambda: ctx.eng.tensor_scalar(
                        stats[:, 7:8], stats[:, 6:7], -1.0, 2.0,
                        op0=ALU.mult, op1=ALU.add))
                    ctx.op("ve", lambda: ctx.eng.tensor_tensor(
                        stats[:, 8:9], stats[:, 5:6], stats[:, 7:8], op=ALU.mult))
                    ctx.op("ve", lambda: ctx.eng.scalar_tensor_tensor(
                        stats[:, 9:10], stats[:, 1:2], -1.0, stats[:, 8:9],
                        op0=ALU.mult, op1=ALU.mult))
                    ctx.ck(f"f.rstd.{tc2}.{tg}", "ve")
                    ctx.op("ve", lambda src=src: ctx.eng.tensor_scalar(
                        t_f[:], src, stats[:, 8:9], stats[:, 9:10],
                        op0=ALU.mult, op1=ALU.add))
                    ctx.ck(f"f.t.{tc2}.{tg}", "ve")
                    ctx.wait("sy", ctx.at(f"f.x2.{tc2}.{tg}"))
                    ctx.op("ve", lambda: ctx.eng.tensor_tensor(
                        x2t[:], x2t[:], t_f[:], op=ALU.add))
                    ctx.ck(f"f.y.{tc2}.{tg}", "ve")
                elif eng == "ac":
                    src = dout[:, tg, :]
                    ctx.wait("ve", ctx.at(f"f.vm.{tc2}.{tg}"))
                    ctx.op("ac", lambda src=src: ctx.eng.activation(
                        out=junk[:], in_=src, func=AF.Square, accum_out=stats[:, 4:5]))
                    ctx.ck(f"f.ssq.{tc2}.{tg}", "ac")
                    ctx.wait("ve", ctx.at(f"f.var.{tc2}.{tg}"))
                    ctx.op("ac", lambda: ctx.eng.activation(
                        out=stats[:, 10:11], in_=stats[:, 3:4], func=AF.Sqrt, bias=stats[:, 11:12]))
                    ctx.ck(f"f.std.{tc2}.{tg}", "ac")
        if eng == "sy":
            ctx.wait("sy", ctx.at("f.yst.1.3"))

    def run_engine(eng):
        phase_pre(eng)
        phase_a(eng)
        phase_b(eng)
        phase_c(eng)
        phase_d(eng)
        phase_ef(eng)

    for eng in ["sy", "ve", "ac", "mm", "gp"]:
        ctx.reset("count")
        run_engine(eng)

    with nc.Block() as block:
        @block.sync
        def _(sync):
            ctx.reset("emit", sync)
            run_engine("sy")

        @block.vector
        def _(vector):
            ctx.reset("emit", vector)
            run_engine("ve")

        @block.scalar
        def _(scalar):
            ctx.reset("emit", scalar)
            run_engine("ac")

        @block.tensor
        def _(tensor):
            ctx.reset("emit", tensor)
            run_engine("mm")

        @block.gpsimd
        def _(gpsimd):
            ctx.reset("emit", gpsimd)
            run_engine("gp")

    es.close()
    return nc


_BUILT = None


def kernel(L0_attn_x, L0_attn_w_qkv, L0_attn_w_proj, L0_ffn_gate_w, L0_ffn_up_w,
           L0_ffn_down_w, L0_ffn_gate_scale, L0_ffn_up_scale, L0_ffn_down_scale):
    global _BUILT
    x = np.ascontiguousarray(np.asarray(L0_attn_x, dtype=np.float32).reshape(S, D))
    w_qkv = np.asarray(L0_attn_w_qkv, dtype=np.float32)
    wp = np.ascontiguousarray(np.asarray(L0_attn_w_proj, dtype=np.float32).T)
    gw = np.ascontiguousarray(np.asarray(L0_ffn_gate_w, dtype=np.float32))
    uw = np.ascontiguousarray(np.asarray(L0_ffn_up_w, dtype=np.float32))
    dw = np.ascontiguousarray(np.asarray(L0_ffn_down_w, dtype=np.float32))
    gs = np.ascontiguousarray(np.asarray(L0_ffn_gate_scale, dtype=np.float32))
    us = np.ascontiguousarray(np.asarray(L0_ffn_up_scale, dtype=np.float32))
    ds = np.ascontiguousarray(np.asarray(L0_ffn_down_scale, dtype=np.float32))
    ident = np.eye(128, dtype=np.float32)

    if _BUILT is None:
        _BUILT = build_program()
    nc = _BUILT

    in_maps = []
    for c in range(NC):
        qrows = w_qkv[NH * HD * c:NH * HD * (c + 1), :]
        krows = w_qkv[D + NH * HD * c:D + NH * HD * (c + 1), :]
        vrows = w_qkv[2 * D + NH * HD * c:2 * D + NH * HD * (c + 1), :]
        in_maps.append({
            "x_full": x,
            "x_own": np.ascontiguousarray(x[T * c:T * (c + 1), :]),
            "wqkT": np.ascontiguousarray(np.concatenate([qrows, krows], axis=0).T),
            "wvT": np.ascontiguousarray(vrows.T),
            "wpT": wp,
            "wg": gw, "wu": uw, "wd": dw,
            "gs": gs, "us": us, "ds": ds,
            "ident": ident,
        })

    res = run_bass_kernel_spmd(nc, in_maps, core_ids=list(range(NC)))
    out = np.concatenate([res.results[c]["y"] for c in range(NC)], axis=0)
    return out.reshape(1, S, D)


# version 13
# speedup vs baseline: 1.0122x; 1.0122x over previous
import os
import numpy as np
from contextlib import ExitStack

os.environ.setdefault("TRN_ALLOW_ICI_ALL_TO_ALL", "1")

import concourse.bass as bass
import concourse.mybir as mybir
from concourse.bass_utils import run_bass_kernel_spmd

F32 = mybir.dt.float32
F16 = mybir.dt.float16
FP8 = mybir.dt.float8e4
F32R = mybir.dt.float32r
U8 = mybir.dt.uint8
AF = mybir.ActivationFunctionType
ALU = mybir.AluOpType
AX = mybir.AxisListType

NC = 8
S = 8192
D = 2048
H = 16
HD = 128
I = 6144
NH = 2
T = S // NC
ND = D // 128
NI = I // 128
NTC = S // 512
NT128 = S // 128
NKC = S // 128
NQC = S // 512
NTT = T // 128
INV_SQRT_HD = float(1.0 / np.sqrt(np.float32(HD)))
EPS = 1e-5
KB = 1024

TAPS = []


K_DETECT = os.environ.get("KDETECT", "0") == "1"
K_SERIAL = os.environ.get("KSERIAL", "0") == "1"


class Ctx:

    def __init__(self):
        self.mode = "count"
        self.cnt = {}
        self.cks = {}
        self.eng = None
        self.sems = {}

    def reset(self, mode, eng=None):
        self.mode = mode
        self.eng = eng
        self.cnt = {k: 0 for k in self.sems}

    AUTO_SERIAL = ("ve", "ac", "mm", "gc")
    DMA_CHAINS = ("sy", "sq", "gp")

    def op(self, chain, fn, k=1):
        if chain in self.AUTO_SERIAL and self.cnt[chain] > 0 and self.mode == "emit":
            self.eng.wait_ge(self.sems[chain], self.cnt[chain])
        if self.mode == "emit":
            fn().then_inc(self.sems[chain], k)
        self.cnt[chain] += k
        if K_SERIAL and chain in self.DMA_CHAINS and self.mode == "emit":
            self.eng.wait_ge(self.sems[chain], self.cnt[chain])
        return self.cnt[chain]

    def raw(self, fn):
        if self.mode == "emit":
            fn()

    def ck(self, key, chain):
        if self.mode == "count":
            self.cks[key] = self.cnt[chain]
        else:
            assert self.cks.get(key) == self.cnt[chain], (
                key, self.cks.get(key), self.cnt[chain])
        bnd = self.DMA_CHAINS + (("sc", "sd") if K_SERIAL else ())
        if chain in bnd and self.mode == "emit" and self.cnt[chain] > 0:
            self.eng.wait_ge(self.sems[chain], self.cnt[chain])
        return self.cks[key]

    def at(self, key):
        if self.mode == "count":
            return self.cks.get(key, 0)
        return self.cks[key]

    def wait(self, chain, count):
        if count and count > 0 and self.mode == "emit":
            self.eng.wait_ge(self.sems[chain], count)


def build_program():
    nc = bass.Bass("TRN2", target_bir_lowering=False,
                   detect_race_conditions=K_DETECT)

    x_full = nc.dram_tensor("x_full", [S, D], F32, kind="ExternalInput").ap()
    x_own = nc.dram_tensor("x_own", [T, D], F32, kind="ExternalInput").ap()
    wqkT = nc.dram_tensor("wqkT", [D, 512], F32, kind="ExternalInput").ap()
    wvT = nc.dram_tensor("wvT", [D, 256], F32, kind="ExternalInput").ap()
    wpT = nc.dram_tensor("wpT", [D, D], F32, kind="ExternalInput").ap()
    wg = nc.dram_tensor("wg", [D, I], F32, kind="ExternalInput").ap()
    wu = nc.dram_tensor("wu", [D, I], F32, kind="ExternalInput").ap()
    wd = nc.dram_tensor("wd", [I, D], F32, kind="ExternalInput").ap()
    gs_in = nc.dram_tensor("gs", [I], F32, kind="ExternalInput").ap()
    us_in = nc.dram_tensor("us", [I], F32, kind="ExternalInput").ap()
    ds_in = nc.dram_tensor("ds", [D], F32, kind="ExternalInput").ap()
    ident_in = nc.dram_tensor("ident", [128, 128], F32, kind="ExternalInput").ap()
    y_out = nc.dram_tensor("y", [T, D], F32, kind="ExternalOutput").ap()

    def scratch(name, shape, dt):
        kind = "ExternalOutput" if name in TAPS else "Internal"
        return nc.dram_tensor(name, shape, dt, kind=kind).ap()

    wg16 = scratch("wg16", [NI, 128, ND, 128], F16)
    wu16 = scratch("wu16", [NI, 128, ND, 128], F16)
    wd16 = scratch("wd16", [4, NI, 128, 512], F16)
    q_scr = scratch("q_scr", [2, NH, 128, S], F16)
    k_scr = scratch("k_scr", [2, NH, 128, S], F16)
    v_scr = scratch("v_scr", [2, NH, 128, NKC, 128], F16)
    a2a_in = scratch("a2a_in", [NC, NH, 128, T], F32)
    a2a_out = scratch("a2a_out", [NC, NH, 128, T], F32)
    po_scr = scratch("po_scr", [NTT, 128, D], F32)
    x2_scr = scratch("x2_scr", [NTT, 128, D], F32)
    ln2_scr = scratch("ln2_scr", [2, ND, 128, T], F16)
    bscr = scratch("bscr", [1, 2048], F32)
    c_scr = scratch("c_scr", [128, 4], F16)

    es = ExitStack()
    sb = lambda n_, sh, dt: es.enter_context(nc.sbuf_tensor(n_, sh, dt))

    AR1 = sb("AR1", [128, 86 * KB], U8)
    AR2 = sb("AR2", [128, 96 * KB], U8)
    ident = sb("ident_sb", [128, 128], F32)
    ones32 = sb("ones32", [128, 1], F32)
    ones16 = sb("ones16", [128, 1], F16)
    junk = sb("junk", [128, D], F32)
    stats = sb("stats", [128, 16], F32)
    nstat = sb("nstat", [1, 24], F32)
    negc = sb("negc", [128, NH], F32)
    invb = sb("invb", [128, 512], F32)
    nr_t = sb("nr_t", [1, 512], F32)
    cmax = sb("cmax", [128, 4], F16)
    rm_sb = sb("rm_sb", [128, 1], F32)
    ncrow = [sb("ncrow0", [1, 512], F16), sb("ncrow1", [1, 512], F16)]
    negones = sb("negones", [1, 128], F16)
    gs_sb = sb("gs_sb", [128, NI], F32)
    us_sb = sb("us_sb", [128, NI], F32)
    dsb = sb("dsb", [128, D], F32)

    def v1(ar, off, nbytes, dt, lead=None):
        ap = ar[:, off:off + nbytes].bitcast(dt)
        if lead is not None:
            ap = ap.rearrange("p (a b) -> p a b", a=lead)
        return ap

    wqkh = v1(AR1, 0, 16 * KB, F16, ND)
    wqkl = v1(AR1, 16 * KB, 16 * KB, F16, ND)
    wvh = v1(AR1, 32 * KB, 8 * KB, F16, ND)
    wvl = v1(AR1, 40 * KB, 8 * KB, F16, ND)
    qk_ev = [[[v1(AR1, 48 * KB + (e * 2 + hl) * KB, KB, F16)
               for hl in range(2)] for e in range(4)] for p in range(1)] * 2
    v_ev = [[v1(AR1, 56 * KB + hl * 2 * KB, 2 * KB, F16, 4)
             for hl in range(2)] for p in range(1)] * 2
    wstrip = [v1(AR1, 60 * KB, 2 * KB, F32), v1(AR1, 62 * KB, 2 * KB, F32)]
    q_in = [v1(AR1, 64 * KB + 4 * i * KB, 4 * KB, F32) for i in range(4)]
    q_8 = [v1(AR1, 80 * KB, KB, FP8), v1(AR1, 81 * KB, KB, FP8)]
    q_16 = [v1(AR1, 82 * KB, 2 * KB, F16), v1(AR1, 84 * KB, 2 * KB, F16)]
    xb = [v1(AR2, 0, 8 * KB, F32), v1(AR2, 8 * KB, 8 * KB, F32)]
    lnt = [v1(AR2, 16 * KB, 8 * KB, F32), v1(AR2, 24 * KB, 8 * KB, F32)]
    lns = [[v1(AR2, 32 * KB + (p * 2 + hl) * 16 * KB, 16 * KB, F16, ND)
            for hl in range(2)] for p in range(2)]

    kh_sb = v1(AR1, 0, 16 * KB, F16)
    kl_sb = v1(AR1, 16 * KB, 16 * KB, F16)
    vh_sb = v1(AR1, 32 * KB, 16 * KB, F16, NKC)
    vl_sb = v1(AR1, 48 * KB, 16 * KB, F16, NKC)
    qb = [[v1(AR2, (p * 2 + hl) * KB, KB, F16) for hl in range(2)] for p in range(2)]
    p_sb = [v1(AR2, 4 * KB + i * 2 * KB, 2 * KB, F32) for i in range(4)]
    ph_sb = [v1(AR2, 12 * KB + i * KB, KB, F16) for i in range(4)]
    pl_sb = [v1(AR2, 16 * KB + i * KB, KB, F16) for i in range(4)]
    acc_sb = [v1(AR2, 20 * KB, 2 * KB, F32), v1(AR2, 22 * KB, 2 * KB, F32)]
    acch = v1(AR2, 24 * KB, KB, F16)
    accl = v1(AR2, 25 * KB, KB, F16)
    att_sb = v1(AR2, 26 * KB, 2 * KB, F32)

    wph = v1(AR1, 0, 32 * KB, F16, ND)
    wpl = v1(AR1, 32 * KB, 32 * KB, F16, ND)
    pstrip = [v1(AR1, 64 * KB, 4 * KB, F32), v1(AR1, 68 * KB, 4 * KB, F32)]
    d_l2h = v1(AR1, 72 * KB, 4 * KB, F16, ND)
    d_l2l = v1(AR1, 76 * KB, 4 * KB, F16, ND)
    atth = v1(AR2, 0, 32 * KB, F16, ND)
    attl = v1(AR2, 32 * KB, 32 * KB, F16, ND)
    astrip = [v1(AR2, 64 * KB, 4 * KB, F32), v1(AR2, 68 * KB, 4 * KB, F32)]
    po_sb = v1(AR2, 72 * KB, 8 * KB, F32)
    xo_sb = v1(AR2, 80 * KB, 8 * KB, F32)
    t_sb = v1(AR2, 88 * KB, 8 * KB, F32)

    fuh = v1(AR2, 0, 48 * KB, F16, NI)
    ful = v1(AR2, 48 * KB, 48 * KB, F16, NI)
    l2h = v1(AR1, 0, 16 * KB, F16, ND)
    l2l = v1(AR1, 16 * KB, 16 * KB, F16, ND)
    x2t = v1(AR1, 0, 8 * KB, F32)
    t_f = v1(AR1, 8 * KB, 8 * KB, F32)
    wgblk = [v1(AR1, 32 * KB + p * 4 * KB, 4 * KB, F16, ND) for p in range(2)]
    wublk = [v1(AR1, 40 * KB + p * 4 * KB, 4 * KB, F16, ND) for p in range(2)]
    wdblk = [v1(AR1, 48 * KB + p * KB, KB, F16) for p in range(2)]
    dout = v1(AR1, 50 * KB, 32 * KB, F32, 4)
    fg_sb = v1(AR1, 82 * KB, 2 * KB, F32)

    ctx = Ctx()
    for s_ in ["sy", "sq", "sc", "sd", "gp", "gc", "mm", "ve", "ac", "cc"]:
        ctx.sems[s_] = es.enter_context(nc.semaphore(s_))

    B = [es.enter_context(nc.psum_tensor(f"PB{i}", [128, 512], F32)) for i in range(8)]
    BF16_ = mybir.dt.bfloat16
    psT = [B[0][:], B[6][:]]
    PQ = [B[1][:], B[2][:], B[3][:], B[4][:]]
    PV = [B[5][:], B[7][:]]
    PN = B[7][:]
    psS = [B[1][:], B[2][:], B[6][:]]
    psAV = [B[3][:], B[4][:]]
    psP = [B[1][:], B[2][:]]
    psM = [B[0][:], B[5][:]]
    psG = B[1][:]
    psU = B[2][:]
    psD = [B[3][:], B[4][:], B[5][:], B[7][:]]

    def phase_pre(eng):
        if eng == "sy":
            ctx.op("sy", lambda: ctx.eng.dma_start(out=ident[:], in_=ident_in), 16)
            ctx.op("sy", lambda: ctx.eng.dma_start(
                out=gs_sb[:], in_=gs_in.rearrange("(a p) -> p a", p=128),
                allow_slow_non_contiguous=True), 16)
            ctx.op("sy", lambda: ctx.eng.dma_start(
                out=us_sb[:], in_=us_in.rearrange("(a p) -> p a", p=128),
                allow_slow_non_contiguous=True), 16)
            ctx.op("sy", lambda: ctx.eng.dma_start(
                out=dsb[:], in_=ds_in[None, :].to_broadcast([128, D])), 16)
            ctx.ck("pre.sy", "sy")
        elif eng == "ve":
            ctx.op("ve", lambda: ctx.eng.memset(ones32[:], 1.0))
            ctx.op("ve", lambda: ctx.eng.memset(ones16[:], 1.0))
            ctx.op("ve", lambda: ctx.eng.memset(nstat[:, 0:4], 0.0))
            ctx.op("ve", lambda: ctx.eng.memset(stats[:, 11:12], EPS))
            ctx.op("ve", lambda: ctx.eng.memset(stats[:, 12:13], 2.0))
            ctx.op("ve", lambda: ctx.eng.memset(negones[:, :], -1.0))
            ctx.ck("pre.ve", "ve")

    NMINI = 288

    def q_mini_src_dst(i):
        if i < 96:
            r, qi = i // 6, i % 6
            src = wg[r * 128:(r + 1) * 128, qi * 1024:(qi + 1) * 1024]
            dst = wg16[qi * 8:(qi + 1) * 8, :, r, :].transpose([1, 0, 2])
            lead = 8
        elif i < 192:
            j = i - 96
            r, qi = j // 6, j % 6
            src = wu[r * 128:(r + 1) * 128, qi * 1024:(qi + 1) * 1024]
            dst = wu16[qi * 8:(qi + 1) * 8, :, r, :].transpose([1, 0, 2])
            lead = 8
        else:
            j = i - 192
            r, di = j // 2, j % 2
            src = wd[r * 128:(r + 1) * 128, di * 1024:(di + 1) * 1024]
            dst = wd16[di * 2:(di + 1) * 2, r, :, :].transpose([1, 0, 2])
            lead = 2
        return src, dst, lead

    def q_ld(i):
        src, dst, lead = q_mini_src_dst(i)
        ctx.op("sc", lambda src=src, b=i % 4: ctx.eng.dma_start(
            out=q_in[b][:], in_=src), 16)
        ctx.ck(f"a.qld.{i}", "sc")

    def emit_q_minis(eng, tc):
        if eng != "ac":
            return
        lo, hi = 18 * tc, min(NMINI, 18 * (tc + 1))
        if tc == 0:
            for i in range(3):
                q_ld(i)
        for i in range(lo, hi):
            b = i % 4
            src, dst, lead = q_mini_src_dst(i)
            if i + 3 < NMINI:
                q_ld(i + 3)
            ctx.wait("sc", ctx.at(f"a.qld.{i}"))
            ctx.op("ac", lambda b=b: ctx.eng.copy(q_8[b % 2][:], q_in[b][:]))
            ctx.ck(f"a.qc1.{i}", "ac")
            if i >= 1:
                ctx.wait("sd", ctx.at(f"a.qst.{i-1}"))
            ctx.op("ac", lambda b=b: ctx.eng.copy(q_16[b % 2][:], q_8[b % 2][:]))
            ctx.ck(f"a.qc2.{i}", "ac")
            ctx.wait("ac", ctx.at(f"a.qc2.{i}"))
            ctx.op("sd", lambda dst=dst, b=b, lead=lead: ctx.eng.dma_start(
                out=dst, in_=q_16[b % 2].rearrange("p (a c) -> p a c", a=lead)), 16)
            ctx.ck(f"a.qst.{i}", "sd")

    def phase_a(eng):
        for i in range(ND):
            b = i % 2
            if eng == "sy":
                if i >= 2:
                    ctx.wait("ve", ctx.at(f"a.wl.{i-2}"))
                ctx.op("sy", lambda i=i, b=b: ctx.eng.dma_start(
                    out=wstrip[b][:, 0:512], in_=wqkT[i * 128:(i + 1) * 128, :]), 16)
                ctx.ck(f"a.wqk.{i}", "sy")
            elif eng == "ac":
                ctx.wait("sy", ctx.at(f"a.wqk.{i}"))
                ctx.op("ac", lambda i=i, b=b: ctx.eng.copy(
                    wqkh[:, i, :], wstrip[b][:, 0:512]))
                ctx.ck(f"a.wh.{i}", "ac")
            elif eng == "ve":
                ctx.wait("sy", ctx.at(f"a.wqk.{i}"))
                ctx.wait("ac", ctx.at(f"a.wh.{i}"))
                ctx.op("ve", lambda i=i, b=b: ctx.eng.tensor_tensor(
                    wqkl[:, i, :], wstrip[b][:, 0:512], wqkh[:, i, :], op=ALU.subtract))
                ctx.ck(f"a.wl.{i}", "ve")
        for i in range(ND):
            b = i % 2
            if eng == "sy":
                ctx.wait("ve", ctx.at(f"a.wvl.{i-2}") if i >= 2
                         else ctx.at(f"a.wl.{ND-2+i}"))
                ctx.op("sy", lambda i=i, b=b: ctx.eng.dma_start(
                    out=wstrip[b][:, 0:256], in_=wvT[i * 128:(i + 1) * 128, :]), 16)
                ctx.ck(f"a.wv.{i}", "sy")
            elif eng == "ac":
                ctx.wait("sy", ctx.at(f"a.wv.{i}"))
                ctx.op("ac", lambda i=i, b=b: ctx.eng.copy(
                    wvh[:, i, :], wstrip[b][:, 0:256]))
                ctx.ck(f"a.wvh.{i}", "ac")
            elif eng == "ve":
                ctx.wait("sy", ctx.at(f"a.wv.{i}"))
                ctx.wait("ac", ctx.at(f"a.wvh.{i}"))
                ctx.op("ve", lambda i=i, b=b: ctx.eng.tensor_tensor(
                    wvl[:, i, :], wstrip[b][:, 0:256], wvh[:, i, :], op=ALU.subtract))
                ctx.ck(f"a.wvl.{i}", "ve")

        for tc in range(NTC):
            par = tc % 2
            for t in range(4):
                g = tc * 4 + t
                gb = g % 2
                if eng == "sy":
                    if g >= 2:
                        ctx.wait("ve", ctx.at(f"a.ln.{g-2}"))
                    ctx.op("sy", lambda g=g, gb=gb: ctx.eng.dma_start(
                        out=xb[gb][:], in_=x_full[g * 128:(g + 1) * 128, :]), 16)
                    ctx.ck(f"a.x.{g}", "sy")
                elif eng == "ve":
                    ctx.wait("sy", ctx.at(f"a.x.{g}"))
                    ctx.op("ve", lambda gb=gb: ctx.eng.reduce_sum(
                        stats[:, 0:1], xb[gb][:], axis=AX.X))
                    ctx.op("ve", lambda: ctx.eng.tensor_scalar_mul(
                        stats[:, 1:2], stats[:, 0:1], 1.0 / D))
                    ctx.op("ve", lambda: ctx.eng.tensor_tensor(
                        stats[:, 2:3], stats[:, 1:2], stats[:, 1:2], op=ALU.mult))
                    ctx.ck(f"a.vm.{g}", "ve")
                    ctx.wait("ac", ctx.at(f"a.ssq.{g}"))
                    ctx.op("ve", lambda: ctx.eng.scalar_tensor_tensor(
                        stats[:, 3:4], stats[:, 4:5], 1.0 / D, stats[:, 2:3],
                        op0=ALU.mult, op1=ALU.subtract))
                    ctx.ck(f"a.var.{g}", "ve")
                    ctx.wait("ac", ctx.at(f"a.std.{g}"))
                    ctx.op("ve", lambda: ctx.eng.reciprocal(stats[:, 5:6], stats[:, 10:11]))
                    ctx.op("ve", lambda: ctx.eng.tensor_tensor(
                        stats[:, 6:7], stats[:, 10:11], stats[:, 5:6], op=ALU.mult))
                    ctx.op("ve", lambda: ctx.eng.tensor_scalar(
                        stats[:, 7:8], stats[:, 6:7], -1.0, 2.0,
                        op0=ALU.mult, op1=ALU.add))
                    ctx.op("ve", lambda: ctx.eng.tensor_tensor(
                        stats[:, 8:9], stats[:, 5:6], stats[:, 7:8], op=ALU.mult))
                    ctx.op("ve", lambda: ctx.eng.scalar_tensor_tensor(
                        stats[:, 9:10], stats[:, 1:2], -1.0, stats[:, 8:9],
                        op0=ALU.mult, op1=ALU.mult))
                    ctx.ck(f"a.rstd.{g}", "ve")
                    if g >= 2:
                        ctx.wait("mm", ctx.at(f"a.tr.{g-2}"))
                    ctx.op("ve", lambda gb=gb: ctx.eng.tensor_scalar(
                        lnt[gb][:], xb[gb][:], stats[:, 8:9], stats[:, 9:10],
                        op0=ALU.mult, op1=ALU.add))
                    ctx.ck(f"a.ln.{g}", "ve")
                elif eng == "ac":
                    ctx.wait("sy", ctx.at(f"a.x.{g}"))
                    ctx.wait("ve", ctx.at(f"a.var.{g-1}") if g >= 1 else 0)
                    ctx.op("ac", lambda gb=gb: ctx.eng.activation(
                        out=junk[:], in_=xb[gb][:], func=AF.Square,
                        accum_out=stats[:, 4:5]))
                    ctx.ck(f"a.ssq.{g}", "ac")
                    ctx.wait("ve", ctx.at(f"a.var.{g}"))
                    if g == 0:
                        ctx.wait("ve", ctx.at("pre.ve"))
                    ctx.op("ac", lambda: ctx.eng.activation(
                        out=stats[:, 10:11], in_=stats[:, 3:4], func=AF.Sqrt, bias=stats[:, 11:12]))
                    ctx.ck(f"a.std.{g}", "ac")
                elif eng == "mm":
                    ctx.wait("ve", ctx.at(f"a.ln.{g}"))
                    for f in range(4):
                        fid = g * 4 + f
                        pT = psT[fid % 2]
                        if fid >= 2:
                            ctx.wait("ac", ctx.at(f"a.feh.{fid-2}"))
                            ctx.wait("ve", ctx.at(f"a.fel.{fid-2}"))
                        for j in range(4):
                            dch = f * 4 + j
                            fn = (lambda gb=gb, dch=dch, j=j, pT=pT: ctx.eng.transpose(
                                pT[:, j * 128:(j + 1) * 128],
                                lnt[gb][:, dch * 128:(dch + 1) * 128], ident[:]))
                            if j == 3:
                                ctx.op("mm", fn)
                            else:
                                ctx.raw(fn)
                        ctx.ck(f"a.fill.{fid}", "mm")
                    ctx.ck(f"a.tr.{g}", "mm")
                for f in range(4):
                    fid = g * 4 + f
                    pT = psT[fid % 2]
                    if eng == "ac":
                        ctx.wait("mm", ctx.at(f"a.fill.{fid}"))
                        if tc >= 2 and t == 0 and f == 0:
                            ctx.wait("mm", ctx.at(f"a.qkv.{tc-2}"))
                        for j in range(4):
                            dch = f * 4 + j
                            fn = (lambda par=par, dch=dch, t=t, j=j, pT=pT: ctx.eng.copy(
                                lns[par][0][:, dch, t * 128:(t + 1) * 128],
                                pT[:, j * 128:(j + 1) * 128]))
                            if j == 3:
                                ctx.op("ac", fn)
                            else:
                                ctx.raw(fn)
                        ctx.ck(f"a.feh.{fid}", "ac")
                    elif eng == "ve":
                        ctx.wait("mm", ctx.at(f"a.fill.{fid}"))
                        ctx.wait("ac", ctx.at(f"a.feh.{fid}"))
                        if tc >= 2 and t == 0 and f == 0:
                            ctx.wait("mm", ctx.at(f"a.qkv.{tc-2}"))
                        for j in range(4):
                            dch = f * 4 + j
                            fn = (lambda par=par, dch=dch, t=t, j=j, pT=pT:
                                  ctx.eng.tensor_tensor(
                                      lns[par][1][:, dch, t * 128:(t + 1) * 128],
                                      pT[:, j * 128:(j + 1) * 128],
                                      lns[par][0][:, dch, t * 128:(t + 1) * 128],
                                      op=ALU.subtract))
                            if j == 3:
                                ctx.op("ve", fn)
                            else:
                                ctx.raw(fn)
                        ctx.ck(f"a.fel.{fid}", "ve")

            if eng == "mm":
                lastfid = (tc * 4 + 3) * 4 + 3
                ctx.wait("ac", ctx.at(f"a.feh.{lastfid}"))
                ctx.wait("ve", ctx.at(f"a.fel.{lastfid}"))
                if tc == 0:
                    ctx.wait("ve", ctx.at(f"a.wl.{ND-1}"))
                    ctx.wait("ve", ctx.at(f"a.wvl.{ND-1}"))
                for e in range(4):
                    if tc >= 1:
                        ctx.wait("ac", ctx.at(f"a.evq.{tc-1}.{e}"))
                        ctx.wait("ve", ctx.at(f"a.evql.{tc-1}.{e}"))
                    for dch in range(ND):
                        w_h = wqkh[:, dch, e * 128:(e + 1) * 128]
                        w_l = wqkl[:, dch, e * 128:(e + 1) * 128]
                        r_h = lns[par][0][:, dch, :]
                        r_l = lns[par][1][:, dch, :]
                        st = (dch == 0)
                        ctx.raw(lambda e=e, w_h=w_h, r_h=r_h, st=st: ctx.eng.matmul(
                            PQ[e][:, 0:512], w_h, r_h, start=st, stop=False))
                        ctx.raw(lambda e=e, w_h=w_h, r_l=r_l: ctx.eng.matmul(
                            PQ[e][:, 0:512], w_h, r_l, start=False, stop=False))
                        lastmm = (dch == ND - 1)
                        fn = (lambda e=e, w_l=w_l, r_h=r_h, lastmm=lastmm: ctx.eng.matmul(
                            PQ[e][:, 0:512], w_l, r_h, start=False, stop=lastmm))
                        if lastmm:
                            ctx.op("mm", fn)
                        else:
                            ctx.raw(fn)
                    ctx.ck(f"a.mmq.{tc}.{e}", "mm")
                for vt in range(4):
                    gv = tc * 4 + vt
                    col = (vt % 2) * 256
                    if gv >= 2:
                        ctx.wait("ac", ctx.at(f"a.evvh.{(gv-2)//4}.{(gv-2)%4}"))
                        ctx.wait("ve", ctx.at(f"a.evvl.{(gv-2)//4}.{(gv-2)%4}"))
                    pv = PV[vt % 2]
                    for dch in range(ND):
                        a_h = lns[par][0][:, dch, vt * 128:(vt + 1) * 128]
                        a_l = lns[par][1][:, dch, vt * 128:(vt + 1) * 128]
                        st = (dch == 0)
                        ctx.raw(lambda pv=pv, a_h=a_h, dch=dch, st=st: ctx.eng.matmul(
                            pv[:, 0:256], a_h, wvh[:, dch, :],
                            start=st, stop=False))
                        ctx.raw(lambda pv=pv, a_h=a_h, dch=dch: ctx.eng.matmul(
                            pv[:, 0:256], a_h, wvl[:, dch, :],
                            start=False, stop=False))
                        lastmm = (dch == ND - 1)
                        fn = (lambda pv=pv, a_l=a_l, dch=dch, lastmm=lastmm:
                              ctx.eng.matmul(pv[:, 0:256], a_l, wvh[:, dch, :],
                                             start=False, stop=lastmm))
                        if lastmm:
                            ctx.op("mm", fn)
                        else:
                            ctx.raw(fn)
                    ctx.ck(f"a.mmv.{tc}.{vt}", "mm")
                ctx.ck(f"a.qkv.{tc}", "mm")
            elif eng == "ac":
                for e in range(4):
                    ctx.wait("mm", ctx.at(f"a.mmq.{tc}.{e}"))
                    if tc >= 1:
                        ctx.wait("sq", ctx.at(f"a.stq.{tc-1}.{e}"))
                    scale = INV_SQRT_HD if e < 2 else 1.0
                    ctx.op("ac", lambda e=e, par=par, scale=scale: ctx.eng.activation(
                        out=qk_ev[par][e][0][:], in_=PQ[e][:, 0:512],
                        func=AF.Copy, scale=scale))
                    ctx.ck(f"a.evq.{tc}.{e}", "ac")
                for vt in range(4):
                    ctx.wait("mm", ctx.at(f"a.mmv.{tc}.{vt}"))
                    if tc >= 1 and vt == 0:
                        ctx.wait("sq", ctx.at(f"a.stv.{tc-1}"))
                    ctx.op("ac", lambda vt=vt, par=par: ctx.eng.activation(
                        out=v_ev[par][0][:, vt, :], in_=PV[vt % 2][:, 0:256],
                        func=AF.Copy))
                    ctx.ck(f"a.evvh.{tc}.{vt}", "ac")
                    if vt == 3:
                        emit_q_minis("ac", tc)
            elif eng == "ve":
                for e in range(4):
                    ctx.wait("mm", ctx.at(f"a.mmq.{tc}.{e}"))
                    ctx.wait("ac", ctx.at(f"a.evq.{tc}.{e}"))
                    if tc >= 1:
                        ctx.wait("sq", ctx.at(f"a.stq.{tc-1}.{e}"))
                    scale = INV_SQRT_HD if e < 2 else 1.0
                    ctx.op("ve", lambda e=e, par=par, scale=scale:
                           ctx.eng.scalar_tensor_tensor(
                               qk_ev[par][e][1][:], PQ[e][:, 0:512], scale,
                               qk_ev[par][e][0][:], op0=ALU.mult, op1=ALU.subtract))
                    ctx.ck(f"a.evql.{tc}.{e}", "ve")
                for vt in range(4):
                    ctx.wait("mm", ctx.at(f"a.mmv.{tc}.{vt}"))
                    ctx.wait("ac", ctx.at(f"a.evvh.{tc}.{vt}"))
                    if tc >= 1 and vt == 0:
                        ctx.wait("sq", ctx.at(f"a.stv.{tc-1}"))
                    ctx.op("ve", lambda vt=vt, par=par: ctx.eng.tensor_tensor(
                        v_ev[par][1][:, vt, :], PV[vt % 2][:, 0:256],
                        v_ev[par][0][:, vt, :], op=ALU.subtract))
                    ctx.ck(f"a.evvl.{tc}.{vt}", "ve")
            elif eng == "sq":
                pass
            if eng == "sy":
                for e in range(4):
                    ctx.wait("ac", ctx.at(f"a.evq.{tc}.{e}"))
                    ctx.wait("ve", ctx.at(f"a.evql.{tc}.{e}"))
                    hd = e % 2
                    scr = q_scr if e < 2 else k_scr
                    for hl in range(2):
                        ctx.op("sq", lambda scr=scr, hl=hl, hd=hd, tc=tc, e=e, par=par:
                               ctx.eng.dma_start(
                                   out=scr[hl, hd, :, tc * 512:(tc + 1) * 512],
                                   in_=qk_ev[par][e][hl][:]), 16)
                    ctx.ck(f"a.stq.{tc}.{e}", "sq")
                ctx.wait("ac", ctx.at(f"a.evvh.{tc}.3"))
                ctx.wait("ve", ctx.at(f"a.evvl.{tc}.3"))
                for hl in range(2):
                    for hd in range(NH):
                        dst = v_scr[hl, hd, :, tc * 4:(tc + 1) * 4, :]
                        src = v_ev[par][hl][:, :, hd * 128:(hd + 1) * 128]
                        ctx.op("sq", lambda dst=dst, src=src: ctx.eng.dma_start(
                            out=dst, in_=src), 16)
                ctx.ck(f"a.stv.{tc}", "sq")
                emit_q_minis("sy", tc)

    def phase_b(eng):
        def mm_maxstep(it, par, j):
            sub, km = j // 16, j % 16
            bank = psM[j % 2]
            if j >= 2:
                ctx.wait("ve", ctx.at(f"b.rmax.{it}.{j-2}"))
            elif it >= 1:
                ctx.wait("ve", ctx.at(f"b.rmax.{it-1}.{62+j}"))
            ctx.op("mm", lambda bank=bank, sub=sub, km=km, par=par:
                   ctx.eng.matmul(
                       bank[:, 0:512],
                       qb[par][0][:, sub * 128:(sub + 1) * 128],
                       kh_sb[:, km * 512:(km + 1) * 512],
                       start=True, stop=True))
            ctx.ck(f"b.mx.{it}.{j}", "mm")

        def mm_maxpass(it, par):
            for j in range(64):
                mm_maxstep(it, par, j)

        def ve_maxinit(it):
            if it >= 1:
                ctx.wait("gp", ctx.at(f"b.cst.{it-1}"))
            ctx.op("ve", lambda: ctx.eng.memset(cmax[:, :], -60000.0))

        def ve_maxstep(it, j):
            sub = j // 16
            ctx.wait("mm", ctx.at(f"b.mx.{it}.{j}"))
            ctx.op("ve", lambda j=j: ctx.eng.reduce_max(
                rm_sb[:, :], psM[j % 2][:, 0:512], axis=AX.X))
            ctx.op("ve", lambda sub=sub: ctx.eng.tensor_tensor(
                cmax[:, sub:sub + 1], cmax[:, sub:sub + 1],
                rm_sb[:, :], op=ALU.max))
            ctx.ck(f"b.rmax.{it}.{j}", "ve")

        def ve_maxred(it):
            ve_maxinit(it)
            for j in range(64):
                ve_maxstep(it, j)

        MID = 16
        for h in range(NH):
            for qc in range(NQC):
                it = h * NQC + qc
                par = qc % 2
                pipe = qc < NQC - 1
                if eng == "sy":
                    if qc == 0:
                        if h == 0:
                            ctx.wait("sq", ctx.at(f"a.stq.{NTC-1}.3"))
                            ctx.wait("sq", ctx.at(f"a.stv.{NTC-1}"))
                        else:
                            ctx.wait("mm", ctx.at(f"b.avmm.{h-1}.{NQC-1}"))
                        for hl in range(2):
                            ctx.op("sy", lambda hl=hl, h=h: ctx.eng.dma_start(
                                out=(kh_sb if hl == 0 else kl_sb)[:],
                                in_=k_scr[hl, h, :, :]), 16)
                            ctx.op("sy", lambda hl=hl, h=h: ctx.eng.dma_start(
                                out=(vh_sb if hl == 0 else vl_sb)[:, :, :],
                                in_=v_scr[hl, h, :, :, :]), 16)
                        ctx.ck(f"b.kv.{h}", "sy")
                        ctx.op("sy", lambda h=h, qc=qc, par=par: ctx.eng.dma_start(
                            out=qb[par][0][:],
                            in_=q_scr[0, h, :, qc * 512:(qc + 1) * 512]), 16)
                        ctx.op("sy", lambda h=h, qc=qc, par=par: ctx.eng.dma_start(
                            out=qb[par][1][:],
                            in_=q_scr[1, h, :, qc * 512:(qc + 1) * 512]), 16)
                        ctx.ck(f"b.q.{it}", "sy")
                    if pipe:
                        if it >= 1:
                            ctx.wait("mm", ctx.at(f"b.avmm.{(it-1)//NQC}.{(it-1)%NQC}"))
                        np_ = (qc + 1) % 2
                        ctx.op("sy", lambda h=h, qc=qc, np_=np_: ctx.eng.dma_start(
                            out=qb[np_][0][:],
                            in_=q_scr[0, h, :, (qc + 1) * 512:(qc + 2) * 512]), 16)
                        ctx.op("sy", lambda h=h, qc=qc, np_=np_: ctx.eng.dma_start(
                            out=qb[np_][1][:],
                            in_=q_scr[1, h, :, (qc + 1) * 512:(qc + 2) * 512]), 16)
                        ctx.ck(f"b.q.{it+1}", "sy")
                    if it >= 1:
                        ctx.wait("ve", ctx.at(f"b.att.{it-1}"))
                        pq = (it - 1) % NQC
                        ph_ = (it - 1) // NQC
                        d = pq // 2
                        off = (pq % 2) * 512
                        ctx.op("sy", lambda d=d, off=off, ph_=ph_: ctx.eng.dma_start(
                            out=a2a_in[d, ph_, :, off:off + 512], in_=att_sb[:]), 16)
                        ctx.ck(f"b.atst.{it-1}", "sy")
                    if it == NH * NQC - 1:
                        ctx.wait("ve", ctx.at(f"b.att.{it}"))
                        d = qc // 2
                        off = (qc % 2) * 512
                        ctx.op("sy", lambda d=d, off=off, h=h: ctx.eng.dma_start(
                            out=a2a_in[d, h, :, off:off + 512], in_=att_sb[:]), 16)
                        ctx.ck(f"b.atst.{it}", "sy")
                elif eng == "mm":
                    if qc == 0:
                        ctx.wait("sy", ctx.at(f"b.kv.{h}"))
                        ctx.wait("sy", ctx.at(f"b.q.{it}"))
                        mm_maxpass(it, par)
                    for kc in range(NKC + 1):
                        if pipe and kc < NKC:
                            if kc == 0:
                                ctx.wait("sy", ctx.at(f"b.q.{it+1}"))
                            mm_maxstep(it + 1, (qc + 1) % 2, kc)
                        if kc < NKC:
                            kb = kc % 3
                            if kc >= 3:
                                ctx.wait("ac", ctx.at(f"b.exp.{it}.{kc-3}"))
                            elif it >= 1:
                                ctx.wait("ac", ctx.at(f"b.exp.{it-1}.{NKC-3+kc}"))
                            if kc == 0:
                                ctx.wait("gp", ctx.at(f"b.ncr.{it}"))
                            k_h = kh_sb[:, kc * 128:(kc + 1) * 128]
                            k_l = kl_sb[:, kc * 128:(kc + 1) * 128]
                            ctx.raw(lambda k_h=k_h, kb=kb, par=par: ctx.eng.matmul(
                                psS[kb][:, 0:512], k_h, qb[par][0][:],
                                start=True, stop=False))
                            ctx.raw(lambda k_h=k_h, kb=kb, par=par: ctx.eng.matmul(
                                psS[kb][:, 0:512], k_h, qb[par][1][:],
                                start=False, stop=False))
                            ctx.raw(lambda k_l=k_l, kb=kb, par=par: ctx.eng.matmul(
                                psS[kb][:, 0:512], k_l, qb[par][0][:],
                                start=False, stop=False))
                            ctx.op("mm", lambda kb=kb, nr=ncrow[it % 2]:
                                   ctx.eng.matmul(
                                       psS[kb][:, 0:512], negones[:], nr[:],
                                       start=False, stop=True))
                            ctx.ck(f"b.smm.{it}.{kc}", "mm")
                        if kc >= 1:
                            ka = kc - 1
                            kb = ka % 4
                            ctx.wait("ac", ctx.at(f"b.ph.{it}.{ka}"))
                            ctx.wait("ve", ctx.at(f"b.pl.{it}.{ka}"))
                            if ka == 0 and it >= 2:
                                ctx.wait("ve", ctx.at(f"b.att.{it-2}"))
                            v_h = vh_sb[:, ka, :]
                            v_l = vl_sb[:, ka, :]
                            st = (ka == 0)
                            lastmm = (ka == NKC - 1)
                            pav = psAV[it % 2]
                            ctx.raw(lambda v_h=v_h, kb=kb, st=st, pav=pav:
                                    ctx.eng.matmul(pav[:, 0:512], v_h, ph_sb[kb][:],
                                                   start=st, stop=False))
                            ctx.raw(lambda v_h=v_h, kb=kb, pav=pav: ctx.eng.matmul(
                                pav[:, 0:512], v_h, pl_sb[kb][:],
                                start=False, stop=False))
                            fn = (lambda v_l=v_l, kb=kb, lastmm=lastmm, pav=pav:
                                  ctx.eng.matmul(pav[:, 0:512], v_l, ph_sb[kb][:],
                                                 start=False, stop=lastmm))
                            if lastmm:
                                ctx.op("mm", fn)
                            else:
                                ctx.raw(fn)
                            ctx.ck(f"b.avp.{it}.{ka}", "mm")
                    ctx.ck(f"b.avmm.{h}.{qc}", "mm")
                    ctx.wait("ac", ctx.at(f"b.acch.{it}"))
                    ctx.wait("ve", ctx.at(f"b.accl.{it}"))
                    if it >= 1:
                        ctx.wait("ve", ctx.at(f"b.inv.{it-1}"))
                    ctx.raw(lambda: ctx.eng.matmul(
                        PN[0:1, 0:512], ones16[:], acch[:], start=True, stop=False))
                    ctx.op("mm", lambda: ctx.eng.matmul(
                        PN[0:1, 0:512], ones16[:], accl[:], start=False, stop=True))
                    ctx.ck(f"b.summ.{it}", "mm")
                elif eng == "ac":
                    for kc in range(NKC):
                        kb = kc % 3
                        pb = kc % 4
                        ctx.wait("mm", ctx.at(f"b.smm.{it}.{kc}"))
                        if kc >= 4:
                            ctx.wait("mm", ctx.at(f"b.avp.{it}.{kc-4}"))
                            ctx.wait("ve", ctx.at(f"b.acc.{it}.{kc-4}"))
                        ctx.op("ac", lambda kb=kb, pb=pb: ctx.eng.activation(
                            out=p_sb[pb][:], in_=psS[kb][:, 0:512], func=AF.Exp,
                            bias=stats[:, 12:13]))
                        ctx.ck(f"b.exp.{it}.{kc}", "ac")
                        ctx.op("ac", lambda pb=pb: ctx.eng.copy(
                            ph_sb[pb][:], p_sb[pb][:]))
                        ctx.ck(f"b.ph.{it}.{kc}", "ac")
                    ctx.wait("ve", ctx.at(f"b.acc.{it}.{NKC-1}"))
                    ctx.op("ac", lambda: ctx.eng.copy(acch[:], acc_sb[1][:]))
                    ctx.ck(f"b.acch.{it}", "ac")
                elif eng == "ve":
                    if qc == 0:
                        ve_maxred(it)
                    if it >= 1:
                        ctx.wait("mm", ctx.at(f"b.summ.{it-1}"))
                        ctx.wait("ac", ctx.at(f"b.acch.{it-1}"))
                    ctx.op("ve", lambda: ctx.eng.memset(acc_sb[1][:], 0.0))
                    ctx.ck(f"b.acc0.{it}", "ve")
                    for kc in range(NKC):
                        if pipe:
                            if kc == 0:
                                ve_maxinit(it + 1)
                            ve_maxstep(it + 1, kc)
                        pb = kc % 4
                        ab = kc % 2
                        ctx.wait("ac", ctx.at(f"b.ph.{it}.{kc}"))
                        if kc >= 4:
                            ctx.wait("mm", ctx.at(f"b.avp.{it}.{kc-4}"))
                        ctx.op("ve", lambda pb=pb: ctx.eng.tensor_tensor(
                            pl_sb[pb][:], p_sb[pb][:], ph_sb[pb][:], op=ALU.subtract))
                        ctx.ck(f"b.pl.{it}.{kc}", "ve")
                        ctx.op("ve", lambda pb=pb, ab=ab: ctx.eng.tensor_tensor(
                            acc_sb[ab][:], acc_sb[1 - ab][:], p_sb[pb][:], op=ALU.add))
                        ctx.ck(f"b.acc.{it}.{kc}", "ve")
                    ctx.wait("ac", ctx.at(f"b.acch.{it}"))
                    ctx.op("ve", lambda: ctx.eng.tensor_tensor(
                        accl[:], acc_sb[1][:], acch[:], op=ALU.subtract))
                    ctx.ck(f"b.accl.{it}", "ve")
                    ctx.wait("mm", ctx.at(f"b.summ.{it}"))
                    ctx.op("ve", lambda: ctx.eng.reciprocal(invb[0:1, :], PN[0:1, 0:512]))
                    ctx.op("ve", lambda: ctx.eng.tensor_tensor(
                        nr_t[:, :], PN[0:1, 0:512], invb[0:1, :], op=ALU.mult))
                    ctx.op("ve", lambda: ctx.eng.tensor_scalar(
                        nr_t[:, :], nr_t[:, :], -1.0, 2.0,
                        op0=ALU.mult, op1=ALU.add))
                    ctx.op("ve", lambda: ctx.eng.tensor_tensor(
                        invb[0:1, :], invb[0:1, :], nr_t[:, :], op=ALU.mult))
                    ctx.ck(f"b.inv.{it}", "ve")
                    ctx.wait("gp", ctx.at(f"b.invbc.{it}"))
                    ctx.wait("mm", ctx.at(f"b.avmm.{h}.{qc}"))
                    if it >= 1:
                        ctx.wait("sy", ctx.at(f"b.atst.{it-1}"))
                    ctx.op("ve", lambda pav=psAV[it % 2]: ctx.eng.tensor_tensor(
                        att_sb[:], pav[:, 0:512], invb[:, :], op=ALU.mult))
                    ctx.ck(f"b.att.{it}", "ve")
                elif eng == "gp":
                    if qc == 0:
                        ctx.wait("ve", ctx.at(f"b.rmax.{it}.63"))
                        ctx.op("gp", lambda: ctx.eng.dma_start(
                            out=c_scr, in_=cmax[:, :]), 16)
                        ctx.ck(f"b.cst.{it}", "gp")
                        ctx.wait("gp", ctx.at(f"b.cst.{it}"))
                        ctx.op("gp", lambda nr=ncrow[it % 2]: ctx.eng.dma_start(
                            out=nr[:, :], in_=c_scr.transpose([1, 0]),
                            allow_slow_non_contiguous=True), 16)
                        ctx.ck(f"b.ncr.{it}", "gp")
                    if pipe:
                        ctx.wait("ve", ctx.at(f"b.rmax.{it+1}.63"))
                        ctx.op("gp", lambda: ctx.eng.dma_start(
                            out=c_scr, in_=cmax[:, :]), 16)
                        ctx.ck(f"b.cst.{it+1}", "gp")
                        ctx.wait("gp", ctx.at(f"b.cst.{it+1}"))
                        ctx.op("gp", lambda nr=ncrow[(it + 1) % 2]:
                               ctx.eng.dma_start(
                                   out=nr[:, :], in_=c_scr.transpose([1, 0]),
                                   allow_slow_non_contiguous=True), 16)
                        ctx.ck(f"b.ncr.{it+1}", "gp")
                    ctx.wait("ve", ctx.at(f"b.inv.{it}"))
                    ctx.op("gp", lambda: ctx.eng.dma_start(
                        out=bscr[0:1, 0:512], in_=invb[0:1, :]), 16)
                    ctx.ck(f"b.invst.{it}", "gp")
                    ctx.wait("gp", ctx.at(f"b.invst.{it}"))
                    ctx.op("gp", lambda: ctx.eng.dma_start(
                        out=invb[:, :], in_=bscr[0:1, 0:512].to_broadcast([128, 512])), 16)
                    ctx.ck(f"b.invbc.{it}", "gp")

    def phase_c(eng):
        if eng == "gp":
            ctx.wait("sy", ctx.at(f"b.atst.{NH * NQC - 1}"))
            from concourse.aot_env import allow_ici_all_to_all

            def do_cc():
                with allow_ici_all_to_all():
                    return ctx.eng.collective_compute(
                        "AllToAll", ALU.bypass,
                        replica_groups=[list(range(NC))],
                        ins=[a2a_in.opt()],
                        outs=[a2a_out.opt()],
                    )
            ctx.op("cc", do_cc)
            ctx.ck("c.done", "cc")

    def phase_d(eng):
        for c in range(ND):
            b = c % 2
            if eng == "sy":
                if c == 0:
                    ctx.wait("cc", ctx.at("c.done"))
                if c >= 2:
                    ctx.wait("ve", ctx.at(f"d.al.{c-2}"))
                ctx.op("sy", lambda c=c, b=b: ctx.eng.dma_start(
                    out=astrip[b][:], in_=a2a_out[c // 2, c % 2, :, :]), 16)
                ctx.ck(f"d.ast.{c}", "sy")
            elif eng == "ac":
                ctx.wait("sy", ctx.at(f"d.ast.{c}"))
                ctx.op("ac", lambda c=c, b=b: ctx.eng.copy(atth[:, c, :], astrip[b][:]))
                ctx.ck(f"d.ah.{c}", "ac")
            elif eng == "ve":
                ctx.wait("sy", ctx.at(f"d.ast.{c}"))
                ctx.wait("ac", ctx.at(f"d.ah.{c}"))
                ctx.op("ve", lambda c=c, b=b: ctx.eng.tensor_tensor(
                    attl[:, c, :], astrip[b][:], atth[:, c, :], op=ALU.subtract))
                ctx.ck(f"d.al.{c}", "ve")
        for half in range(2):
            for c in range(ND):
                b = c % 2
                if eng == "sy":
                    if half > 0 and c < 2:
                        ctx.wait("mm", ctx.at(f"d.pj.{half-1}.{NTT-1}"))
                    if c >= 2:
                        ctx.wait("ve", ctx.at(f"d.wl.{half}.{c-2}"))
                    ctx.op("sy", lambda half=half, c=c, b=b: ctx.eng.dma_start(
                        out=pstrip[b][:],
                        in_=wpT[c * 128:(c + 1) * 128,
                                half * 1024:(half + 1) * 1024]), 16)
                    ctx.ck(f"d.wst.{half}.{c}", "sy")
                elif eng == "ac":
                    ctx.wait("sy", ctx.at(f"d.wst.{half}.{c}"))
                    if half > 0 and c == 0:
                        ctx.wait("mm", ctx.at(f"d.pj.{half-1}.{NTT-1}"))
                    ctx.op("ac", lambda half=half, c=c, b=b: ctx.eng.copy(
                        wph[:, c, :], pstrip[b][:]))
                    ctx.ck(f"d.wh.{half}.{c}", "ac")
                elif eng == "ve":
                    ctx.wait("sy", ctx.at(f"d.wst.{half}.{c}"))
                    ctx.wait("ac", ctx.at(f"d.wh.{half}.{c}"))
                    if half > 0 and c == 0:
                        ctx.wait("mm", ctx.at(f"d.pj.{half-1}.{NTT-1}"))
                    ctx.op("ve", lambda half=half, c=c, b=b: ctx.eng.tensor_tensor(
                        wpl[:, c, :], pstrip[b][:], wph[:, c, :], op=ALU.subtract))
                    ctx.ck(f"d.wl.{half}.{c}", "ve")
            for t in range(NTT):
                gt = half * NTT + t
                if eng == "mm":
                    if half == 0 and t == 0:
                        ctx.wait("ve", ctx.at(f"d.al.{ND-1}"))
                    if t == 0:
                        ctx.wait("ve", ctx.at(f"d.wl.{half}.{ND-1}"))
                    if gt >= 1:
                        ctx.wait("ac", ctx.at(f"d.pev.{(gt-1)//NTT}.{(gt-1)%NTT}"))
                    for c in range(ND):
                        a_h = atth[:, c, t * 128:(t + 1) * 128]
                        a_l = attl[:, c, t * 128:(t + 1) * 128]
                        for n in range(2):
                            w_h = wph[:, c, n * 512:(n + 1) * 512]
                            w_l = wpl[:, c, n * 512:(n + 1) * 512]
                            st = (c == 0)
                            lastmm = (c == ND - 1)
                            ctx.raw(lambda a_h=a_h, w_h=w_h, n=n, st=st: ctx.eng.matmul(
                                psP[n][:, 0:512], a_h, w_h, start=st, stop=False))
                            ctx.raw(lambda a_h=a_h, w_l=w_l, n=n: ctx.eng.matmul(
                                psP[n][:, 0:512], a_h, w_l, start=False, stop=False))
                            fn = (lambda a_l=a_l, w_h=w_h, n=n, lastmm=lastmm:
                                  ctx.eng.matmul(psP[n][:, 0:512], a_l, w_h,
                                                 start=False, stop=lastmm))
                            if lastmm:
                                ctx.op("mm", fn)
                            else:
                                ctx.raw(fn)
                    ctx.ck(f"d.pj.{half}.{t}", "mm")
                elif eng == "ac":
                    ctx.wait("mm", ctx.at(f"d.pj.{half}.{t}"))
                    if gt >= 1:
                        ctx.wait("sy", ctx.at(f"d.post.{(gt-1)//NTT}.{(gt-1)%NTT}"))
                    ctx.op("ac", lambda half=half: ctx.eng.copy(
                        po_sb[:, half * 1024:half * 1024 + 512], psP[0][:, 0:512]))
                    ctx.op("ac", lambda half=half: ctx.eng.copy(
                        po_sb[:, half * 1024 + 512:half * 1024 + 1024],
                        psP[1][:, 0:512]))
                    ctx.ck(f"d.pev.{half}.{t}", "ac")
                elif eng == "sy":
                    ctx.wait("ac", ctx.at(f"d.pev.{half}.{t}"))
                    ctx.op("sy", lambda half=half, t=t: ctx.eng.dma_start(
                        out=po_scr[t, :, half * 1024:(half + 1) * 1024],
                        in_=po_sb[:, half * 1024:(half + 1) * 1024]), 16)
                    ctx.ck(f"d.post.{half}.{t}", "sy")
        for t in range(NTT):
            if eng == "sy":
                if t == 0:
                    ctx.wait("sy", ctx.at(f"d.post.1.{NTT-1}"))
                else:
                    ctx.wait("ve", ctx.at(f"d.ln2.{t-1}"))
                ctx.op("sy", lambda t=t: ctx.eng.dma_start(
                    out=po_sb[:], in_=po_scr[t, :, :]), 16)
                ctx.op("sy", lambda t=t: ctx.eng.dma_start(
                    out=xo_sb[:], in_=x_own[t * 128:(t + 1) * 128, :]), 16)
                ctx.ck(f"d.ld.{t}", "sy")
                ctx.wait("ve", ctx.at(f"d.x2.{t}"))
                ctx.op("sy", lambda t=t: ctx.eng.dma_start(
                    out=x2_scr[t, :, :], in_=xo_sb[:]), 16)
                ctx.ck(f"d.x2st.{t}", "sy")
                ctx.wait("ac", ctx.at(f"d.l2h.{t}"))
                ctx.wait("ve", ctx.at(f"d.l2l.{t}"))
                for hl in range(2):
                    src = d_l2h if hl == 0 else d_l2l
                    dst = ln2_scr[hl, :, :, t * 128:(t + 1) * 128].transpose([1, 0, 2])
                    ctx.op("sy", lambda dst=dst, src=src: ctx.eng.dma_start(
                        out=dst, in_=src[:, :, :]), 16)
                ctx.ck(f"d.l2st.{t}", "sy")
            elif eng == "ve":
                ctx.wait("sy", ctx.at(f"d.ld.{t}"))
                ctx.op("ve", lambda: ctx.eng.reduce_sum(stats[:, 0:1], po_sb[:], axis=AX.X))
                ctx.op("ve", lambda: ctx.eng.tensor_scalar_mul(
                    stats[:, 1:2], stats[:, 0:1], 1.0 / D))
                ctx.op("ve", lambda: ctx.eng.tensor_tensor(
                    stats[:, 2:3], stats[:, 1:2], stats[:, 1:2], op=ALU.mult))
                ctx.ck(f"d.vm.{t}", "ve")
                ctx.wait("ac", ctx.at(f"d.ssq.{t}"))
                ctx.op("ve", lambda: ctx.eng.scalar_tensor_tensor(
                    stats[:, 3:4], stats[:, 4:5], 1.0 / D, stats[:, 2:3],
                    op0=ALU.mult, op1=ALU.subtract))
                ctx.ck(f"d.var.{t}", "ve")
                ctx.wait("ac", ctx.at(f"d.std.{t}"))
                ctx.op("ve", lambda: ctx.eng.reciprocal(stats[:, 5:6], stats[:, 10:11]))
                ctx.op("ve", lambda: ctx.eng.tensor_tensor(
                    stats[:, 6:7], stats[:, 10:11], stats[:, 5:6], op=ALU.mult))
                ctx.op("ve", lambda: ctx.eng.tensor_scalar(
                    stats[:, 7:8], stats[:, 6:7], -1.0, 2.0, op0=ALU.mult, op1=ALU.add))
                ctx.op("ve", lambda: ctx.eng.tensor_tensor(
                    stats[:, 8:9], stats[:, 5:6], stats[:, 7:8], op=ALU.mult))
                ctx.op("ve", lambda: ctx.eng.scalar_tensor_tensor(
                    stats[:, 9:10], stats[:, 1:2], -1.0, stats[:, 8:9],
                    op0=ALU.mult, op1=ALU.mult))
                ctx.ck(f"d.rstd.{t}", "ve")
                if t >= 1:
                    ctx.wait("mm", ctx.at(f"d.tr.{t-1}"))
                ctx.op("ve", lambda: ctx.eng.tensor_scalar(
                    t_sb[:], po_sb[:], stats[:, 8:9], stats[:, 9:10],
                    op0=ALU.mult, op1=ALU.add))
                ctx.ck(f"d.t.{t}", "ve")
                ctx.op("ve", lambda: ctx.eng.tensor_tensor(
                    xo_sb[:], xo_sb[:], t_sb[:], op=ALU.add))
                ctx.ck(f"d.x2.{t}", "ve")
                ctx.op("ve", lambda: ctx.eng.reduce_sum(stats[:, 0:1], xo_sb[:], axis=AX.X))
                ctx.op("ve", lambda: ctx.eng.tensor_scalar_mul(
                    stats[:, 1:2], stats[:, 0:1], 1.0 / D))
                ctx.op("ve", lambda: ctx.eng.tensor_tensor(
                    stats[:, 2:3], stats[:, 1:2], stats[:, 1:2], op=ALU.mult))
                ctx.ck(f"d.vm2.{t}", "ve")
                ctx.wait("ac", ctx.at(f"d.ssq2.{t}"))
                ctx.op("ve", lambda: ctx.eng.scalar_tensor_tensor(
                    stats[:, 3:4], stats[:, 4:5], 1.0 / D, stats[:, 2:3],
                    op0=ALU.mult, op1=ALU.subtract))
                ctx.ck(f"d.var2.{t}", "ve")
                ctx.wait("ac", ctx.at(f"d.std2.{t}"))
                ctx.op("ve", lambda: ctx.eng.reciprocal(stats[:, 5:6], stats[:, 10:11]))
                ctx.op("ve", lambda: ctx.eng.tensor_tensor(
                    stats[:, 6:7], stats[:, 10:11], stats[:, 5:6], op=ALU.mult))
                ctx.op("ve", lambda: ctx.eng.tensor_scalar(
                    stats[:, 7:8], stats[:, 6:7], -1.0, 2.0, op0=ALU.mult, op1=ALU.add))
                ctx.op("ve", lambda: ctx.eng.tensor_tensor(
                    stats[:, 8:9], stats[:, 5:6], stats[:, 7:8], op=ALU.mult))
                ctx.op("ve", lambda: ctx.eng.scalar_tensor_tensor(
                    stats[:, 9:10], stats[:, 1:2], -1.0, stats[:, 8:9],
                    op0=ALU.mult, op1=ALU.mult))
                ctx.ck(f"d.rstd2.{t}", "ve")
                ctx.op("ve", lambda: ctx.eng.tensor_scalar(
                    t_sb[:], xo_sb[:], stats[:, 8:9], stats[:, 9:10],
                    op0=ALU.mult, op1=ALU.add))
                ctx.ck(f"d.ln2.{t}", "ve")
                for f in range(4):
                    fid = t * 4 + f
                    pT = psT[fid % 2]
                    ctx.wait("mm", ctx.at(f"d.fill.{fid}"))
                    ctx.wait("ac", ctx.at(f"d.feh.{fid}"))
                    if t >= 1 and f == 0:
                        ctx.wait("sy", ctx.at(f"d.l2st.{t-1}"))
                    for j in range(4):
                        dch = f * 4 + j
                        fn = (lambda dch=dch, j=j, pT=pT: ctx.eng.tensor_tensor(
                            d_l2l[:, dch, :], pT[:, j * 128:(j + 1) * 128],
                            d_l2h[:, dch, :], op=ALU.subtract))
                        if j == 3:
                            ctx.op("ve", fn)
                        else:
                            ctx.raw(fn)
                    ctx.ck(f"d.fel.{fid}", "ve")
                ctx.ck(f"d.l2l.{t}", "ve")
            elif eng == "ac":
                ctx.wait("sy", ctx.at(f"d.ld.{t}"))
                ctx.op("ac", lambda: ctx.eng.activation(
                    out=junk[:], in_=po_sb[:], func=AF.Square, accum_out=stats[:, 4:5]))
                ctx.ck(f"d.ssq.{t}", "ac")
                ctx.wait("ve", ctx.at(f"d.var.{t}"))
                ctx.op("ac", lambda: ctx.eng.activation(
                    out=stats[:, 10:11], in_=stats[:, 3:4], func=AF.Sqrt, bias=stats[:, 11:12]))
                ctx.ck(f"d.std.{t}", "ac")
                ctx.wait("ve", ctx.at(f"d.x2.{t}"))
                ctx.op("ac", lambda: ctx.eng.activation(
                    out=junk[:], in_=xo_sb[:], func=AF.Square, accum_out=stats[:, 4:5]))
                ctx.ck(f"d.ssq2.{t}", "ac")
                ctx.wait("ve", ctx.at(f"d.var2.{t}"))
                ctx.op("ac", lambda: ctx.eng.activation(
                    out=stats[:, 10:11], in_=stats[:, 3:4], func=AF.Sqrt, bias=stats[:, 11:12]))
                ctx.ck(f"d.std2.{t}", "ac")
                for f in range(4):
                    fid = t * 4 + f
                    pT = psT[fid % 2]
                    ctx.wait("mm", ctx.at(f"d.fill.{fid}"))
                    if t >= 1 and f == 0:
                        ctx.wait("sy", ctx.at(f"d.l2st.{t-1}"))
                    for j in range(4):
                        dch = f * 4 + j
                        fn = (lambda dch=dch, j=j, pT=pT: ctx.eng.copy(
                            d_l2h[:, dch, :], pT[:, j * 128:(j + 1) * 128]))
                        if j == 3:
                            ctx.op("ac", fn)
                        else:
                            ctx.raw(fn)
                    ctx.ck(f"d.feh.{fid}", "ac")
                ctx.ck(f"d.l2h.{t}", "ac")
            elif eng == "mm":
                ctx.wait("ve", ctx.at(f"d.ln2.{t}"))
                for f in range(4):
                    fid = t * 4 + f
                    pT = psT[fid % 2]
                    if fid >= 2:
                        ctx.wait("ac", ctx.at(f"d.feh.{fid-2}"))
                        ctx.wait("ve", ctx.at(f"d.fel.{fid-2}"))
                    for j in range(4):
                        dch = f * 4 + j
                        fn = (lambda dch=dch, j=j, pT=pT: ctx.eng.transpose(
                            pT[:, j * 128:(j + 1) * 128],
                            t_sb[:, dch * 128:(dch + 1) * 128], ident[:]))
                        if j == 3:
                            ctx.op("mm", fn)
                        else:
                            ctx.raw(fn)
                    ctx.ck(f"d.fill.{fid}", "mm")
                ctx.ck(f"d.tr.{t}", "mm")

    def phase_ef(eng):
        for tc2 in range(2):
            if eng == "sy":
                if tc2 == 0:
                    ctx.wait("sy", ctx.at(f"d.l2st.{NTT-1}"))
                else:
                    ctx.wait("ve", ctx.at(f"f.y.{tc2-1}.3"))
                    ctx.wait("sy", ctx.at(f"f.yst.{tc2-1}.3"))
                    ctx.wait("mm", ctx.at(f"e.gu.{tc2-1}.{NI-1}"))
                for hl in range(2):
                    src = ln2_scr[hl, :, :, tc2 * 512:(tc2 + 1) * 512].transpose([1, 0, 2])
                    dst = l2h if hl == 0 else l2l
                    ctx.op("sy", lambda src=src, dst=dst: ctx.eng.dma_start(
                        out=dst[:, :, :], in_=src), 16)
                ctx.ck(f"e.l2.{tc2}", "sy")
                for ich in range(NI):
                    b = ich % 2
                    gi = tc2 * NI + ich
                    if gi >= 2:
                        ctx.wait("mm", ctx.at(f"e.gu.{(gi-2)//NI}.{(gi-2)%NI}"))
                    if gi == 0:
                        ctx.wait("sd", ctx.at(f"a.qst.{NMINI-1}"))
                    ctx.op("sy", lambda ich=ich, b=b: ctx.eng.dma_start(
                        out=wgblk[b][:, :, :], in_=wg16[ich, :, :, :]), 16)
                    ctx.op("sy", lambda ich=ich, b=b: ctx.eng.dma_start(
                        out=wublk[b][:, :, :], in_=wu16[ich, :, :, :]), 16)
                    ctx.ck(f"e.wb.{tc2}.{ich}", "sy")
                for d2 in range(4):
                    for ich in range(NI):
                        b = ich % 2
                        gdi = tc2 * 4 * NI + d2 * NI + ich
                        if gdi >= 2:
                            pd = gdi - 2
                            ptc = pd // (4 * NI)
                            pr = pd % (4 * NI)
                            ctx.wait("mm", ctx.at(f"e.dmm.{ptc}.{pr//NI}.{pr%NI}"))
                        ctx.op("sy", lambda d2=d2, ich=ich, b=b: ctx.eng.dma_start(
                            out=wdblk[b][:], in_=wd16[d2, ich, :, :]), 16)
                        ctx.ck(f"e.wd.{tc2}.{d2}.{ich}", "sy")
            elif eng == "mm":
                ctx.wait("sy", ctx.at(f"e.l2.{tc2}"))
                for ich in range(NI):
                    b = ich % 2
                    ctx.wait("sy", ctx.at(f"e.wb.{tc2}.{ich}"))
                    gi = tc2 * NI + ich
                    if gi >= 1:
                        pg = gi - 1
                        ctx.wait("ac", ctx.at(f"e.silu.{pg//NI}.{pg%NI}"))
                        ctx.wait("ve", ctx.at(f"e.ff.{pg//NI}.{pg%NI}"))
                    if tc2 == 0 and ich == 0:
                        ctx.wait("ac", ctx.at(f"d.pev.1.{NTT-1}"))
                    for dch in range(ND):
                        st = (dch == 0)
                        lastmm = (dch == ND - 1)
                        ctx.raw(lambda b=b, dch=dch, st=st: ctx.eng.matmul(
                            psG[:, 0:512], wgblk[b][:, dch, :], l2h[:, dch, :],
                            start=st, stop=False))
                        fn = (lambda b=b, dch=dch, lastmm=lastmm: ctx.eng.matmul(
                            psG[:, 0:512], wgblk[b][:, dch, :], l2l[:, dch, :],
                            start=False, stop=lastmm))
                        if lastmm:
                            ctx.op("mm", fn)
                        else:
                            ctx.raw(fn)
                    ctx.ck(f"e.gmm.{tc2}.{ich}", "mm")
                    for dch in range(ND):
                        st = (dch == 0)
                        lastmm = (dch == ND - 1)
                        ctx.raw(lambda b=b, dch=dch, st=st: ctx.eng.matmul(
                            psU[:, 0:512], wublk[b][:, dch, :], l2h[:, dch, :],
                            start=st, stop=False))
                        fn = (lambda b=b, dch=dch, lastmm=lastmm: ctx.eng.matmul(
                            psU[:, 0:512], wublk[b][:, dch, :], l2l[:, dch, :],
                            start=False, stop=lastmm))
                        if lastmm:
                            ctx.op("mm", fn)
                        else:
                            ctx.raw(fn)
                    ctx.ck(f"e.umm.{tc2}.{ich}", "mm")
                    ctx.ck(f"e.gu.{tc2}.{ich}", "mm")
                for d2 in range(4):
                    for ich in range(NI):
                        b = ich % 2
                        ctx.wait("sy", ctx.at(f"e.wd.{tc2}.{d2}.{ich}"))
                        if d2 == 0:
                            ctx.wait("ac", ctx.at(f"e.fh.{tc2}.{ich}"))
                            ctx.wait("ve", ctx.at(f"e.fl.{tc2}.{ich}"))
                        if d2 == 0 and ich == 0 and tc2 >= 1:
                            ctx.wait("ve", ctx.at(f"e.dse.{tc2-1}.3"))
                        if d2 >= 1 and ich == 0:
                            ctx.wait("ve", ctx.at(f"e.dse.{tc2}.{d2-1}"))
                        for tg in range(4):
                            st = (ich == 0)
                            lastmm = (ich == NI - 1)
                            ctx.raw(lambda b=b, ich=ich, tg=tg, st=st: ctx.eng.matmul(
                                psD[tg][:, 0:512],
                                fuh[:, ich, tg * 128:(tg + 1) * 128],
                                wdblk[b][:], start=st, stop=False))
                            fn = (lambda b=b, ich=ich, tg=tg, lastmm=lastmm:
                                  ctx.eng.matmul(
                                      psD[tg][:, 0:512],
                                      ful[:, ich, tg * 128:(tg + 1) * 128],
                                      wdblk[b][:], start=False, stop=lastmm))
                            if tg == 3:
                                ctx.op("mm", fn)
                            else:
                                ctx.raw(fn)
                        ctx.ck(f"e.dmm.{tc2}.{d2}.{ich}", "mm")
                    ctx.ck(f"e.dcol.{tc2}.{d2}", "mm")
            elif eng == "ac":
                for ich in range(NI):
                    ctx.wait("mm", ctx.at(f"e.gmm.{tc2}.{ich}"))
                    ctx.op("ac", lambda ich=ich: ctx.eng.activation(
                        out=fg_sb[:], in_=psG[:, 0:512], func=AF.Silu,
                        scale=gs_sb[:, ich:ich + 1]))
                    ctx.ck(f"e.silu.{tc2}.{ich}", "ac")
                    ctx.wait("ve", ctx.at(f"e.ff.{tc2}.{ich}"))
                    ctx.op("ac", lambda ich=ich: ctx.eng.copy(
                        fuh[:, ich, :], junk[:, 0:512]))
                    ctx.ck(f"e.fh.{tc2}.{ich}", "ac")
            elif eng == "ve":
                for ich in range(NI):
                    ctx.wait("mm", ctx.at(f"e.umm.{tc2}.{ich}"))
                    ctx.wait("ac", ctx.at(f"e.silu.{tc2}.{ich}"))
                    if tc2 >= 1 and ich == 0:
                        ctx.wait("mm", ctx.at(f"e.dcol.{tc2-1}.3"))
                        ctx.wait("ac", ctx.at(f"f.ssq.{tc2-1}.3"))
                    ctx.op("ve", lambda ich=ich: ctx.eng.scalar_tensor_tensor(
                        junk[:, 0:512], psU[:, 0:512], us_sb[:, ich:ich + 1],
                        fg_sb[:], op0=ALU.mult, op1=ALU.mult))
                    ctx.ck(f"e.ff.{tc2}.{ich}", "ve")
                    ctx.wait("ac", ctx.at(f"e.fh.{tc2}.{ich}"))
                    ctx.op("ve", lambda ich=ich: ctx.eng.tensor_tensor(
                        ful[:, ich, :], junk[:, 0:512], fuh[:, ich, :],
                        op=ALU.subtract))
                    ctx.ck(f"e.fl.{tc2}.{ich}", "ve")
                for d2 in range(4):
                    ctx.wait("mm", ctx.at(f"e.dmm.{tc2}.{d2}.{NI-1}"))
                    if tc2 >= 1 and d2 == 0:
                        ctx.wait("sy", ctx.at(f"f.yst.{tc2-1}.3"))
                    for tg in range(4):
                        ctx.op("ve", lambda d2=d2, tg=tg: ctx.eng.tensor_tensor(
                            dout[:, tg, d2 * 512:(d2 + 1) * 512], psD[tg][:, 0:512],
                            dsb[:, d2 * 512:(d2 + 1) * 512], op=ALU.mult))
                    ctx.ck(f"e.dse.{tc2}.{d2}", "ve")
            for tg in range(4):
                gt = tc2 * 4 + tg
                if eng == "sy":
                    ctx.wait("ve", ctx.at(f"e.dse.{tc2}.3"))
                    if gt >= 1:
                        ctx.wait("ve", ctx.at(f"f.y.{(gt-1)//4}.{(gt-1)%4}"))
                    ctx.op("sy", lambda gt=gt: ctx.eng.dma_start(
                        out=x2t[:], in_=x2_scr[gt, :, :]), 16)
                    ctx.ck(f"f.x2.{tc2}.{tg}", "sy")
                    ctx.wait("ve", ctx.at(f"f.y.{tc2}.{tg}"))
                    ctx.op("sy", lambda gt=gt: ctx.eng.dma_start(
                        out=y_out[gt * 128:(gt + 1) * 128, :], in_=x2t[:]), 16)
                    ctx.ck(f"f.yst.{tc2}.{tg}", "sy")
                elif eng == "ve":
                    src = dout[:, tg, :]
                    if tg == 0:
                        pass
                    ctx.op("ve", lambda src=src: ctx.eng.reduce_sum(
                        stats[:, 0:1], src, axis=AX.X))
                    ctx.op("ve", lambda: ctx.eng.tensor_scalar_mul(
                        stats[:, 1:2], stats[:, 0:1], 1.0 / D))
                    ctx.op("ve", lambda: ctx.eng.tensor_tensor(
                        stats[:, 2:3], stats[:, 1:2], stats[:, 1:2], op=ALU.mult))
                    ctx.ck(f"f.vm.{tc2}.{tg}", "ve")
                    ctx.wait("ac", ctx.at(f"f.ssq.{tc2}.{tg}"))
                    ctx.op("ve", lambda: ctx.eng.scalar_tensor_tensor(
                        stats[:, 3:4], stats[:, 4:5], 1.0 / D, stats[:, 2:3],
                        op0=ALU.mult, op1=ALU.subtract))
                    ctx.ck(f"f.var.{tc2}.{tg}", "ve")
                    ctx.wait("ac", ctx.at(f"f.std.{tc2}.{tg}"))
                    ctx.op("ve", lambda: ctx.eng.reciprocal(stats[:, 5:6], stats[:, 10:11]))
                    ctx.op("ve", lambda: ctx.eng.tensor_tensor(
                        stats[:, 6:7], stats[:, 10:11], stats[:, 5:6], op=ALU.mult))
                    ctx.op("ve", lambda: ctx.eng.tensor_scalar(
                        stats[:, 7:8], stats[:, 6:7], -1.0, 2.0,
                        op0=ALU.mult, op1=ALU.add))
                    ctx.op("ve", lambda: ctx.eng.tensor_tensor(
                        stats[:, 8:9], stats[:, 5:6], stats[:, 7:8], op=ALU.mult))
                    ctx.op("ve", lambda: ctx.eng.scalar_tensor_tensor(
                        stats[:, 9:10], stats[:, 1:2], -1.0, stats[:, 8:9],
                        op0=ALU.mult, op1=ALU.mult))
                    ctx.ck(f"f.rstd.{tc2}.{tg}", "ve")
                    ctx.op("ve", lambda src=src: ctx.eng.tensor_scalar(
                        t_f[:], src, stats[:, 8:9], stats[:, 9:10],
                        op0=ALU.mult, op1=ALU.add))
                    ctx.ck(f"f.t.{tc2}.{tg}", "ve")
                    ctx.wait("sy", ctx.at(f"f.x2.{tc2}.{tg}"))
                    ctx.op("ve", lambda: ctx.eng.tensor_tensor(
                        x2t[:], x2t[:], t_f[:], op=ALU.add))
                    ctx.ck(f"f.y.{tc2}.{tg}", "ve")
                elif eng == "ac":
                    src = dout[:, tg, :]
                    ctx.wait("ve", ctx.at(f"f.vm.{tc2}.{tg}"))
                    ctx.op("ac", lambda src=src: ctx.eng.activation(
                        out=junk[:], in_=src, func=AF.Square, accum_out=stats[:, 4:5]))
                    ctx.ck(f"f.ssq.{tc2}.{tg}", "ac")
                    ctx.wait("ve", ctx.at(f"f.var.{tc2}.{tg}"))
                    ctx.op("ac", lambda: ctx.eng.activation(
                        out=stats[:, 10:11], in_=stats[:, 3:4], func=AF.Sqrt, bias=stats[:, 11:12]))
                    ctx.ck(f"f.std.{tc2}.{tg}", "ac")
        if eng == "sy":
            ctx.wait("sy", ctx.at("f.yst.1.3"))

    def run_engine(eng):
        phase_pre(eng)
        phase_a(eng)
        phase_b(eng)
        phase_c(eng)
        phase_d(eng)
        phase_ef(eng)

    for eng in ["sy", "ve", "ac", "mm", "gp"]:
        ctx.reset("count")
        run_engine(eng)

    with nc.Block() as block:
        @block.sync
        def _(sync):
            ctx.reset("emit", sync)
            run_engine("sy")

        @block.vector
        def _(vector):
            ctx.reset("emit", vector)
            run_engine("ve")

        @block.scalar
        def _(scalar):
            ctx.reset("emit", scalar)
            run_engine("ac")

        @block.tensor
        def _(tensor):
            ctx.reset("emit", tensor)
            run_engine("mm")

        @block.gpsimd
        def _(gpsimd):
            ctx.reset("emit", gpsimd)
            run_engine("gp")

    es.close()
    return nc


_BUILT = None


def kernel(L0_attn_x, L0_attn_w_qkv, L0_attn_w_proj, L0_ffn_gate_w, L0_ffn_up_w,
           L0_ffn_down_w, L0_ffn_gate_scale, L0_ffn_up_scale, L0_ffn_down_scale):
    global _BUILT
    x = np.ascontiguousarray(np.asarray(L0_attn_x, dtype=np.float32).reshape(S, D))
    w_qkv = np.asarray(L0_attn_w_qkv, dtype=np.float32)
    wp = np.ascontiguousarray(np.asarray(L0_attn_w_proj, dtype=np.float32).T)
    gw = np.ascontiguousarray(np.asarray(L0_ffn_gate_w, dtype=np.float32))
    uw = np.ascontiguousarray(np.asarray(L0_ffn_up_w, dtype=np.float32))
    dw = np.ascontiguousarray(np.asarray(L0_ffn_down_w, dtype=np.float32))
    gs = np.ascontiguousarray(np.asarray(L0_ffn_gate_scale, dtype=np.float32))
    us = np.ascontiguousarray(np.asarray(L0_ffn_up_scale, dtype=np.float32))
    ds = np.ascontiguousarray(np.asarray(L0_ffn_down_scale, dtype=np.float32))
    ident = np.eye(128, dtype=np.float32)

    if _BUILT is None:
        _BUILT = build_program()
    nc = _BUILT

    in_maps = []
    for c in range(NC):
        qrows = w_qkv[NH * HD * c:NH * HD * (c + 1), :]
        krows = w_qkv[D + NH * HD * c:D + NH * HD * (c + 1), :]
        vrows = w_qkv[2 * D + NH * HD * c:2 * D + NH * HD * (c + 1), :]
        in_maps.append({
            "x_full": x,
            "x_own": np.ascontiguousarray(x[T * c:T * (c + 1), :]),
            "wqkT": np.ascontiguousarray(np.concatenate([qrows, krows], axis=0).T),
            "wvT": np.ascontiguousarray(vrows.T),
            "wpT": wp,
            "wg": gw, "wu": uw, "wd": dw,
            "gs": gs, "us": us, "ds": ds,
            "ident": ident,
        })

    res = run_bass_kernel_spmd(nc, in_maps, core_ids=list(range(NC)))
    out = np.concatenate([res.results[c]["y"] for c in range(NC)], axis=0)
    return out.reshape(1, S, D)


# version 14 (final)
# speedup vs baseline: 1.0123x; 1.0001x over previous
import os
import numpy as np
from contextlib import ExitStack

os.environ.setdefault("TRN_ALLOW_ICI_ALL_TO_ALL", "1")

import concourse.bass as bass
import concourse.mybir as mybir
from concourse.bass_utils import run_bass_kernel_spmd

F32 = mybir.dt.float32
F16 = mybir.dt.float16
FP8 = mybir.dt.float8e4
F32R = mybir.dt.float32r
U8 = mybir.dt.uint8
AF = mybir.ActivationFunctionType
ALU = mybir.AluOpType
AX = mybir.AxisListType

NC = 8
S = 8192
D = 2048
H = 16
HD = 128
I = 6144
NH = 2
T = S // NC
ND = D // 128
NI = I // 128
NTC = S // 512
NT128 = S // 128
NKC = S // 128
NQC = S // 512
NTT = T // 128
INV_SQRT_HD = float(1.0 / np.sqrt(np.float32(HD)))
EPS = 1e-5
KB = 1024

TAPS = []


K_DETECT = os.environ.get("KDETECT", "0") == "1"
K_SERIAL = os.environ.get("KSERIAL", "0") == "1"


class Ctx:

    def __init__(self):
        self.mode = "count"
        self.cnt = {}
        self.cks = {}
        self.eng = None
        self.sems = {}

    def reset(self, mode, eng=None):
        self.mode = mode
        self.eng = eng
        self.cnt = {k: 0 for k in self.sems}

    AUTO_SERIAL = ("ve", "ac", "mm", "gc")
    DMA_CHAINS = ("sy", "sq", "gp")

    def op(self, chain, fn, k=1):
        if chain in self.AUTO_SERIAL and self.cnt[chain] > 0 and self.mode == "emit":
            self.eng.wait_ge(self.sems[chain], self.cnt[chain])
        if self.mode == "emit":
            fn().then_inc(self.sems[chain], k)
        self.cnt[chain] += k
        if K_SERIAL and chain in self.DMA_CHAINS and self.mode == "emit":
            self.eng.wait_ge(self.sems[chain], self.cnt[chain])
        return self.cnt[chain]

    def raw(self, fn):
        if self.mode == "emit":
            fn()

    def ck(self, key, chain):
        if self.mode == "count":
            self.cks[key] = self.cnt[chain]
        else:
            assert self.cks.get(key) == self.cnt[chain], (
                key, self.cks.get(key), self.cnt[chain])
        bnd = self.DMA_CHAINS + (("sc", "sd") if K_SERIAL else ())
        if chain in bnd and self.mode == "emit" and self.cnt[chain] > 0:
            self.eng.wait_ge(self.sems[chain], self.cnt[chain])
        return self.cks[key]

    def at(self, key):
        if self.mode == "count":
            return self.cks.get(key, 0)
        return self.cks[key]

    def wait(self, chain, count):
        if count and count > 0 and self.mode == "emit":
            self.eng.wait_ge(self.sems[chain], count)


def build_program():
    nc = bass.Bass("TRN2", target_bir_lowering=False,
                   detect_race_conditions=K_DETECT)

    x_full = nc.dram_tensor("x_full", [S, D], F32, kind="ExternalInput").ap()
    x_own = nc.dram_tensor("x_own", [T, D], F32, kind="ExternalInput").ap()
    wqkT = nc.dram_tensor("wqkT", [D, 512], F32, kind="ExternalInput").ap()
    wvT = nc.dram_tensor("wvT", [D, 256], F32, kind="ExternalInput").ap()
    wpT = nc.dram_tensor("wpT", [D, D], F32, kind="ExternalInput").ap()
    wg = nc.dram_tensor("wg", [D, I], F32, kind="ExternalInput").ap()
    wu = nc.dram_tensor("wu", [D, I], F32, kind="ExternalInput").ap()
    wd = nc.dram_tensor("wd", [I, D], F32, kind="ExternalInput").ap()
    gs_in = nc.dram_tensor("gs", [I], F32, kind="ExternalInput").ap()
    us_in = nc.dram_tensor("us", [I], F32, kind="ExternalInput").ap()
    ds_in = nc.dram_tensor("ds", [D], F32, kind="ExternalInput").ap()
    ident_in = nc.dram_tensor("ident", [128, 128], F32, kind="ExternalInput").ap()
    y_out = nc.dram_tensor("y", [T, D], F32, kind="ExternalOutput").ap()

    def scratch(name, shape, dt):
        kind = "ExternalOutput" if name in TAPS else "Internal"
        return nc.dram_tensor(name, shape, dt, kind=kind).ap()

    wg16 = scratch("wg16", [NI, 128, ND, 128], F16)
    wu16 = scratch("wu16", [NI, 128, ND, 128], F16)
    wd16 = scratch("wd16", [4, NI, 128, 512], F16)
    q_scr = scratch("q_scr", [2, NH, 128, S], F16)
    k_scr = scratch("k_scr", [2, NH, 128, S], F16)
    v_scr = scratch("v_scr", [2, NH, 128, NKC, 128], F16)
    a2a_in = scratch("a2a_in", [NC, NH, 128, T], F32)
    a2a_out = scratch("a2a_out", [NC, NH, 128, T], F32)
    po_scr = scratch("po_scr", [NTT, 128, D], F32)
    x2_scr = scratch("x2_scr", [NTT, 128, D], F32)
    ln2_scr = scratch("ln2_scr", [2, ND, 128, T], F16)
    bscr = scratch("bscr", [1, 2048], F32)
    c_scr = scratch("c_scr", [128, 4], F16)

    es = ExitStack()
    sb = lambda n_, sh, dt: es.enter_context(nc.sbuf_tensor(n_, sh, dt))

    AR1 = sb("AR1", [128, 86 * KB], U8)
    AR2 = sb("AR2", [128, 96 * KB], U8)
    ident = sb("ident_sb", [128, 128], F32)
    ones32 = sb("ones32", [128, 1], F32)
    ones16 = sb("ones16", [128, 1], F16)
    junk = sb("junk", [128, D], F32)
    stats = sb("stats", [128, 16], F32)
    nstat = sb("nstat", [1, 24], F32)
    negc = sb("negc", [128, NH], F32)
    invb = sb("invb", [128, 512], F32)
    nr_t = sb("nr_t", [1, 512], F32)
    cmax = sb("cmax", [128, 4], F16)
    rm_sb = sb("rm_sb", [128, 1], F32)
    ncrow = [sb("ncrow0", [1, 512], F16), sb("ncrow1", [1, 512], F16)]
    negones = sb("negones", [1, 128], F16)
    gs_sb = sb("gs_sb", [128, NI], F32)
    us_sb = sb("us_sb", [128, NI], F32)
    dsb = sb("dsb", [128, D], F32)

    def v1(ar, off, nbytes, dt, lead=None):
        ap = ar[:, off:off + nbytes].bitcast(dt)
        if lead is not None:
            ap = ap.rearrange("p (a b) -> p a b", a=lead)
        return ap

    wqkh = v1(AR1, 0, 16 * KB, F16, ND)
    wqkl = v1(AR1, 16 * KB, 16 * KB, F16, ND)
    wvh = v1(AR1, 32 * KB, 8 * KB, F16, ND)
    wvl = v1(AR1, 40 * KB, 8 * KB, F16, ND)
    qk_ev = [[[v1(AR1, 48 * KB + (e * 2 + hl) * KB, KB, F16)
               for hl in range(2)] for e in range(4)] for p in range(1)] * 2
    v_ev = [[v1(AR1, 56 * KB + hl * 2 * KB, 2 * KB, F16, 4)
             for hl in range(2)] for p in range(1)] * 2
    wstrip = [v1(AR1, 60 * KB, 2 * KB, F32), v1(AR1, 62 * KB, 2 * KB, F32)]
    q_in = [v1(AR1, 64 * KB + 4 * i * KB, 4 * KB, F32) for i in range(4)]
    q_8 = [v1(AR1, 80 * KB, KB, FP8), v1(AR1, 81 * KB, KB, FP8)]
    q_16 = [v1(AR1, 82 * KB, 2 * KB, F16), v1(AR1, 84 * KB, 2 * KB, F16)]
    xb = [v1(AR2, 0, 8 * KB, F32), v1(AR2, 8 * KB, 8 * KB, F32)]
    lnt = [v1(AR2, 16 * KB, 8 * KB, F32), v1(AR2, 24 * KB, 8 * KB, F32)]
    lns = [[v1(AR2, 32 * KB + (p * 2 + hl) * 16 * KB, 16 * KB, F16, ND)
            for hl in range(2)] for p in range(2)]

    kh_sb = v1(AR1, 0, 16 * KB, F16)
    kl_sb = v1(AR1, 16 * KB, 16 * KB, F16)
    vh_sb = v1(AR1, 32 * KB, 16 * KB, F16, NKC)
    vl_sb = v1(AR1, 48 * KB, 16 * KB, F16, NKC)
    qb = [[v1(AR2, (p * 2 + hl) * KB, KB, F16) for hl in range(2)] for p in range(2)]
    p_sb = [v1(AR2, 4 * KB + i * 2 * KB, 2 * KB, F32) for i in range(4)]
    ph_sb = [v1(AR2, 12 * KB + i * KB, KB, F16) for i in range(4)]
    pl_sb = [v1(AR2, 16 * KB + i * KB, KB, F16) for i in range(4)]
    acc_sb = [v1(AR2, 20 * KB, 2 * KB, F32), v1(AR2, 22 * KB, 2 * KB, F32)]
    acch = v1(AR2, 24 * KB, KB, F16)
    accl = v1(AR2, 25 * KB, KB, F16)
    att_sb = v1(AR2, 26 * KB, 2 * KB, F32)

    wph = v1(AR1, 0, 32 * KB, F16, ND)
    wpl = v1(AR1, 32 * KB, 32 * KB, F16, ND)
    pstrip = [v1(AR1, 64 * KB, 4 * KB, F32), v1(AR1, 68 * KB, 4 * KB, F32)]
    d_l2h = v1(AR1, 72 * KB, 4 * KB, F16, ND)
    d_l2l = v1(AR1, 76 * KB, 4 * KB, F16, ND)
    atth = v1(AR2, 0, 32 * KB, F16, ND)
    attl = v1(AR2, 32 * KB, 32 * KB, F16, ND)
    astrip = [v1(AR2, 64 * KB, 4 * KB, F32), v1(AR2, 68 * KB, 4 * KB, F32)]
    po_sb = v1(AR2, 72 * KB, 8 * KB, F32)
    xo_sb = v1(AR2, 80 * KB, 8 * KB, F32)
    t_sb = v1(AR2, 88 * KB, 8 * KB, F32)

    fuh = v1(AR2, 0, 48 * KB, F16, NI)
    ful = v1(AR2, 48 * KB, 48 * KB, F16, NI)
    l2h = v1(AR1, 0, 16 * KB, F16, ND)
    l2l = v1(AR1, 16 * KB, 16 * KB, F16, ND)
    x2t = v1(AR1, 0, 8 * KB, F32)
    t_f = v1(AR1, 8 * KB, 8 * KB, F32)
    wgblk = [v1(AR1, 32 * KB + p * 4 * KB, 4 * KB, F16, ND) for p in range(2)]
    wublk = [v1(AR1, 40 * KB + p * 4 * KB, 4 * KB, F16, ND) for p in range(2)]
    wdblk = [v1(AR1, 48 * KB + p * 2 * KB, 2 * KB, F16, 2) for p in range(2)]
    dout = v1(AR1, 52 * KB, 32 * KB, F32, 4)
    fg_sb = v1(AR1, 84 * KB, 2 * KB, F32)

    ctx = Ctx()
    for s_ in ["sy", "sq", "sc", "sd", "gp", "gc", "mm", "ve", "ac", "cc"]:
        ctx.sems[s_] = es.enter_context(nc.semaphore(s_))

    B = [es.enter_context(nc.psum_tensor(f"PB{i}", [128, 512], F32)) for i in range(8)]
    BF16_ = mybir.dt.bfloat16
    psT = [B[0][:], B[6][:]]
    PQ = [B[1][:], B[2][:], B[3][:], B[4][:]]
    PV = [B[5][:], B[7][:]]
    PN = B[7][:]
    psS = [B[1][:], B[2][:], B[6][:]]
    psAV = [B[3][:], B[4][:]]
    psP = [B[1][:], B[2][:]]
    psM = [B[0][:], B[5][:]]
    psG = B[1][:]
    psU = B[2][:]
    psD = [B[3][:], B[4][:], B[5][:], B[7][:]]

    def phase_pre(eng):
        if eng == "sy":
            ctx.op("sy", lambda: ctx.eng.dma_start(out=ident[:], in_=ident_in), 16)
            ctx.op("sy", lambda: ctx.eng.dma_start(
                out=gs_sb[:], in_=gs_in.rearrange("(a p) -> p a", p=128),
                allow_slow_non_contiguous=True), 16)
            ctx.op("sy", lambda: ctx.eng.dma_start(
                out=us_sb[:], in_=us_in.rearrange("(a p) -> p a", p=128),
                allow_slow_non_contiguous=True), 16)
            ctx.op("sy", lambda: ctx.eng.dma_start(
                out=dsb[:], in_=ds_in[None, :].to_broadcast([128, D])), 16)
            ctx.ck("pre.sy", "sy")
        elif eng == "ve":
            ctx.op("ve", lambda: ctx.eng.memset(ones32[:], 1.0))
            ctx.op("ve", lambda: ctx.eng.memset(ones16[:], 1.0))
            ctx.op("ve", lambda: ctx.eng.memset(nstat[:, 0:4], 0.0))
            ctx.op("ve", lambda: ctx.eng.memset(stats[:, 11:12], EPS))
            ctx.op("ve", lambda: ctx.eng.memset(stats[:, 12:13], 2.0))
            ctx.op("ve", lambda: ctx.eng.memset(negones[:, :], -1.0))
            ctx.ck("pre.ve", "ve")

    NMINI = 288

    def q_mini_src_dst(i):
        if i < 96:
            r, qi = i // 6, i % 6
            src = wg[r * 128:(r + 1) * 128, qi * 1024:(qi + 1) * 1024]
            dst = wg16[qi * 8:(qi + 1) * 8, :, r, :].transpose([1, 0, 2])
            lead = 8
        elif i < 192:
            j = i - 96
            r, qi = j // 6, j % 6
            src = wu[r * 128:(r + 1) * 128, qi * 1024:(qi + 1) * 1024]
            dst = wu16[qi * 8:(qi + 1) * 8, :, r, :].transpose([1, 0, 2])
            lead = 8
        else:
            j = i - 192
            r, di = j // 2, j % 2
            src = wd[r * 128:(r + 1) * 128, di * 1024:(di + 1) * 1024]
            dst = wd16[di * 2:(di + 1) * 2, r, :, :].transpose([1, 0, 2])
            lead = 2
        return src, dst, lead

    def q_ld(i):
        src, dst, lead = q_mini_src_dst(i)
        ctx.op("sc", lambda src=src, b=i % 4: ctx.eng.dma_start(
            out=q_in[b][:], in_=src), 16)
        ctx.ck(f"a.qld.{i}", "sc")

    def emit_q_minis(eng, tc):
        if eng != "ac":
            return
        lo, hi = 18 * tc, min(NMINI, 18 * (tc + 1))
        if tc == 0:
            for i in range(3):
                q_ld(i)
        for i in range(lo, hi):
            b = i % 4
            src, dst, lead = q_mini_src_dst(i)
            if i + 3 < NMINI:
                q_ld(i + 3)
            ctx.wait("sc", ctx.at(f"a.qld.{i}"))
            ctx.op("ac", lambda b=b: ctx.eng.copy(q_8[b % 2][:], q_in[b][:]))
            ctx.ck(f"a.qc1.{i}", "ac")
            if i >= 1:
                ctx.wait("sd", ctx.at(f"a.qst.{i-1}"))
            ctx.op("ac", lambda b=b: ctx.eng.copy(q_16[b % 2][:], q_8[b % 2][:]))
            ctx.ck(f"a.qc2.{i}", "ac")
            ctx.wait("ac", ctx.at(f"a.qc2.{i}"))
            ctx.op("sd", lambda dst=dst, b=b, lead=lead: ctx.eng.dma_start(
                out=dst, in_=q_16[b % 2].rearrange("p (a c) -> p a c", a=lead)), 16)
            ctx.ck(f"a.qst.{i}", "sd")

    def phase_a(eng):
        for i in range(ND):
            b = i % 2
            if eng == "sy":
                if i >= 2:
                    ctx.wait("ve", ctx.at(f"a.wl.{i-2}"))
                ctx.op("sy", lambda i=i, b=b: ctx.eng.dma_start(
                    out=wstrip[b][:, 0:512], in_=wqkT[i * 128:(i + 1) * 128, :]), 16)
                ctx.ck(f"a.wqk.{i}", "sy")
            elif eng == "ac":
                ctx.wait("sy", ctx.at(f"a.wqk.{i}"))
                ctx.op("ac", lambda i=i, b=b: ctx.eng.copy(
                    wqkh[:, i, :], wstrip[b][:, 0:512]))
                ctx.ck(f"a.wh.{i}", "ac")
            elif eng == "ve":
                ctx.wait("sy", ctx.at(f"a.wqk.{i}"))
                ctx.wait("ac", ctx.at(f"a.wh.{i}"))
                ctx.op("ve", lambda i=i, b=b: ctx.eng.tensor_tensor(
                    wqkl[:, i, :], wstrip[b][:, 0:512], wqkh[:, i, :], op=ALU.subtract))
                ctx.ck(f"a.wl.{i}", "ve")
        for i in range(ND):
            b = i % 2
            if eng == "sy":
                ctx.wait("ve", ctx.at(f"a.wvl.{i-2}") if i >= 2
                         else ctx.at(f"a.wl.{ND-2+i}"))
                ctx.op("sy", lambda i=i, b=b: ctx.eng.dma_start(
                    out=wstrip[b][:, 0:256], in_=wvT[i * 128:(i + 1) * 128, :]), 16)
                ctx.ck(f"a.wv.{i}", "sy")
            elif eng == "ac":
                ctx.wait("sy", ctx.at(f"a.wv.{i}"))
                ctx.op("ac", lambda i=i, b=b: ctx.eng.copy(
                    wvh[:, i, :], wstrip[b][:, 0:256]))
                ctx.ck(f"a.wvh.{i}", "ac")
            elif eng == "ve":
                ctx.wait("sy", ctx.at(f"a.wv.{i}"))
                ctx.wait("ac", ctx.at(f"a.wvh.{i}"))
                ctx.op("ve", lambda i=i, b=b: ctx.eng.tensor_tensor(
                    wvl[:, i, :], wstrip[b][:, 0:256], wvh[:, i, :], op=ALU.subtract))
                ctx.ck(f"a.wvl.{i}", "ve")

        for tc in range(NTC):
            par = tc % 2
            for t in range(4):
                g = tc * 4 + t
                gb = g % 2
                if eng == "sy":
                    if g >= 2:
                        ctx.wait("ve", ctx.at(f"a.ln.{g-2}"))
                    ctx.op("sy", lambda g=g, gb=gb: ctx.eng.dma_start(
                        out=xb[gb][:], in_=x_full[g * 128:(g + 1) * 128, :]), 16)
                    ctx.ck(f"a.x.{g}", "sy")
                elif eng == "ve":
                    ctx.wait("sy", ctx.at(f"a.x.{g}"))
                    ctx.op("ve", lambda gb=gb: ctx.eng.reduce_sum(
                        stats[:, 0:1], xb[gb][:], axis=AX.X))
                    ctx.op("ve", lambda: ctx.eng.tensor_scalar_mul(
                        stats[:, 1:2], stats[:, 0:1], 1.0 / D))
                    ctx.op("ve", lambda: ctx.eng.tensor_tensor(
                        stats[:, 2:3], stats[:, 1:2], stats[:, 1:2], op=ALU.mult))
                    ctx.ck(f"a.vm.{g}", "ve")
                    ctx.wait("ac", ctx.at(f"a.ssq.{g}"))
                    ctx.op("ve", lambda: ctx.eng.scalar_tensor_tensor(
                        stats[:, 3:4], stats[:, 4:5], 1.0 / D, stats[:, 2:3],
                        op0=ALU.mult, op1=ALU.subtract))
                    ctx.ck(f"a.var.{g}", "ve")
                    ctx.wait("ac", ctx.at(f"a.std.{g}"))
                    ctx.op("ve", lambda: ctx.eng.reciprocal(stats[:, 5:6], stats[:, 10:11]))
                    ctx.op("ve", lambda: ctx.eng.tensor_tensor(
                        stats[:, 6:7], stats[:, 10:11], stats[:, 5:6], op=ALU.mult))
                    ctx.op("ve", lambda: ctx.eng.tensor_scalar(
                        stats[:, 7:8], stats[:, 6:7], -1.0, 2.0,
                        op0=ALU.mult, op1=ALU.add))
                    ctx.op("ve", lambda: ctx.eng.tensor_tensor(
                        stats[:, 8:9], stats[:, 5:6], stats[:, 7:8], op=ALU.mult))
                    ctx.op("ve", lambda: ctx.eng.scalar_tensor_tensor(
                        stats[:, 9:10], stats[:, 1:2], -1.0, stats[:, 8:9],
                        op0=ALU.mult, op1=ALU.mult))
                    ctx.ck(f"a.rstd.{g}", "ve")
                    if g >= 2:
                        ctx.wait("mm", ctx.at(f"a.tr.{g-2}"))
                    ctx.op("ve", lambda gb=gb: ctx.eng.tensor_scalar(
                        lnt[gb][:], xb[gb][:], stats[:, 8:9], stats[:, 9:10],
                        op0=ALU.mult, op1=ALU.add))
                    ctx.ck(f"a.ln.{g}", "ve")
                elif eng == "ac":
                    ctx.wait("sy", ctx.at(f"a.x.{g}"))
                    ctx.wait("ve", ctx.at(f"a.var.{g-1}") if g >= 1 else 0)
                    ctx.op("ac", lambda gb=gb: ctx.eng.activation(
                        out=junk[:], in_=xb[gb][:], func=AF.Square,
                        accum_out=stats[:, 4:5]))
                    ctx.ck(f"a.ssq.{g}", "ac")
                    ctx.wait("ve", ctx.at(f"a.var.{g}"))
                    if g == 0:
                        ctx.wait("ve", ctx.at("pre.ve"))
                    ctx.op("ac", lambda: ctx.eng.activation(
                        out=stats[:, 10:11], in_=stats[:, 3:4], func=AF.Sqrt, bias=stats[:, 11:12]))
                    ctx.ck(f"a.std.{g}", "ac")
                elif eng == "mm":
                    ctx.wait("ve", ctx.at(f"a.ln.{g}"))
                    for f in range(4):
                        fid = g * 4 + f
                        pT = psT[fid % 2]
                        if fid >= 2:
                            ctx.wait("ac", ctx.at(f"a.feh.{fid-2}"))
                            ctx.wait("ve", ctx.at(f"a.fel.{fid-2}"))
                        for j in range(4):
                            dch = f * 4 + j
                            fn = (lambda gb=gb, dch=dch, j=j, pT=pT: ctx.eng.transpose(
                                pT[:, j * 128:(j + 1) * 128],
                                lnt[gb][:, dch * 128:(dch + 1) * 128], ident[:]))
                            if j == 3:
                                ctx.op("mm", fn)
                            else:
                                ctx.raw(fn)
                        ctx.ck(f"a.fill.{fid}", "mm")
                    ctx.ck(f"a.tr.{g}", "mm")
                for f in range(4):
                    fid = g * 4 + f
                    pT = psT[fid % 2]
                    if eng == "ac":
                        ctx.wait("mm", ctx.at(f"a.fill.{fid}"))
                        if tc >= 2 and t == 0 and f == 0:
                            ctx.wait("mm", ctx.at(f"a.qkv.{tc-2}"))
                        for j in range(4):
                            dch = f * 4 + j
                            fn = (lambda par=par, dch=dch, t=t, j=j, pT=pT: ctx.eng.copy(
                                lns[par][0][:, dch, t * 128:(t + 1) * 128],
                                pT[:, j * 128:(j + 1) * 128]))
                            if j == 3:
                                ctx.op("ac", fn)
                            else:
                                ctx.raw(fn)
                        ctx.ck(f"a.feh.{fid}", "ac")
                    elif eng == "ve":
                        ctx.wait("mm", ctx.at(f"a.fill.{fid}"))
                        ctx.wait("ac", ctx.at(f"a.feh.{fid}"))
                        if tc >= 2 and t == 0 and f == 0:
                            ctx.wait("mm", ctx.at(f"a.qkv.{tc-2}"))
                        for j in range(4):
                            dch = f * 4 + j
                            fn = (lambda par=par, dch=dch, t=t, j=j, pT=pT:
                                  ctx.eng.tensor_tensor(
                                      lns[par][1][:, dch, t * 128:(t + 1) * 128],
                                      pT[:, j * 128:(j + 1) * 128],
                                      lns[par][0][:, dch, t * 128:(t + 1) * 128],
                                      op=ALU.subtract))
                            if j == 3:
                                ctx.op("ve", fn)
                            else:
                                ctx.raw(fn)
                        ctx.ck(f"a.fel.{fid}", "ve")

            if eng == "mm":
                lastfid = (tc * 4 + 3) * 4 + 3
                ctx.wait("ac", ctx.at(f"a.feh.{lastfid}"))
                ctx.wait("ve", ctx.at(f"a.fel.{lastfid}"))
                if tc == 0:
                    ctx.wait("ve", ctx.at(f"a.wl.{ND-1}"))
                    ctx.wait("ve", ctx.at(f"a.wvl.{ND-1}"))
                for e in range(4):
                    if tc >= 1:
                        ctx.wait("ac", ctx.at(f"a.evq.{tc-1}.{e}"))
                        ctx.wait("ve", ctx.at(f"a.evql.{tc-1}.{e}"))
                    for dch in range(ND):
                        w_h = wqkh[:, dch, e * 128:(e + 1) * 128]
                        w_l = wqkl[:, dch, e * 128:(e + 1) * 128]
                        r_h = lns[par][0][:, dch, :]
                        r_l = lns[par][1][:, dch, :]
                        st = (dch == 0)
                        ctx.raw(lambda e=e, w_h=w_h, r_h=r_h, st=st: ctx.eng.matmul(
                            PQ[e][:, 0:512], w_h, r_h, start=st, stop=False))
                        ctx.raw(lambda e=e, w_h=w_h, r_l=r_l: ctx.eng.matmul(
                            PQ[e][:, 0:512], w_h, r_l, start=False, stop=False))
                        lastmm = (dch == ND - 1)
                        fn = (lambda e=e, w_l=w_l, r_h=r_h, lastmm=lastmm: ctx.eng.matmul(
                            PQ[e][:, 0:512], w_l, r_h, start=False, stop=lastmm))
                        if lastmm:
                            ctx.op("mm", fn)
                        else:
                            ctx.raw(fn)
                    ctx.ck(f"a.mmq.{tc}.{e}", "mm")
                for vt in range(4):
                    gv = tc * 4 + vt
                    col = (vt % 2) * 256
                    if gv >= 2:
                        ctx.wait("ac", ctx.at(f"a.evvh.{(gv-2)//4}.{(gv-2)%4}"))
                        ctx.wait("ve", ctx.at(f"a.evvl.{(gv-2)//4}.{(gv-2)%4}"))
                    pv = PV[vt % 2]
                    for dch in range(ND):
                        a_h = lns[par][0][:, dch, vt * 128:(vt + 1) * 128]
                        a_l = lns[par][1][:, dch, vt * 128:(vt + 1) * 128]
                        st = (dch == 0)
                        ctx.raw(lambda pv=pv, a_h=a_h, dch=dch, st=st: ctx.eng.matmul(
                            pv[:, 0:256], a_h, wvh[:, dch, :],
                            start=st, stop=False))
                        ctx.raw(lambda pv=pv, a_h=a_h, dch=dch: ctx.eng.matmul(
                            pv[:, 0:256], a_h, wvl[:, dch, :],
                            start=False, stop=False))
                        lastmm = (dch == ND - 1)
                        fn = (lambda pv=pv, a_l=a_l, dch=dch, lastmm=lastmm:
                              ctx.eng.matmul(pv[:, 0:256], a_l, wvh[:, dch, :],
                                             start=False, stop=lastmm))
                        if lastmm:
                            ctx.op("mm", fn)
                        else:
                            ctx.raw(fn)
                    ctx.ck(f"a.mmv.{tc}.{vt}", "mm")
                ctx.ck(f"a.qkv.{tc}", "mm")
            elif eng == "ac":
                for e in range(4):
                    ctx.wait("mm", ctx.at(f"a.mmq.{tc}.{e}"))
                    if tc >= 1:
                        ctx.wait("sq", ctx.at(f"a.stq.{tc-1}.{e}"))
                    scale = INV_SQRT_HD if e < 2 else 1.0
                    ctx.op("ac", lambda e=e, par=par, scale=scale: ctx.eng.activation(
                        out=qk_ev[par][e][0][:], in_=PQ[e][:, 0:512],
                        func=AF.Copy, scale=scale))
                    ctx.ck(f"a.evq.{tc}.{e}", "ac")
                for vt in range(4):
                    ctx.wait("mm", ctx.at(f"a.mmv.{tc}.{vt}"))
                    if tc >= 1 and vt == 0:
                        ctx.wait("sq", ctx.at(f"a.stv.{tc-1}"))
                    ctx.op("ac", lambda vt=vt, par=par: ctx.eng.activation(
                        out=v_ev[par][0][:, vt, :], in_=PV[vt % 2][:, 0:256],
                        func=AF.Copy))
                    ctx.ck(f"a.evvh.{tc}.{vt}", "ac")
                    if vt == 3:
                        emit_q_minis("ac", tc)
            elif eng == "ve":
                for e in range(4):
                    ctx.wait("mm", ctx.at(f"a.mmq.{tc}.{e}"))
                    ctx.wait("ac", ctx.at(f"a.evq.{tc}.{e}"))
                    if tc >= 1:
                        ctx.wait("sq", ctx.at(f"a.stq.{tc-1}.{e}"))
                    scale = INV_SQRT_HD if e < 2 else 1.0
                    ctx.op("ve", lambda e=e, par=par, scale=scale:
                           ctx.eng.scalar_tensor_tensor(
                               qk_ev[par][e][1][:], PQ[e][:, 0:512], scale,
                               qk_ev[par][e][0][:], op0=ALU.mult, op1=ALU.subtract))
                    ctx.ck(f"a.evql.{tc}.{e}", "ve")
                for vt in range(4):
                    ctx.wait("mm", ctx.at(f"a.mmv.{tc}.{vt}"))
                    ctx.wait("ac", ctx.at(f"a.evvh.{tc}.{vt}"))
                    if tc >= 1 and vt == 0:
                        ctx.wait("sq", ctx.at(f"a.stv.{tc-1}"))
                    ctx.op("ve", lambda vt=vt, par=par: ctx.eng.tensor_tensor(
                        v_ev[par][1][:, vt, :], PV[vt % 2][:, 0:256],
                        v_ev[par][0][:, vt, :], op=ALU.subtract))
                    ctx.ck(f"a.evvl.{tc}.{vt}", "ve")
            elif eng == "sq":
                pass
            if eng == "sy":
                for e in range(4):
                    ctx.wait("ac", ctx.at(f"a.evq.{tc}.{e}"))
                    ctx.wait("ve", ctx.at(f"a.evql.{tc}.{e}"))
                    hd = e % 2
                    scr = q_scr if e < 2 else k_scr
                    for hl in range(2):
                        ctx.op("sq", lambda scr=scr, hl=hl, hd=hd, tc=tc, e=e, par=par:
                               ctx.eng.dma_start(
                                   out=scr[hl, hd, :, tc * 512:(tc + 1) * 512],
                                   in_=qk_ev[par][e][hl][:]), 16)
                    ctx.ck(f"a.stq.{tc}.{e}", "sq")
                ctx.wait("ac", ctx.at(f"a.evvh.{tc}.3"))
                ctx.wait("ve", ctx.at(f"a.evvl.{tc}.3"))
                for hl in range(2):
                    for hd in range(NH):
                        dst = v_scr[hl, hd, :, tc * 4:(tc + 1) * 4, :]
                        src = v_ev[par][hl][:, :, hd * 128:(hd + 1) * 128]
                        ctx.op("sq", lambda dst=dst, src=src: ctx.eng.dma_start(
                            out=dst, in_=src), 16)
                ctx.ck(f"a.stv.{tc}", "sq")
                emit_q_minis("sy", tc)

    def phase_b(eng):
        def mm_maxstep(it, par, j):
            sub, km = j // 16, j % 16
            bank = psM[j % 2]
            if j >= 2:
                ctx.wait("ve", ctx.at(f"b.rmax.{it}.{j-2}"))
            elif it >= 1:
                ctx.wait("ve", ctx.at(f"b.rmax.{it-1}.{62+j}"))
            ctx.op("mm", lambda bank=bank, sub=sub, km=km, par=par:
                   ctx.eng.matmul(
                       bank[:, 0:512],
                       qb[par][0][:, sub * 128:(sub + 1) * 128],
                       kh_sb[:, km * 512:(km + 1) * 512],
                       start=True, stop=True))
            ctx.ck(f"b.mx.{it}.{j}", "mm")

        def mm_maxpass(it, par):
            for j in range(64):
                mm_maxstep(it, par, j)

        def ve_maxinit(it):
            if it >= 1:
                ctx.wait("gp", ctx.at(f"b.cst.{it-1}"))
            ctx.op("ve", lambda: ctx.eng.memset(cmax[:, :], -60000.0))

        def ve_maxstep(it, j):
            sub = j // 16
            ctx.wait("mm", ctx.at(f"b.mx.{it}.{j}"))
            ctx.op("ve", lambda j=j: ctx.eng.reduce_max(
                rm_sb[:, :], psM[j % 2][:, 0:512], axis=AX.X))
            ctx.op("ve", lambda sub=sub: ctx.eng.tensor_tensor(
                cmax[:, sub:sub + 1], cmax[:, sub:sub + 1],
                rm_sb[:, :], op=ALU.max))
            ctx.ck(f"b.rmax.{it}.{j}", "ve")

        def ve_maxred(it):
            ve_maxinit(it)
            for j in range(64):
                ve_maxstep(it, j)

        MID = 16
        for h in range(NH):
            for qc in range(NQC):
                it = h * NQC + qc
                par = qc % 2
                pipe = qc < NQC - 1
                if eng == "sy":
                    if qc == 0:
                        if h == 0:
                            ctx.wait("sq", ctx.at(f"a.stq.{NTC-1}.3"))
                            ctx.wait("sq", ctx.at(f"a.stv.{NTC-1}"))
                        else:
                            ctx.wait("mm", ctx.at(f"b.avmm.{h-1}.{NQC-1}"))
                        for hl in range(2):
                            ctx.op("sy", lambda hl=hl, h=h: ctx.eng.dma_start(
                                out=(kh_sb if hl == 0 else kl_sb)[:],
                                in_=k_scr[hl, h, :, :]), 16)
                            ctx.op("sy", lambda hl=hl, h=h: ctx.eng.dma_start(
                                out=(vh_sb if hl == 0 else vl_sb)[:, :, :],
                                in_=v_scr[hl, h, :, :, :]), 16)
                        ctx.ck(f"b.kv.{h}", "sy")
                        ctx.op("sy", lambda h=h, qc=qc, par=par: ctx.eng.dma_start(
                            out=qb[par][0][:],
                            in_=q_scr[0, h, :, qc * 512:(qc + 1) * 512]), 16)
                        ctx.op("sy", lambda h=h, qc=qc, par=par: ctx.eng.dma_start(
                            out=qb[par][1][:],
                            in_=q_scr[1, h, :, qc * 512:(qc + 1) * 512]), 16)
                        ctx.ck(f"b.q.{it}", "sy")
                    if pipe:
                        if it >= 1:
                            ctx.wait("mm", ctx.at(f"b.avmm.{(it-1)//NQC}.{(it-1)%NQC}"))
                        np_ = (qc + 1) % 2
                        ctx.op("sy", lambda h=h, qc=qc, np_=np_: ctx.eng.dma_start(
                            out=qb[np_][0][:],
                            in_=q_scr[0, h, :, (qc + 1) * 512:(qc + 2) * 512]), 16)
                        ctx.op("sy", lambda h=h, qc=qc, np_=np_: ctx.eng.dma_start(
                            out=qb[np_][1][:],
                            in_=q_scr[1, h, :, (qc + 1) * 512:(qc + 2) * 512]), 16)
                        ctx.ck(f"b.q.{it+1}", "sy")
                    if it >= 1:
                        ctx.wait("ve", ctx.at(f"b.att.{it-1}"))
                        pq = (it - 1) % NQC
                        ph_ = (it - 1) // NQC
                        d = pq // 2
                        off = (pq % 2) * 512
                        ctx.op("sy", lambda d=d, off=off, ph_=ph_: ctx.eng.dma_start(
                            out=a2a_in[d, ph_, :, off:off + 512], in_=att_sb[:]), 16)
                        ctx.ck(f"b.atst.{it-1}", "sy")
                    if it == NH * NQC - 1:
                        ctx.wait("ve", ctx.at(f"b.att.{it}"))
                        d = qc // 2
                        off = (qc % 2) * 512
                        ctx.op("sy", lambda d=d, off=off, h=h: ctx.eng.dma_start(
                            out=a2a_in[d, h, :, off:off + 512], in_=att_sb[:]), 16)
                        ctx.ck(f"b.atst.{it}", "sy")
                elif eng == "mm":
                    if qc == 0:
                        ctx.wait("sy", ctx.at(f"b.kv.{h}"))
                        ctx.wait("sy", ctx.at(f"b.q.{it}"))
                        mm_maxpass(it, par)
                    for kc in range(NKC + 1):
                        if pipe and kc < NKC:
                            if kc == 0:
                                ctx.wait("sy", ctx.at(f"b.q.{it+1}"))
                            mm_maxstep(it + 1, (qc + 1) % 2, kc)
                        if kc < NKC:
                            kb = kc % 3
                            if kc >= 3:
                                ctx.wait("ac", ctx.at(f"b.exp.{it}.{kc-3}"))
                            elif it >= 1:
                                ctx.wait("ac", ctx.at(f"b.exp.{it-1}.{NKC-3+kc}"))
                            if kc == 0:
                                ctx.wait("gp", ctx.at(f"b.ncr.{it}"))
                            k_h = kh_sb[:, kc * 128:(kc + 1) * 128]
                            k_l = kl_sb[:, kc * 128:(kc + 1) * 128]
                            ctx.raw(lambda k_h=k_h, kb=kb, par=par: ctx.eng.matmul(
                                psS[kb][:, 0:512], k_h, qb[par][0][:],
                                start=True, stop=False))
                            ctx.raw(lambda k_h=k_h, kb=kb, par=par: ctx.eng.matmul(
                                psS[kb][:, 0:512], k_h, qb[par][1][:],
                                start=False, stop=False))
                            ctx.raw(lambda k_l=k_l, kb=kb, par=par: ctx.eng.matmul(
                                psS[kb][:, 0:512], k_l, qb[par][0][:],
                                start=False, stop=False))
                            ctx.op("mm", lambda kb=kb, nr=ncrow[it % 2]:
                                   ctx.eng.matmul(
                                       psS[kb][:, 0:512], negones[:], nr[:],
                                       start=False, stop=True))
                            ctx.ck(f"b.smm.{it}.{kc}", "mm")
                        if kc >= 1:
                            ka = kc - 1
                            kb = ka % 4
                            ctx.wait("ac", ctx.at(f"b.ph.{it}.{ka}"))
                            ctx.wait("ve", ctx.at(f"b.pl.{it}.{ka}"))
                            if ka == 0 and it >= 2:
                                ctx.wait("ve", ctx.at(f"b.att.{it-2}"))
                            v_h = vh_sb[:, ka, :]
                            v_l = vl_sb[:, ka, :]
                            st = (ka == 0)
                            lastmm = (ka == NKC - 1)
                            pav = psAV[it % 2]
                            ctx.raw(lambda v_h=v_h, kb=kb, st=st, pav=pav:
                                    ctx.eng.matmul(pav[:, 0:512], v_h, ph_sb[kb][:],
                                                   start=st, stop=False))
                            ctx.raw(lambda v_h=v_h, kb=kb, pav=pav: ctx.eng.matmul(
                                pav[:, 0:512], v_h, pl_sb[kb][:],
                                start=False, stop=False))
                            fn = (lambda v_l=v_l, kb=kb, lastmm=lastmm, pav=pav:
                                  ctx.eng.matmul(pav[:, 0:512], v_l, ph_sb[kb][:],
                                                 start=False, stop=lastmm))
                            if lastmm:
                                ctx.op("mm", fn)
                            else:
                                ctx.raw(fn)
                            ctx.ck(f"b.avp.{it}.{ka}", "mm")
                    ctx.ck(f"b.avmm.{h}.{qc}", "mm")
                    ctx.wait("ac", ctx.at(f"b.acch.{it}"))
                    ctx.wait("ve", ctx.at(f"b.accl.{it}"))
                    if it >= 1:
                        ctx.wait("ve", ctx.at(f"b.inv.{it-1}"))
                    ctx.raw(lambda: ctx.eng.matmul(
                        PN[0:1, 0:512], ones16[:], acch[:], start=True, stop=False))
                    ctx.op("mm", lambda: ctx.eng.matmul(
                        PN[0:1, 0:512], ones16[:], accl[:], start=False, stop=True))
                    ctx.ck(f"b.summ.{it}", "mm")
                elif eng == "ac":
                    for kc in range(NKC):
                        kb = kc % 3
                        pb = kc % 4
                        ctx.wait("mm", ctx.at(f"b.smm.{it}.{kc}"))
                        if kc >= 4:
                            ctx.wait("mm", ctx.at(f"b.avp.{it}.{kc-4}"))
                            ctx.wait("ve", ctx.at(f"b.acc.{it}.{kc-4}"))
                        ctx.op("ac", lambda kb=kb, pb=pb: ctx.eng.activation(
                            out=p_sb[pb][:], in_=psS[kb][:, 0:512], func=AF.Exp,
                            bias=stats[:, 12:13]))
                        ctx.ck(f"b.exp.{it}.{kc}", "ac")
                        ctx.op("ac", lambda pb=pb: ctx.eng.copy(
                            ph_sb[pb][:], p_sb[pb][:]))
                        ctx.ck(f"b.ph.{it}.{kc}", "ac")
                    ctx.wait("ve", ctx.at(f"b.acc.{it}.{NKC-1}"))
                    ctx.op("ac", lambda: ctx.eng.copy(acch[:], acc_sb[1][:]))
                    ctx.ck(f"b.acch.{it}", "ac")
                elif eng == "ve":
                    if qc == 0:
                        ve_maxred(it)
                    if it >= 1:
                        ctx.wait("mm", ctx.at(f"b.summ.{it-1}"))
                        ctx.wait("ac", ctx.at(f"b.acch.{it-1}"))
                    ctx.op("ve", lambda: ctx.eng.memset(acc_sb[1][:], 0.0))
                    ctx.ck(f"b.acc0.{it}", "ve")
                    for kc in range(NKC):
                        if pipe:
                            if kc == 0:
                                ve_maxinit(it + 1)
                            ve_maxstep(it + 1, kc)
                        pb = kc % 4
                        ab = kc % 2
                        ctx.wait("ac", ctx.at(f"b.ph.{it}.{kc}"))
                        if kc >= 4:
                            ctx.wait("mm", ctx.at(f"b.avp.{it}.{kc-4}"))
                        ctx.op("ve", lambda pb=pb: ctx.eng.tensor_tensor(
                            pl_sb[pb][:], p_sb[pb][:], ph_sb[pb][:], op=ALU.subtract))
                        ctx.ck(f"b.pl.{it}.{kc}", "ve")
                        ctx.op("ve", lambda pb=pb, ab=ab: ctx.eng.tensor_tensor(
                            acc_sb[ab][:], acc_sb[1 - ab][:], p_sb[pb][:], op=ALU.add))
                        ctx.ck(f"b.acc.{it}.{kc}", "ve")
                    ctx.wait("ac", ctx.at(f"b.acch.{it}"))
                    ctx.op("ve", lambda: ctx.eng.tensor_tensor(
                        accl[:], acc_sb[1][:], acch[:], op=ALU.subtract))
                    ctx.ck(f"b.accl.{it}", "ve")
                    ctx.wait("mm", ctx.at(f"b.summ.{it}"))
                    ctx.op("ve", lambda: ctx.eng.reciprocal(invb[0:1, :], PN[0:1, 0:512]))
                    ctx.op("ve", lambda: ctx.eng.tensor_tensor(
                        nr_t[:, :], PN[0:1, 0:512], invb[0:1, :], op=ALU.mult))
                    ctx.op("ve", lambda: ctx.eng.tensor_scalar(
                        nr_t[:, :], nr_t[:, :], -1.0, 2.0,
                        op0=ALU.mult, op1=ALU.add))
                    ctx.op("ve", lambda: ctx.eng.tensor_tensor(
                        invb[0:1, :], invb[0:1, :], nr_t[:, :], op=ALU.mult))
                    ctx.ck(f"b.inv.{it}", "ve")
                    ctx.wait("gp", ctx.at(f"b.invbc.{it}"))
                    ctx.wait("mm", ctx.at(f"b.avmm.{h}.{qc}"))
                    if it >= 1:
                        ctx.wait("sy", ctx.at(f"b.atst.{it-1}"))
                    ctx.op("ve", lambda pav=psAV[it % 2]: ctx.eng.tensor_tensor(
                        att_sb[:], pav[:, 0:512], invb[:, :], op=ALU.mult))
                    ctx.ck(f"b.att.{it}", "ve")
                elif eng == "gp":
                    if qc == 0:
                        ctx.wait("ve", ctx.at(f"b.rmax.{it}.63"))
                        ctx.op("gp", lambda: ctx.eng.dma_start(
                            out=c_scr, in_=cmax[:, :]), 16)
                        ctx.ck(f"b.cst.{it}", "gp")
                        ctx.wait("gp", ctx.at(f"b.cst.{it}"))
                        ctx.op("gp", lambda nr=ncrow[it % 2]: ctx.eng.dma_start(
                            out=nr[:, :], in_=c_scr.transpose([1, 0]),
                            allow_slow_non_contiguous=True), 16)
                        ctx.ck(f"b.ncr.{it}", "gp")
                    if pipe:
                        ctx.wait("ve", ctx.at(f"b.rmax.{it+1}.63"))
                        ctx.op("gp", lambda: ctx.eng.dma_start(
                            out=c_scr, in_=cmax[:, :]), 16)
                        ctx.ck(f"b.cst.{it+1}", "gp")
                        ctx.wait("gp", ctx.at(f"b.cst.{it+1}"))
                        ctx.op("gp", lambda nr=ncrow[(it + 1) % 2]:
                               ctx.eng.dma_start(
                                   out=nr[:, :], in_=c_scr.transpose([1, 0]),
                                   allow_slow_non_contiguous=True), 16)
                        ctx.ck(f"b.ncr.{it+1}", "gp")
                    ctx.wait("ve", ctx.at(f"b.inv.{it}"))
                    ctx.op("gp", lambda: ctx.eng.dma_start(
                        out=bscr[0:1, 0:512], in_=invb[0:1, :]), 16)
                    ctx.ck(f"b.invst.{it}", "gp")
                    ctx.wait("gp", ctx.at(f"b.invst.{it}"))
                    ctx.op("gp", lambda: ctx.eng.dma_start(
                        out=invb[:, :], in_=bscr[0:1, 0:512].to_broadcast([128, 512])), 16)
                    ctx.ck(f"b.invbc.{it}", "gp")

    def phase_c(eng):
        if eng == "gp":
            ctx.wait("sy", ctx.at(f"b.atst.{NH * NQC - 1}"))
            from concourse.aot_env import allow_ici_all_to_all

            def do_cc():
                with allow_ici_all_to_all():
                    return ctx.eng.collective_compute(
                        "AllToAll", ALU.bypass,
                        replica_groups=[list(range(NC))],
                        ins=[a2a_in.opt()],
                        outs=[a2a_out.opt()],
                    )
            ctx.op("cc", do_cc)
            ctx.ck("c.done", "cc")

    def phase_d(eng):
        for c in range(ND):
            b = c % 2
            if eng == "sy":
                if c == 0:
                    ctx.wait("cc", ctx.at("c.done"))
                if c >= 2:
                    ctx.wait("ve", ctx.at(f"d.al.{c-2}"))
                ctx.op("sy", lambda c=c, b=b: ctx.eng.dma_start(
                    out=astrip[b][:], in_=a2a_out[c // 2, c % 2, :, :]), 16)
                ctx.ck(f"d.ast.{c}", "sy")
            elif eng == "ac":
                ctx.wait("sy", ctx.at(f"d.ast.{c}"))
                ctx.op("ac", lambda c=c, b=b: ctx.eng.copy(atth[:, c, :], astrip[b][:]))
                ctx.ck(f"d.ah.{c}", "ac")
            elif eng == "ve":
                ctx.wait("sy", ctx.at(f"d.ast.{c}"))
                ctx.wait("ac", ctx.at(f"d.ah.{c}"))
                ctx.op("ve", lambda c=c, b=b: ctx.eng.tensor_tensor(
                    attl[:, c, :], astrip[b][:], atth[:, c, :], op=ALU.subtract))
                ctx.ck(f"d.al.{c}", "ve")
        for half in range(2):
            for c in range(ND):
                b = c % 2
                if eng == "sy":
                    if half > 0 and c < 2:
                        ctx.wait("mm", ctx.at(f"d.pj.{half-1}.{NTT-1}"))
                    if c >= 2:
                        ctx.wait("ve", ctx.at(f"d.wl.{half}.{c-2}"))
                    ctx.op("sy", lambda half=half, c=c, b=b: ctx.eng.dma_start(
                        out=pstrip[b][:],
                        in_=wpT[c * 128:(c + 1) * 128,
                                half * 1024:(half + 1) * 1024]), 16)
                    ctx.ck(f"d.wst.{half}.{c}", "sy")
                elif eng == "ac":
                    ctx.wait("sy", ctx.at(f"d.wst.{half}.{c}"))
                    if half > 0 and c == 0:
                        ctx.wait("mm", ctx.at(f"d.pj.{half-1}.{NTT-1}"))
                    ctx.op("ac", lambda half=half, c=c, b=b: ctx.eng.copy(
                        wph[:, c, :], pstrip[b][:]))
                    ctx.ck(f"d.wh.{half}.{c}", "ac")
                elif eng == "ve":
                    ctx.wait("sy", ctx.at(f"d.wst.{half}.{c}"))
                    ctx.wait("ac", ctx.at(f"d.wh.{half}.{c}"))
                    if half > 0 and c == 0:
                        ctx.wait("mm", ctx.at(f"d.pj.{half-1}.{NTT-1}"))
                    ctx.op("ve", lambda half=half, c=c, b=b: ctx.eng.tensor_tensor(
                        wpl[:, c, :], pstrip[b][:], wph[:, c, :], op=ALU.subtract))
                    ctx.ck(f"d.wl.{half}.{c}", "ve")
            for t in range(NTT):
                gt = half * NTT + t
                if eng == "mm":
                    if half == 0 and t == 0:
                        ctx.wait("ve", ctx.at(f"d.al.{ND-1}"))
                    if t == 0:
                        ctx.wait("ve", ctx.at(f"d.wl.{half}.{ND-1}"))
                    if gt >= 1:
                        ctx.wait("ac", ctx.at(f"d.pev.{(gt-1)//NTT}.{(gt-1)%NTT}"))
                    for c in range(ND):
                        a_h = atth[:, c, t * 128:(t + 1) * 128]
                        a_l = attl[:, c, t * 128:(t + 1) * 128]
                        for n in range(2):
                            w_h = wph[:, c, n * 512:(n + 1) * 512]
                            w_l = wpl[:, c, n * 512:(n + 1) * 512]
                            st = (c == 0)
                            lastmm = (c == ND - 1)
                            ctx.raw(lambda a_h=a_h, w_h=w_h, n=n, st=st: ctx.eng.matmul(
                                psP[n][:, 0:512], a_h, w_h, start=st, stop=False))
                            ctx.raw(lambda a_h=a_h, w_l=w_l, n=n: ctx.eng.matmul(
                                psP[n][:, 0:512], a_h, w_l, start=False, stop=False))
                            fn = (lambda a_l=a_l, w_h=w_h, n=n, lastmm=lastmm:
                                  ctx.eng.matmul(psP[n][:, 0:512], a_l, w_h,
                                                 start=False, stop=lastmm))
                            if lastmm:
                                ctx.op("mm", fn)
                            else:
                                ctx.raw(fn)
                    ctx.ck(f"d.pj.{half}.{t}", "mm")
                elif eng == "ac":
                    ctx.wait("mm", ctx.at(f"d.pj.{half}.{t}"))
                    if gt >= 1:
                        ctx.wait("sy", ctx.at(f"d.post.{(gt-1)//NTT}.{(gt-1)%NTT}"))
                    ctx.op("ac", lambda half=half: ctx.eng.copy(
                        po_sb[:, half * 1024:half * 1024 + 512], psP[0][:, 0:512]))
                    ctx.op("ac", lambda half=half: ctx.eng.copy(
                        po_sb[:, half * 1024 + 512:half * 1024 + 1024],
                        psP[1][:, 0:512]))
                    ctx.ck(f"d.pev.{half}.{t}", "ac")
                elif eng == "sy":
                    ctx.wait("ac", ctx.at(f"d.pev.{half}.{t}"))
                    ctx.op("sy", lambda half=half, t=t: ctx.eng.dma_start(
                        out=po_scr[t, :, half * 1024:(half + 1) * 1024],
                        in_=po_sb[:, half * 1024:(half + 1) * 1024]), 16)
                    ctx.ck(f"d.post.{half}.{t}", "sy")
        for t in range(NTT):
            if eng == "sy":
                if t == 0:
                    ctx.wait("sy", ctx.at(f"d.post.1.{NTT-1}"))
                else:
                    ctx.wait("ve", ctx.at(f"d.ln2.{t-1}"))
                ctx.op("sy", lambda t=t: ctx.eng.dma_start(
                    out=po_sb[:], in_=po_scr[t, :, :]), 16)
                ctx.op("sy", lambda t=t: ctx.eng.dma_start(
                    out=xo_sb[:], in_=x_own[t * 128:(t + 1) * 128, :]), 16)
                ctx.ck(f"d.ld.{t}", "sy")
                ctx.wait("ve", ctx.at(f"d.x2.{t}"))
                ctx.op("sy", lambda t=t: ctx.eng.dma_start(
                    out=x2_scr[t, :, :], in_=xo_sb[:]), 16)
                ctx.ck(f"d.x2st.{t}", "sy")
                ctx.wait("ac", ctx.at(f"d.l2h.{t}"))
                ctx.wait("ve", ctx.at(f"d.l2l.{t}"))
                for hl in range(2):
                    src = d_l2h if hl == 0 else d_l2l
                    dst = ln2_scr[hl, :, :, t * 128:(t + 1) * 128].transpose([1, 0, 2])
                    ctx.op("sy", lambda dst=dst, src=src: ctx.eng.dma_start(
                        out=dst, in_=src[:, :, :]), 16)
                ctx.ck(f"d.l2st.{t}", "sy")
            elif eng == "ve":
                ctx.wait("sy", ctx.at(f"d.ld.{t}"))
                ctx.op("ve", lambda: ctx.eng.reduce_sum(stats[:, 0:1], po_sb[:], axis=AX.X))
                ctx.op("ve", lambda: ctx.eng.tensor_scalar_mul(
                    stats[:, 1:2], stats[:, 0:1], 1.0 / D))
                ctx.op("ve", lambda: ctx.eng.tensor_tensor(
                    stats[:, 2:3], stats[:, 1:2], stats[:, 1:2], op=ALU.mult))
                ctx.ck(f"d.vm.{t}", "ve")
                ctx.wait("ac", ctx.at(f"d.ssq.{t}"))
                ctx.op("ve", lambda: ctx.eng.scalar_tensor_tensor(
                    stats[:, 3:4], stats[:, 4:5], 1.0 / D, stats[:, 2:3],
                    op0=ALU.mult, op1=ALU.subtract))
                ctx.ck(f"d.var.{t}", "ve")
                ctx.wait("ac", ctx.at(f"d.std.{t}"))
                ctx.op("ve", lambda: ctx.eng.reciprocal(stats[:, 5:6], stats[:, 10:11]))
                ctx.op("ve", lambda: ctx.eng.tensor_tensor(
                    stats[:, 6:7], stats[:, 10:11], stats[:, 5:6], op=ALU.mult))
                ctx.op("ve", lambda: ctx.eng.tensor_scalar(
                    stats[:, 7:8], stats[:, 6:7], -1.0, 2.0, op0=ALU.mult, op1=ALU.add))
                ctx.op("ve", lambda: ctx.eng.tensor_tensor(
                    stats[:, 8:9], stats[:, 5:6], stats[:, 7:8], op=ALU.mult))
                ctx.op("ve", lambda: ctx.eng.scalar_tensor_tensor(
                    stats[:, 9:10], stats[:, 1:2], -1.0, stats[:, 8:9],
                    op0=ALU.mult, op1=ALU.mult))
                ctx.ck(f"d.rstd.{t}", "ve")
                if t >= 1:
                    ctx.wait("mm", ctx.at(f"d.tr.{t-1}"))
                ctx.op("ve", lambda: ctx.eng.tensor_scalar(
                    t_sb[:], po_sb[:], stats[:, 8:9], stats[:, 9:10],
                    op0=ALU.mult, op1=ALU.add))
                ctx.ck(f"d.t.{t}", "ve")
                ctx.op("ve", lambda: ctx.eng.tensor_tensor(
                    xo_sb[:], xo_sb[:], t_sb[:], op=ALU.add))
                ctx.ck(f"d.x2.{t}", "ve")
                ctx.op("ve", lambda: ctx.eng.reduce_sum(stats[:, 0:1], xo_sb[:], axis=AX.X))
                ctx.op("ve", lambda: ctx.eng.tensor_scalar_mul(
                    stats[:, 1:2], stats[:, 0:1], 1.0 / D))
                ctx.op("ve", lambda: ctx.eng.tensor_tensor(
                    stats[:, 2:3], stats[:, 1:2], stats[:, 1:2], op=ALU.mult))
                ctx.ck(f"d.vm2.{t}", "ve")
                ctx.wait("ac", ctx.at(f"d.ssq2.{t}"))
                ctx.op("ve", lambda: ctx.eng.scalar_tensor_tensor(
                    stats[:, 3:4], stats[:, 4:5], 1.0 / D, stats[:, 2:3],
                    op0=ALU.mult, op1=ALU.subtract))
                ctx.ck(f"d.var2.{t}", "ve")
                ctx.wait("ac", ctx.at(f"d.std2.{t}"))
                ctx.op("ve", lambda: ctx.eng.reciprocal(stats[:, 5:6], stats[:, 10:11]))
                ctx.op("ve", lambda: ctx.eng.tensor_tensor(
                    stats[:, 6:7], stats[:, 10:11], stats[:, 5:6], op=ALU.mult))
                ctx.op("ve", lambda: ctx.eng.tensor_scalar(
                    stats[:, 7:8], stats[:, 6:7], -1.0, 2.0, op0=ALU.mult, op1=ALU.add))
                ctx.op("ve", lambda: ctx.eng.tensor_tensor(
                    stats[:, 8:9], stats[:, 5:6], stats[:, 7:8], op=ALU.mult))
                ctx.op("ve", lambda: ctx.eng.scalar_tensor_tensor(
                    stats[:, 9:10], stats[:, 1:2], -1.0, stats[:, 8:9],
                    op0=ALU.mult, op1=ALU.mult))
                ctx.ck(f"d.rstd2.{t}", "ve")
                ctx.op("ve", lambda: ctx.eng.tensor_scalar(
                    t_sb[:], xo_sb[:], stats[:, 8:9], stats[:, 9:10],
                    op0=ALU.mult, op1=ALU.add))
                ctx.ck(f"d.ln2.{t}", "ve")
                for f in range(4):
                    fid = t * 4 + f
                    pT = psT[fid % 2]
                    ctx.wait("mm", ctx.at(f"d.fill.{fid}"))
                    ctx.wait("ac", ctx.at(f"d.feh.{fid}"))
                    if t >= 1 and f == 0:
                        ctx.wait("sy", ctx.at(f"d.l2st.{t-1}"))
                    for j in range(4):
                        dch = f * 4 + j
                        fn = (lambda dch=dch, j=j, pT=pT: ctx.eng.tensor_tensor(
                            d_l2l[:, dch, :], pT[:, j * 128:(j + 1) * 128],
                            d_l2h[:, dch, :], op=ALU.subtract))
                        if j == 3:
                            ctx.op("ve", fn)
                        else:
                            ctx.raw(fn)
                    ctx.ck(f"d.fel.{fid}", "ve")
                ctx.ck(f"d.l2l.{t}", "ve")
            elif eng == "ac":
                ctx.wait("sy", ctx.at(f"d.ld.{t}"))
                ctx.op("ac", lambda: ctx.eng.activation(
                    out=junk[:], in_=po_sb[:], func=AF.Square, accum_out=stats[:, 4:5]))
                ctx.ck(f"d.ssq.{t}", "ac")
                ctx.wait("ve", ctx.at(f"d.var.{t}"))
                ctx.op("ac", lambda: ctx.eng.activation(
                    out=stats[:, 10:11], in_=stats[:, 3:4], func=AF.Sqrt, bias=stats[:, 11:12]))
                ctx.ck(f"d.std.{t}", "ac")
                ctx.wait("ve", ctx.at(f"d.x2.{t}"))
                ctx.op("ac", lambda: ctx.eng.activation(
                    out=junk[:], in_=xo_sb[:], func=AF.Square, accum_out=stats[:, 4:5]))
                ctx.ck(f"d.ssq2.{t}", "ac")
                ctx.wait("ve", ctx.at(f"d.var2.{t}"))
                ctx.op("ac", lambda: ctx.eng.activation(
                    out=stats[:, 10:11], in_=stats[:, 3:4], func=AF.Sqrt, bias=stats[:, 11:12]))
                ctx.ck(f"d.std2.{t}", "ac")
                for f in range(4):
                    fid = t * 4 + f
                    pT = psT[fid % 2]
                    ctx.wait("mm", ctx.at(f"d.fill.{fid}"))
                    if t >= 1 and f == 0:
                        ctx.wait("sy", ctx.at(f"d.l2st.{t-1}"))
                    for j in range(4):
                        dch = f * 4 + j
                        fn = (lambda dch=dch, j=j, pT=pT: ctx.eng.copy(
                            d_l2h[:, dch, :], pT[:, j * 128:(j + 1) * 128]))
                        if j == 3:
                            ctx.op("ac", fn)
                        else:
                            ctx.raw(fn)
                    ctx.ck(f"d.feh.{fid}", "ac")
                ctx.ck(f"d.l2h.{t}", "ac")
            elif eng == "mm":
                ctx.wait("ve", ctx.at(f"d.ln2.{t}"))
                for f in range(4):
                    fid = t * 4 + f
                    pT = psT[fid % 2]
                    if fid >= 2:
                        ctx.wait("ac", ctx.at(f"d.feh.{fid-2}"))
                        ctx.wait("ve", ctx.at(f"d.fel.{fid-2}"))
                    for j in range(4):
                        dch = f * 4 + j
                        fn = (lambda dch=dch, j=j, pT=pT: ctx.eng.transpose(
                            pT[:, j * 128:(j + 1) * 128],
                            t_sb[:, dch * 128:(dch + 1) * 128], ident[:]))
                        if j == 3:
                            ctx.op("mm", fn)
                        else:
                            ctx.raw(fn)
                    ctx.ck(f"d.fill.{fid}", "mm")
                ctx.ck(f"d.tr.{t}", "mm")

    def phase_ef(eng):
        for tc2 in range(2):
            if eng == "sy":
                if tc2 == 0:
                    ctx.wait("sy", ctx.at(f"d.l2st.{NTT-1}"))
                else:
                    ctx.wait("ve", ctx.at(f"f.y.{tc2-1}.3"))
                    ctx.wait("sy", ctx.at(f"f.yst.{tc2-1}.3"))
                    ctx.wait("mm", ctx.at(f"e.gu.{tc2-1}.{NI-1}"))
                for hl in range(2):
                    src = ln2_scr[hl, :, :, tc2 * 512:(tc2 + 1) * 512].transpose([1, 0, 2])
                    dst = l2h if hl == 0 else l2l
                    ctx.op("sy", lambda src=src, dst=dst: ctx.eng.dma_start(
                        out=dst[:, :, :], in_=src), 16)
                ctx.ck(f"e.l2.{tc2}", "sy")
                for ich in range(NI):
                    b = ich % 2
                    gi = tc2 * NI + ich
                    if gi >= 2:
                        ctx.wait("mm", ctx.at(f"e.gu.{(gi-2)//NI}.{(gi-2)%NI}"))
                    if gi == 0:
                        ctx.wait("sd", ctx.at(f"a.qst.{NMINI-1}"))
                    ctx.op("sy", lambda ich=ich, b=b: ctx.eng.dma_start(
                        out=wgblk[b][:, :, :], in_=wg16[ich, :, :, :]), 16)
                    ctx.op("sy", lambda ich=ich, b=b: ctx.eng.dma_start(
                        out=wublk[b][:, :, :], in_=wu16[ich, :, :, :]), 16)
                    ctx.ck(f"e.wb.{tc2}.{ich}", "sy")
                for d2 in range(4):
                    for ich in range(0, NI, 2):
                        b = (ich // 2) % 2
                        gdi = tc2 * 4 * NI + d2 * NI + ich
                        if gdi >= 4:
                            pd = gdi - 4
                            ptc = pd // (4 * NI)
                            pr = pd % (4 * NI)
                            ctx.wait("mm", ctx.at(f"e.dmm.{ptc}.{pr//NI}.{pr%NI+1}"))
                        ctx.op("sy", lambda d2=d2, ich=ich, b=b: ctx.eng.dma_start(
                            out=wdblk[b][:, :, :],
                            in_=wd16[d2, ich:ich + 2, :, :].transpose([1, 0, 2])), 16)
                        ctx.ck(f"e.wd.{tc2}.{d2}.{ich}", "sy")
            elif eng == "mm":
                ctx.wait("sy", ctx.at(f"e.l2.{tc2}"))
                for ich in range(NI):
                    b = ich % 2
                    ctx.wait("sy", ctx.at(f"e.wb.{tc2}.{ich}"))
                    gi = tc2 * NI + ich
                    if gi >= 1:
                        pg = gi - 1
                        ctx.wait("ac", ctx.at(f"e.silu.{pg//NI}.{pg%NI}"))
                        ctx.wait("ve", ctx.at(f"e.ff.{pg//NI}.{pg%NI}"))
                    if tc2 == 0 and ich == 0:
                        ctx.wait("ac", ctx.at(f"d.pev.1.{NTT-1}"))
                    for dch in range(ND):
                        st = (dch == 0)
                        lastmm = (dch == ND - 1)
                        ctx.raw(lambda b=b, dch=dch, st=st: ctx.eng.matmul(
                            psG[:, 0:512], wgblk[b][:, dch, :], l2h[:, dch, :],
                            start=st, stop=False))
                        fn = (lambda b=b, dch=dch, lastmm=lastmm: ctx.eng.matmul(
                            psG[:, 0:512], wgblk[b][:, dch, :], l2l[:, dch, :],
                            start=False, stop=lastmm))
                        if lastmm:
                            ctx.op("mm", fn)
                        else:
                            ctx.raw(fn)
                    ctx.ck(f"e.gmm.{tc2}.{ich}", "mm")
                    for dch in range(ND):
                        st = (dch == 0)
                        lastmm = (dch == ND - 1)
                        ctx.raw(lambda b=b, dch=dch, st=st: ctx.eng.matmul(
                            psU[:, 0:512], wublk[b][:, dch, :], l2h[:, dch, :],
                            start=st, stop=False))
                        fn = (lambda b=b, dch=dch, lastmm=lastmm: ctx.eng.matmul(
                            psU[:, 0:512], wublk[b][:, dch, :], l2l[:, dch, :],
                            start=False, stop=lastmm))
                        if lastmm:
                            ctx.op("mm", fn)
                        else:
                            ctx.raw(fn)
                    ctx.ck(f"e.umm.{tc2}.{ich}", "mm")
                    ctx.ck(f"e.gu.{tc2}.{ich}", "mm")
                for d2 in range(4):
                    for ich in range(NI):
                        b = (ich // 2) % 2
                        sl = ich % 2
                        if sl == 0:
                            ctx.wait("sy", ctx.at(f"e.wd.{tc2}.{d2}.{ich}"))
                        if d2 == 0:
                            ctx.wait("ac", ctx.at(f"e.fh.{tc2}.{ich}"))
                            ctx.wait("ve", ctx.at(f"e.fl.{tc2}.{ich}"))
                        if d2 == 0 and ich == 0 and tc2 >= 1:
                            ctx.wait("ve", ctx.at(f"e.dse.{tc2-1}.3"))
                        if d2 >= 1 and ich == 0:
                            ctx.wait("ve", ctx.at(f"e.dse.{tc2}.{d2-1}"))
                        for tg in range(4):
                            st = (ich == 0)
                            lastmm = (ich == NI - 1)
                            ctx.raw(lambda b=b, sl=sl, ich=ich, tg=tg, st=st:
                                    ctx.eng.matmul(
                                        psD[tg][:, 0:512],
                                        fuh[:, ich, tg * 128:(tg + 1) * 128],
                                        wdblk[b][:, sl, :], start=st, stop=False))
                            fn = (lambda b=b, sl=sl, ich=ich, tg=tg, lastmm=lastmm:
                                  ctx.eng.matmul(
                                      psD[tg][:, 0:512],
                                      ful[:, ich, tg * 128:(tg + 1) * 128],
                                      wdblk[b][:, sl, :], start=False, stop=lastmm))
                            if tg == 3:
                                ctx.op("mm", fn)
                            else:
                                ctx.raw(fn)
                        ctx.ck(f"e.dmm.{tc2}.{d2}.{ich}", "mm")
                    ctx.ck(f"e.dcol.{tc2}.{d2}", "mm")
            elif eng == "ac":
                for ich in range(NI):
                    ctx.wait("mm", ctx.at(f"e.gmm.{tc2}.{ich}"))
                    ctx.op("ac", lambda ich=ich: ctx.eng.activation(
                        out=fg_sb[:], in_=psG[:, 0:512], func=AF.Silu,
                        scale=gs_sb[:, ich:ich + 1]))
                    ctx.ck(f"e.silu.{tc2}.{ich}", "ac")
                    ctx.wait("ve", ctx.at(f"e.ff.{tc2}.{ich}"))
                    ctx.op("ac", lambda ich=ich: ctx.eng.copy(
                        fuh[:, ich, :], junk[:, 0:512]))
                    ctx.ck(f"e.fh.{tc2}.{ich}", "ac")
            elif eng == "ve":
                for ich in range(NI):
                    ctx.wait("mm", ctx.at(f"e.umm.{tc2}.{ich}"))
                    ctx.wait("ac", ctx.at(f"e.silu.{tc2}.{ich}"))
                    if tc2 >= 1 and ich == 0:
                        ctx.wait("mm", ctx.at(f"e.dcol.{tc2-1}.3"))
                        ctx.wait("ac", ctx.at(f"f.ssq.{tc2-1}.3"))
                    ctx.op("ve", lambda ich=ich: ctx.eng.scalar_tensor_tensor(
                        junk[:, 0:512], psU[:, 0:512], us_sb[:, ich:ich + 1],
                        fg_sb[:], op0=ALU.mult, op1=ALU.mult))
                    ctx.ck(f"e.ff.{tc2}.{ich}", "ve")
                    ctx.wait("ac", ctx.at(f"e.fh.{tc2}.{ich}"))
                    ctx.op("ve", lambda ich=ich: ctx.eng.tensor_tensor(
                        ful[:, ich, :], junk[:, 0:512], fuh[:, ich, :],
                        op=ALU.subtract))
                    ctx.ck(f"e.fl.{tc2}.{ich}", "ve")
                for d2 in range(4):
                    ctx.wait("mm", ctx.at(f"e.dmm.{tc2}.{d2}.{NI-1}"))
                    if tc2 >= 1 and d2 == 0:
                        ctx.wait("sy", ctx.at(f"f.yst.{tc2-1}.3"))
                    for tg in range(4):
                        ctx.op("ve", lambda d2=d2, tg=tg: ctx.eng.tensor_tensor(
                            dout[:, tg, d2 * 512:(d2 + 1) * 512], psD[tg][:, 0:512],
                            dsb[:, d2 * 512:(d2 + 1) * 512], op=ALU.mult))
                    ctx.ck(f"e.dse.{tc2}.{d2}", "ve")
            for tg in range(4):
                gt = tc2 * 4 + tg
                if eng == "sy":
                    ctx.wait("ve", ctx.at(f"e.dse.{tc2}.3"))
                    if gt >= 1:
                        ctx.wait("ve", ctx.at(f"f.y.{(gt-1)//4}.{(gt-1)%4}"))
                    ctx.op("sy", lambda gt=gt: ctx.eng.dma_start(
                        out=x2t[:], in_=x2_scr[gt, :, :]), 16)
                    ctx.ck(f"f.x2.{tc2}.{tg}", "sy")
                    ctx.wait("ve", ctx.at(f"f.y.{tc2}.{tg}"))
                    ctx.op("sy", lambda gt=gt: ctx.eng.dma_start(
                        out=y_out[gt * 128:(gt + 1) * 128, :], in_=x2t[:]), 16)
                    ctx.ck(f"f.yst.{tc2}.{tg}", "sy")
                elif eng == "ve":
                    src = dout[:, tg, :]
                    if tg == 0:
                        pass
                    ctx.op("ve", lambda src=src: ctx.eng.reduce_sum(
                        stats[:, 0:1], src, axis=AX.X))
                    ctx.op("ve", lambda: ctx.eng.tensor_scalar_mul(
                        stats[:, 1:2], stats[:, 0:1], 1.0 / D))
                    ctx.op("ve", lambda: ctx.eng.tensor_tensor(
                        stats[:, 2:3], stats[:, 1:2], stats[:, 1:2], op=ALU.mult))
                    ctx.ck(f"f.vm.{tc2}.{tg}", "ve")
                    ctx.wait("ac", ctx.at(f"f.ssq.{tc2}.{tg}"))
                    ctx.op("ve", lambda: ctx.eng.scalar_tensor_tensor(
                        stats[:, 3:4], stats[:, 4:5], 1.0 / D, stats[:, 2:3],
                        op0=ALU.mult, op1=ALU.subtract))
                    ctx.ck(f"f.var.{tc2}.{tg}", "ve")
                    ctx.wait("ac", ctx.at(f"f.std.{tc2}.{tg}"))
                    ctx.op("ve", lambda: ctx.eng.reciprocal(stats[:, 5:6], stats[:, 10:11]))
                    ctx.op("ve", lambda: ctx.eng.tensor_tensor(
                        stats[:, 6:7], stats[:, 10:11], stats[:, 5:6], op=ALU.mult))
                    ctx.op("ve", lambda: ctx.eng.tensor_scalar(
                        stats[:, 7:8], stats[:, 6:7], -1.0, 2.0,
                        op0=ALU.mult, op1=ALU.add))
                    ctx.op("ve", lambda: ctx.eng.tensor_tensor(
                        stats[:, 8:9], stats[:, 5:6], stats[:, 7:8], op=ALU.mult))
                    ctx.op("ve", lambda: ctx.eng.scalar_tensor_tensor(
                        stats[:, 9:10], stats[:, 1:2], -1.0, stats[:, 8:9],
                        op0=ALU.mult, op1=ALU.mult))
                    ctx.ck(f"f.rstd.{tc2}.{tg}", "ve")
                    ctx.op("ve", lambda src=src: ctx.eng.tensor_scalar(
                        t_f[:], src, stats[:, 8:9], stats[:, 9:10],
                        op0=ALU.mult, op1=ALU.add))
                    ctx.ck(f"f.t.{tc2}.{tg}", "ve")
                    ctx.wait("sy", ctx.at(f"f.x2.{tc2}.{tg}"))
                    ctx.op("ve", lambda: ctx.eng.tensor_tensor(
                        x2t[:], x2t[:], t_f[:], op=ALU.add))
                    ctx.ck(f"f.y.{tc2}.{tg}", "ve")
                elif eng == "ac":
                    src = dout[:, tg, :]
                    ctx.wait("ve", ctx.at(f"f.vm.{tc2}.{tg}"))
                    ctx.op("ac", lambda src=src: ctx.eng.activation(
                        out=junk[:], in_=src, func=AF.Square, accum_out=stats[:, 4:5]))
                    ctx.ck(f"f.ssq.{tc2}.{tg}", "ac")
                    ctx.wait("ve", ctx.at(f"f.var.{tc2}.{tg}"))
                    ctx.op("ac", lambda: ctx.eng.activation(
                        out=stats[:, 10:11], in_=stats[:, 3:4], func=AF.Sqrt, bias=stats[:, 11:12]))
                    ctx.ck(f"f.std.{tc2}.{tg}", "ac")
        if eng == "sy":
            ctx.wait("sy", ctx.at("f.yst.1.3"))

    def run_engine(eng):
        phase_pre(eng)
        phase_a(eng)
        phase_b(eng)
        phase_c(eng)
        phase_d(eng)
        phase_ef(eng)

    for eng in ["sy", "ve", "ac", "mm", "gp"]:
        ctx.reset("count")
        run_engine(eng)

    with nc.Block() as block:
        @block.sync
        def _(sync):
            ctx.reset("emit", sync)
            run_engine("sy")

        @block.vector
        def _(vector):
            ctx.reset("emit", vector)
            run_engine("ve")

        @block.scalar
        def _(scalar):
            ctx.reset("emit", scalar)
            run_engine("ac")

        @block.tensor
        def _(tensor):
            ctx.reset("emit", tensor)
            run_engine("mm")

        @block.gpsimd
        def _(gpsimd):
            ctx.reset("emit", gpsimd)
            run_engine("gp")

    es.close()
    return nc


_BUILT = None


def kernel(L0_attn_x, L0_attn_w_qkv, L0_attn_w_proj, L0_ffn_gate_w, L0_ffn_up_w,
           L0_ffn_down_w, L0_ffn_gate_scale, L0_ffn_up_scale, L0_ffn_down_scale):
    global _BUILT
    x = np.ascontiguousarray(np.asarray(L0_attn_x, dtype=np.float32).reshape(S, D))
    w_qkv = np.asarray(L0_attn_w_qkv, dtype=np.float32)
    wp = np.ascontiguousarray(np.asarray(L0_attn_w_proj, dtype=np.float32).T)
    gw = np.ascontiguousarray(np.asarray(L0_ffn_gate_w, dtype=np.float32))
    uw = np.ascontiguousarray(np.asarray(L0_ffn_up_w, dtype=np.float32))
    dw = np.ascontiguousarray(np.asarray(L0_ffn_down_w, dtype=np.float32))
    gs = np.ascontiguousarray(np.asarray(L0_ffn_gate_scale, dtype=np.float32))
    us = np.ascontiguousarray(np.asarray(L0_ffn_up_scale, dtype=np.float32))
    ds = np.ascontiguousarray(np.asarray(L0_ffn_down_scale, dtype=np.float32))
    ident = np.eye(128, dtype=np.float32)

    if _BUILT is None:
        _BUILT = build_program()
    nc = _BUILT

    in_maps = []
    for c in range(NC):
        qrows = w_qkv[NH * HD * c:NH * HD * (c + 1), :]
        krows = w_qkv[D + NH * HD * c:D + NH * HD * (c + 1), :]
        vrows = w_qkv[2 * D + NH * HD * c:2 * D + NH * HD * (c + 1), :]
        in_maps.append({
            "x_full": x,
            "x_own": np.ascontiguousarray(x[T * c:T * (c + 1), :]),
            "wqkT": np.ascontiguousarray(np.concatenate([qrows, krows], axis=0).T),
            "wvT": np.ascontiguousarray(vrows.T),
            "wpT": wp,
            "wg": gw, "wu": uw, "wd": dw,
            "gs": gs, "us": us, "ds": ds,
            "ident": ident,
        })

    res = run_bass_kernel_spmd(nc, in_maps, core_ids=list(range(NC)))
    out = np.concatenate([res.results[c]["y"] for c in range(NC)], axis=0)
    return out.reshape(1, S, D)
